# version 13
# baseline (speedup 1.0000x reference)
import numpy as np
import ml_dtypes
from contextlib import ExitStack
import concourse.bass as bass
import concourse.mybir as mybir
from concourse.bass_utils import run_bass_kernel_spmd

BF16N = ml_dtypes.bfloat16
F32 = mybir.dt.float32
BF16 = mybir.dt.bfloat16
AF = mybir.ActivationFunctionType
ALU = mybir.AluOpType
AX = mybir.AxisListType

D = 4096
DC = 32
W1 = 1024
WC = 8
QC = 24
T = 1024
TH = 512
H = 16
NB = 8
ERC = 2

OFF_ONES128 = 0
OFF_MASK = 256
OFF_IDF32 = 4352
OFF_CONVB = 4864
OFF_BQS = 4992
OFF_BK = 5024
OFF_BO = 5056
OFF_PROJB = 5088
OFF_FFNB = 5216
OFF_PROBS = 5344
OFF_ONES1 = 5376
OFF_BVF = 5632
OFF_BVB = 9728
OFF_M1ALL = 11776
OFF_NEGG8 = 11808
OFF_NEGROWF = 11840
OFF_NEGROWB = 11872
OFF_GNEG = 11904
OFF_SQTMP = 11936
OFF_STATS = 14016
OFF_HBOX = 22208
OFF_LINV = 26304
REG0 = 42688
REG1 = 108352
REG2 = 157504
OFF_SPARE = 206656
OFF_WK7 = 210752


import dataclasses


def bc3(ap2d, reps):
    return dataclasses.replace(ap2d, ap=[ap2d.ap[0], [0, reps], ap2d.ap[1]])


class Sched:
    ENGS = ("sync", "gpsimd", "tensor", "scalar", "vector")

    def __init__(self):
        self.q = {e: [] for e in self.ENGS}
        self.cnt = {}
        self.hw = {e: {} for e in self.ENGS}

    def n(self, sem):
        return self.cnt.get(sem, 0)

    def op(self, eng, fn, wait=(), inc=None, k=1):
        if eng == "vector":
            wait = list(wait) + [("vC", self.n("vC"))]
            if inc is None:
                inc = "vC"
        elif eng == "gpsimd":
            wait = list(wait) + [("gC", self.n("gC"))]
            if inc is None and k == 1:
                inc = "gC"
        elif eng == "scalar":
            wait = list(wait) + [("cC", self.n("cC"))]
            if inc is None:
                inc = "cC"
        waits = []
        for s, v in wait:
            if v is None or v <= 0:
                continue
            if self.hw[eng].get(s, 0) >= v:
                continue
            self.hw[eng][s] = v
            waits.append((s, v))
        if inc is not None:
            self.cnt[inc] = self.cnt.get(inc, 0) + k

        def run(e, sems, fn=fn, waits=waits, inc=inc, k=k):
            for s, v in waits:
                e.wait_ge(sems[s], v)
            inst = fn(e)
            if inc is not None:
                inst.then_inc(sems[inc], k)

        self.q[eng].append(run)
        return self.cnt.get(inc, 0) if inc is not None else None


def build_nc(taps=(), stop_after='full'):
    nc = bass.Bass("TRN2", target_bir_lowering=False)
    dt = lambda name, shape, dtype=F32: nc.dram_tensor(
        name, list(shape), dtype, kind="ExternalInput").ap()

    xT = dt("xT", (D, T + 2))
    cw = dt("cw", (3, D, 3 * W1))
    wq = dt("wq", (W1, W1))
    wk = dt("wk", (W1, W1))
    wv = dt("wv", (W1, W1))
    wo = dt("wo", (W1, W1))
    projw = dt("projw", (W1, D))
    Wt = dt("Wt", (D, D))
    At = dt("At", (D, 2 * 128))
    Bm = dt("Bm", (2 * 128, D))
    convb_d = dt("convb", (128, QC))
    bq_d = dt("bq", (128, WC))
    bk_d = dt("bk", (128, WC))
    bo_d = dt("bo", (128, WC))
    projb_d = dt("projb", (128, DC))
    ffnb_d = dt("ffnb", (128, DC))
    probs_d = dt("probs", (128, ERC))
    bv_d = dt("bv", (1, W1))
    ones128_d = dt("ones128", (128, 128), BF16)
    ones1_d = dt("ones1", (1, 128), BF16)
    mask_d = dt("mask", (128, 2048), BF16)
    idf_d = dt("idf", (128, 128), F32)
    idb_d = dt("idb", (128, 128), BF16)

    outT = nc.dram_tensor("outT", [D, T], F32, kind="ExternalOutput").ap()
    tap_outs = {}
    for tname, shape, tdt in taps:
        tap_outs[tname] = nc.dram_tensor(
            "tap_" + tname, list(shape), tdt, kind="ExternalOutput").ap()

    base = (nc._sbuf_addr_for_side("left") + 63) // 64 * 64

    def sb(name, off, shape, dtype):
        return nc.alloc_sbuf_tensor_at(name, list(shape), dtype,
                                       offset=base + off)

    ones128 = sb("ones128_s", OFF_ONES128, (128, 128), BF16)
    maskt = sb("maskt", OFF_MASK, (128, 2048), BF16)
    idf = sb("idf_s", OFF_IDF32, (128, 128), F32)
    convb = sb("convb_s", OFF_CONVB, (128, QC), F32)
    bqs = sb("bqs", OFF_BQS, (128, WC), F32)
    bks = sb("bks", OFF_BK, (128, WC), F32)
    bos = sb("bos", OFF_BO, (128, WC), F32)
    projb = sb("projb_s", OFF_PROJB, (128, DC), F32)
    ffnb = sb("ffnb_s", OFF_FFNB, (128, DC), F32)
    probs = sb("probs_s", OFF_PROBS, (128, ERC), F32)
    ones1 = sb("ones1_s", OFF_ONES1, (1, 128), BF16)
    bvf = sb("bvf", OFF_BVF, (1, W1), F32)
    bvb = sb("bvb", OFF_BVB, (1, W1), BF16)
    m1all = sb("m1all", OFF_M1ALL, (128, NB), F32)
    negg8 = sb("negg8", OFF_NEGG8, (NB, 1), F32)
    negrowf = sb("negrowf", OFF_NEGROWF, (1, NB), F32)
    negrowb = sb("negrowb", OFF_NEGROWB, (1, NB), BF16)
    gneg = sb("gneg", OFF_GNEG, (128, NB), F32)
    sqtmp = sb("sqtmp", OFF_SQTMP, (128, 1024), BF16)
    negmean = sb("negmean", OFF_STATS, (128, TH), F32)
    tmp1 = sb("tmp1", OFF_STATS + 2048, (128, TH), F32)
    tmp2 = sb("tmp2", OFF_STATS + 4096, (128, TH), F32)
    rstd = sb("rstd", OFF_STATS + 6144, (128, TH), F32)
    nm2 = sb("nm2", OFF_LINV + 8192, (128, TH), F32)
    rstd2 = sb("rstd2", OFF_LINV + 10240, (128, TH), F32)
    nmb = sb("nmb", OFF_STATS + 2048, (128, TH), BF16)
    rstdb = sb("rstdb", OFF_STATS + 4096, (128, TH), BF16)
    idb = sb("idb", OFF_BVF, (128, 128), BF16)
    hb = sb("hb", OFF_SPARE, (128, 2048), BF16)
    linv0 = sb("linv0", OFF_LINV, (128, 2048), F32)
    linv1 = sb("linv1", OFF_LINV + 8192, (128, 2048), F32)
    atb = sb("atb", OFF_LINV, (128, 32 * 256), BF16)
    ob = [sb(f"ob{i}", OFF_LINV + i * 2048, (128, TH), F32) for i in range(4)]
    wqpre = [sb(f"wqpre{c}", OFF_LINV + c * 2048, (128, 1024), BF16)
             for c in range(8)]
    wk_offs = [OFF_STATS, OFF_STATS + 2048, OFF_STATS + 4096, OFF_STATS + 6144,
               OFF_HBOX, OFF_HBOX + 2048, OFF_SQTMP, OFF_WK7]
    wkpre = [sb(f"wkpre{c}", wk_offs[c], (128, 1024), BF16) for c in range(8)]
    xb = sb("xb", REG0, (128, DC * (T + 2)), BF16)
    wvb = sb("wvb", REG0 + 32768, (128, 8 * 1024), BF16)
    otb = sb("otb", REG0, (128, 8 * 1024), BF16)
    x1bA = sb("x1bA", REG0 + 16384, (128, DC * TH), BF16)
    x1bB_lo = sb("x1bB_lo", REG0, (128, 16 * TH), BF16)
    x1bB_hi = sb("x1bB_hi", REG0 + 49152, (128, 16 * TH), BF16)
    qkvb = sb("qkvb", REG1, (128, QC * 1024), BF16)
    pb0 = sb("pb0", REG1 + 16384, (128, 2048), BF16)
    pb1 = sb("pb1", REG1 + 20480, (128, 2048), BF16)
    wsA = sb("wsA", REG1, (128, 4096), BF16)
    wsB = sb("wsB", REG1 + 8192, (128, 4096), BF16)
    Bmb = sb("Bmb", OFF_LINV, (128, 2 * D), BF16)
    wsF0 = sb("wsF0", REG1, (128, 8192), BF16)
    wsF1 = sb("wsF1", REG1 + 16384, (128, 8192), BF16)
    res2_m = sb("res2_m", REG1 + 32768, (128, 8 * 1024), BF16)
    cwb0 = sb("cwb0", REG2, (128, 3 * 4096), BF16)
    cwb1 = sb("cwb1", REG2 + 24576, (128, 3 * 4096), BF16)
    qhb = sb("qhb", REG2, (128, 8 * 1024), BF16)
    khb = sb("khb", REG2 + 16384, (128, 8 * 1024), BF16)
    vtb = sb("vtb", REG2 + 32768, (128, 8 * 1024), BF16)
    res1lo = sb("res1lo", REG1 + 32768, (128, 8 * TH), BF16)
    res1hi = sb("res1hi", REG2, (128, 24 * TH), BF16)
    a1b = sb("a1b", REG1 + 24576, (128, WC * TH), BF16)
    xjb0 = sb("xjb0", REG0 + 53248, (128, TH), BF16)
    xjb1 = sb("xjb1", REG0 + 54272, (128, TH), BF16)
    res2_lo = sb("res2_lo", REG2, (128, 16 * 1024), BF16)
    res2_hi = sb("res2_hi", REG2 + 32768, (128, 8 * 1024), BF16)

    psA = nc.alloc_psum_tensor("psA", [128, 2048], F32)
    psB = nc.alloc_psum_tensor("psB", [128, 2048], F32)

    def pbank(k):
        t = psA if k < 4 else psB
        return t[:, (k % 4) * 512:(k % 4 + 1) * 512]

    def pset(p):
        return psA if p == 0 else psB

    S = Sched()
    SEMS = ("xS", "c0", "c1", "w2", "s0", "s1", "sD", "gC",
            "x0", "x1", "o0", "o1", "o2", "o3", "tC", "vC", "cC")

    def res1_ap(c):
        if c < 8:
            return res1lo[:, c * TH:(c + 1) * TH]
        return res1hi[:, (c - 8) * TH:(c - 7) * TH]

    def res1_grp(g):
        if g < 2:
            return res1lo[:, g * 2048:(g + 1) * 2048]
        return res1hi[:, (g - 2) * 2048:(g - 1) * 2048]

    def x1b_ap(half, c):
        if half == 0:
            return x1bA[:, c * TH:(c + 1) * TH]
        if c < 16:
            return x1bB_lo[:, c * TH:(c + 1) * TH]
        return x1bB_hi[:, (c - 16) * TH:(c - 15) * TH]

    def res2_ap(c):
        if c < 16:
            return res2_lo[:, c * 1024:(c + 1) * 1024]
        if c < 24:
            return res2_m[:, (c - 16) * 1024:(c - 15) * 1024]
        return res2_hi[:, (c - 24) * 1024:(c - 23) * 1024]

    for sbuf_t, dram_ap in (
            (ones128, ones128_d), (maskt, mask_d), (idf, idf_d),
            (convb, convb_d), (bqs, bq_d), (bks, bk_d), (bos, bo_d),
            (projb, projb_d), (ffnb, ffnb_d), (probs, probs_d),
            (ones1, ones1_d), (bvf, bv_d)):
        S.op("sync", lambda e, o=sbuf_t, i=dram_ap: e.dma_start(
            out=o[:, :], in_=i), inc="sD", k=16)
    n_const = S.n("sD")
    S.op("vector", lambda e: e.tensor_scalar_mul(bqs[:, :], bqs[:, :], 0.125),
         wait=[("sD", n_const)])
    S.op("vector", lambda e: e.tensor_copy(bvb[:, :], bvf[:, :]))
    v_prep = S.n("vC")
    S.op("sync", lambda e: e.dma_start(out=idb[:, :], in_=idb_d),
         wait=[("vC", v_prep)], inc="sD", k=16)
    sd_idb = ("sD", S.n("sD"))

    for i in range(4):
        def xldma(e, i=i):
            src = xT[i * 8 * 128:(i + 1) * 8 * 128, :].rearrange(
                "(c p) f -> p c f", p=128)
            dst = xb[:, i * 8 * (T + 2):(i + 1) * 8 * (T + 2)].rearrange(
                "p (c f) -> p c f", c=8)
            return e.dma_start(out=dst, in_=src)
        S.op("gpsimd", xldma, inc="xS", k=16)
    g_xb = ("xS", S.n("xS"))

    conv_w_load_done = {}
    conv_last_mm_using_buf = {}

    def emit_conv_load(oc):
        par = oc % 2
        buf = cwb0 if par == 0 else cwb1
        sem = "c0" if par == 0 else "c1"
        waits = []
        if oc >= 2:
            waits.append(("tC", conv_last_mm_using_buf[oc - 2]))
        for w in range(3):
            for hh in range(2):
                def cwdma(e, oc=oc, w=w, hh=hh, buf=buf):
                    src = cw[w, hh * 2048:(hh + 1) * 2048,
                             oc * 128:(oc + 1) * 128].rearrange(
                        "(c p) f -> p c f", p=128)
                    dst = buf[:, w * 4096 + hh * 2048:
                              w * 4096 + (hh + 1) * 2048].rearrange(
                        "p (c f) -> p c f", c=16)
                    return e.dma_start(out=dst, in_=src)
                S.op("gpsimd", cwdma, wait=waits, inc=sem, k=16)
                waits = []
        conv_w_load_done[oc] = (sem, S.n(sem))

    def emit_pre_load(i):
        if i < 8:
            wt, dst = wq, wqpre[i]
            srcsl = wq[i * 128:(i + 1) * 128, :]
        else:
            wt, dst = wk, wkpre[i - 8]
            srcsl = wk[(i - 8) * 128:(i - 8 + 1) * 128, :]
        def pldma(e, srcsl=srcsl, dst=dst):
            return e.dma_start(out=dst[:, :], in_=srcsl)
        S.op("gpsimd", pldma, inc="w2", k=16)

    emit_conv_load(0)
    emit_conv_load(1)

    bank_free = {}
    tile_i = 0
    first_tensor = [("sD", n_const)]
    first_scalar = [("sD", n_const)]
    for oc in range(QC):
        par = oc % 2
        buf = cwb0 if par == 0 else cwb1
        for tokc in range(2):
            bank = tile_i % 8
            waits = [conv_w_load_done[oc], g_xb] + first_tensor
            first_tensor = []
            if bank in bank_free:
                waits.append(bank_free[bank])
            for dc in range(DC):
                for w in range(3):
                    def cmm(e, oc=oc, tokc=tokc, dc=dc, w=w, buf=buf,
                            bank=bank, first=(dc == 0 and w == 0),
                            last=(dc == DC - 1 and w == 2)):
                        return e.matmul(
                            pbank(bank),
                            buf[:, (w * 32 + dc) * 128:(w * 32 + dc + 1) * 128],
                            xb[:, dc * (T + 2) + tokc * 512 + w:
                               dc * (T + 2) + tokc * 512 + w + 512],
                            start=first, stop=last)
                    S.op("tensor", cmm, wait=waits,
                         inc="tC" if (dc == DC - 1 and w == 2) else None)
                    waits = []
            t_grp = S.n("tC")
            if tile_i % 3 != 2:
                def cevac(e, oc=oc, tokc=tokc, bank=bank):
                    return e.activation(
                        qkvb[:, oc * 1024 + tokc * 512:
                             oc * 1024 + tokc * 512 + 512],
                        pbank(bank), AF.Identity,
                        bias=convb[:, oc:oc + 1], scale=1.0)
                S.op("scalar", cevac, wait=[("tC", t_grp)] + first_scalar)
                first_scalar = []
                bank_free[bank] = ("cC", S.n("cC"))
            else:
                def cevac(e, oc=oc, tokc=tokc, bank=bank):
                    return e.tensor_scalar_add(
                        qkvb[:, oc * 1024 + tokc * 512:
                             oc * 1024 + tokc * 512 + 512],
                        pbank(bank), convb[:, oc:oc + 1])
                S.op("vector", cevac, wait=[("tC", t_grp)])
                bank_free[bank] = ("vC", S.n("vC"))
            tile_i += 1
        conv_last_mm_using_buf[oc] = S.n("tC")
        if oc + 2 < QC:
            emit_conv_load(oc + 2)
        if oc < 16:
            emit_pre_load(oc)
    conv_all_mm = S.n("tC")
    conv_all_evac = [("cC", S.n("cC")), ("vC", S.n("vC"))]
    g_pre = ("w2", S.n("w2"))

    def wvdma(e):
        src = wv[:, :].rearrange("(c p) f -> p c f", p=128)
        dst = wvb[:, :].rearrange("p (c f) -> p c f", c=8)
        return e.dma_start(out=dst, in_=src)
    S.op("gpsimd", wvdma, wait=[("tC", conv_all_mm), g_pre], inc="w2", k=16)
    g_wv = ("w2", S.n("w2"))

    def emit_qk(wchunks, qoff, outb, bias_col, scale):
        nonlocal tile_i
        for hc in range(WC):
            for tokc in range(2):
                bank = tile_i % 8
                waits = [g_pre] + conv_all_evac
                if bank in bank_free:
                    waits.append(bank_free[bank])
                for dc in range(8):
                    def qmm(e, hc=hc, tokc=tokc, dc=dc, bank=bank,
                            qoff=qoff, wcs=wchunks,
                            first=(dc == 0), last=(dc == 7)):
                        return e.matmul(
                            pbank(bank),
                            wcs[dc][:, hc * 128:hc * 128 + 128],
                            qkvb[:, (qoff + dc) * 1024 + tokc * 512:
                                 (qoff + dc) * 1024 + tokc * 512 + 512],
                            start=first, stop=last)
                    S.op("tensor", qmm, wait=waits,
                         inc="tC" if dc == 7 else None)
                    waits = []
                t_grp = S.n("tC")
                def qevac(e, hc=hc, tokc=tokc, bank=bank, outb=outb,
                          bias_col=bias_col, scale=scale):
                    return e.activation(
                        outb[:, hc * 1024 + tokc * 512:
                             hc * 1024 + tokc * 512 + 512],
                        pbank(bank), AF.Identity,
                        bias=bias_col[:, hc:hc + 1], scale=scale)
                S.op("scalar", qevac, wait=[("tC", t_grp), ("vC", v_prep)])
                bank_free[bank] = ("cC", S.n("cC"))
                tile_i += 1

    emit_qk(wqpre, 0, qhb, bqs, 0.125)
    emit_qk(wkpre, 8, khb, bks, 1.0)

    for tokt in range(8):
        for hec in range(2):
            bank = tile_i % 8
            waits = [g_wv] + conv_all_evac
            if bank in bank_free:
                waits.append(bank_free[bank])
            for dc in range(8):
                def vmm(e, tokt=tokt, hec=hec, dc=dc, bank=bank,
                        first=(dc == 0)):
                    return e.matmul(
                        pbank(bank),
                        qkvb[:, (16 + dc) * 1024 + tokt * 128:
                             (16 + dc) * 1024 + tokt * 128 + 128],
                        wvb[:, dc * 1024 + hec * 512:
                            dc * 1024 + hec * 512 + 512],
                        start=first, stop=False)
                S.op("tensor", vmm, wait=waits)
                waits = []
            def vbias(e, hec=hec, bank=bank):
                return e.matmul(pbank(bank), ones1[:, :],
                                bvb[0:1, hec * 512:hec * 512 + 512],
                                start=False, stop=True)
            S.op("tensor", vbias, wait=[("vC", v_prep)], inc="tC")
            t_grp = S.n("tC")
            def vevac(e, tokt=tokt, hec=hec, bank=bank):
                return e.tensor_copy(
                    vtb[:, tokt * 1024 + hec * 512:
                        tokt * 1024 + hec * 512 + 512],
                    pbank(bank))
            S.op("vector", vevac, wait=[("tC", t_grp)])
            bank_free[bank] = ("vC", S.n("vC"))
            tile_i += 1
    p2_done = [("cC", S.n("cC")), ("vC", S.n("vC"))]

    def scol(h):
        return (h % 2) * 1024 + (h // 2) * 128

    def qk_slice(buf, h, b):
        return buf[(h % 2) * 64:(h % 2) * 64 + 64,
                   (h // 2) * 1024 + b * 128:(h // 2) * 1024 + b * 128 + 128]

    p2_t_end = S.n("tC")
    pre_m1 = {}
    blk_oevac = {}
    blk_state = {}

    def emit_pre(b, ps, extra=()):
        waits = list(p2_done) + list(extra)
        horder = list(range(0, H, 2)) + list(range(1, H, 2))
        for i, h in enumerate(horder):
            def smm(e, b=b, h=h, ps=ps):
                return e.matmul(
                    ps[:, scol(h):scol(h) + 128],
                    qk_slice(khb, h, b), qk_slice(qhb, h, b),
                    start=True, stop=True)
            S.op("tensor", smm, wait=waits,
                 inc="tC" if i == H - 1 else None)
            waits = []
        t_grp = S.n("tC")
        def m1red(e, b=b, ps=ps):
            return e.reduce_max(m1all[:, b:b + 1], ps[:, :], axis=AX.X)
        S.op("vector", m1red, wait=[("tC", t_grp)])
        pre_m1[b] = S.n("vC")

    def emit_finish(blo, bhi, psm, coloff):
        nb_ = bhi - blo
        v_pre = S.n("vC")
        S.op("tensor", lambda e, psm=psm: e.transpose(
            psm[0:nb_, 0:128], m1all[:, blo:bhi], idf[:, :]),
            wait=[("vC", v_pre)], inc="tC")
        t1 = S.n("tC")
        S.op("vector", lambda e, psm=psm: e.reduce_max(
            negg8[0:nb_, :], psm[0:nb_, 0:128], axis=AX.X, negate=True),
            wait=[("tC", t1)])
        v8 = S.n("vC")
        S.op("tensor", lambda e, psm=psm: e.transpose(
            psm[0:1, 256:256 + nb_], negg8[0:nb_, :], idf[0:nb_, 0:nb_]),
            wait=[("vC", v8)], inc="tC")
        t2 = S.n("tC")
        S.op("vector", lambda e, psm=psm: e.tensor_copy(
            negrowf[:, 0:nb_], psm[0:1, 256:256 + nb_]), wait=[("tC", t2)])
        S.op("vector", lambda e: e.tensor_copy(
            negrowb[:, 0:nb_], negrowf[:, 0:nb_]))
        vr = S.n("vC")
        for j in range(nb_):
            def bcast(e, j=j, psm=psm):
                return e.matmul(psm[:, coloff + j:coloff + j + 1], ones1[:, :],
                                negrowb[0:1, j:j + 1], start=True, stop=True)
            S.op("tensor", bcast, wait=[("vC", vr)],
                 inc="tC" if j == nb_ - 1 else None)
        tb = S.n("tC")
        S.op("vector", lambda e, psm=psm: e.tensor_copy(
            gneg[:, blo:bhi], psm[:, coloff:coloff + nb_]), wait=[("tC", tb)])
        return ("vC", S.n("vC"))

    def attn_smm(b, ps, extra=()):
        waits = list(extra)
        horder = list(range(0, H, 2)) + list(range(1, H, 2))
        for i, h in enumerate(horder):
            def smm(e, b=b, h=h, ps=ps):
                return e.matmul(
                    ps[:, scol(h):scol(h) + 128],
                    qk_slice(khb, h, b), qk_slice(qhb, h, b),
                    start=True, stop=True)
            S.op("tensor", smm, wait=waits,
                 inc="tC" if i == H - 1 else None)
            waits = []
        blk_state[b] = dict(smm=S.n("tC"))

    def attn_chain_pre(b, ps):
        pb = pb0 if b % 2 == 0 else pb1
        lv = linv0 if b % 2 == 0 else linv1
        st = blk_state[b]
        S.op("scalar", lambda e, b=b, ps=ps, pb=pb: e.activation(
            pb[:, :], ps[:, :], AF.Exp, bias=gneg[:, b:b + 1], scale=1.0),
            wait=[("tC", st["smm"])])
        st["exp"] = S.n("cC")
        S.op("vector", lambda e, pb=pb: e.tensor_mul(
            pb[:, :], pb[:, :], maskt[:, :]),
            wait=[("cC", st["exp"])])
        st["mask"] = ("vC", S.n("vC"))
        st["pb"], st["lv"] = pb, lv

    def attn_l(b, ps):
        st = blk_state[b]
        pb, lv = st["pb"], st["lv"]
        for j in range(4):
            def lmm(e, j=j, ps=ps, pb=pb):
                return e.matmul(ps[:, j * 512:(j + 1) * 512], ones128[:, :],
                                pb[:, j * 512:(j + 1) * 512],
                                start=True, stop=True)
            S.op("tensor", lmm, wait=[st["mask"]],
                 inc="tC" if j == 3 else None)
        st["lmm"] = S.n("tC")
        S.op("scalar", lambda e, ps=ps, lv=lv: e.activation(
            lv[:, :], ps[:, :], AF.Ln), wait=[("tC", st["lmm"])])
        S.op("scalar", lambda e, lv=lv: e.activation(
            lv[:, :], lv[:, :], AF.Exp, bias=0.0, scale=-1.0))
        st["recip"] = ("cC", S.n("cC"))
        S.op("vector", lambda e, pb=pb, lv=lv: e.tensor_mul(
            pb[:, :], pb[:, :], lv[:, :]), wait=[st["recip"]])
        st["pnorm"] = ("vC", S.n("vC"))

    def attn_o(b, ps):
        st = blk_state[b]
        pb = st["pb"]
        for h in range(H):
            def omm(e, b=b, h=h, ps=ps, pb=pb):
                out = ps[(h % 2) * 64:(h % 2) * 64 + 64,
                         (h // 8) * 512 + ((h // 2) % 4) * 128:
                         (h // 8) * 512 + ((h // 2) % 4) * 128 + 128]
                return e.matmul(
                    out,
                    vtb[:, b * 1024 + h * 64:b * 1024 + h * 64 + 64],
                    pb[:, scol(h):scol(h) + 128],
                    start=True, stop=True)
            S.op("tensor", omm, wait=[st["pnorm"]],
                 inc="tC" if h == H - 1 else None)
        st["omm"] = S.n("tC")
        for g in range(2):
            for bot in range(2):
                def oev(e, b=b, g=g, bot=bot, ps=ps):
                    p0 = bot * 64
                    src = ps[p0:p0 + 64, g * 512:(g + 1) * 512].rearrange(
                        "p (c f) -> p c f", c=4)
                    dst = otb[p0:p0 + 64, g * 4096:(g + 1) * 4096].rearrange(
                        "p (c f) -> p c f", c=4)[:, :, b * 128:(b + 1) * 128]
                    return e.activation(dst, src, AF.Copy)
                S.op("scalar", oev, wait=[("tC", st["omm"])])
        blk_oevac[b] = [("cC", S.n("cC")), ("vC", st["pnorm"][1])]

    for b in range(NB):
        ps = pset(b % 2)
        extra = [("vC", pre_m1[b - 2])] if b >= 2 else []
        emit_pre(b, ps, extra)
    v_gnegA = emit_finish(0, 8, psA, 256 + 8)
    v_gnegB = v_gnegA
    attn_smm(0, psA, [v_gnegA, ("vC", pre_m1[6])])
    attn_chain_pre(0, psA)
    attn_smm(1, psB, [v_gnegA, ("vC", pre_m1[7])])
    attn_chain_pre(1, psB)
    attn_l(0, psA)
    attn_o(0, psA)
    attn_smm(2, psA, blk_oevac[0])
    attn_chain_pre(2, psA)
    attn_l(1, psB)
    attn_o(1, psB)
    attn_smm(3, psB, blk_oevac[1])
    attn_chain_pre(3, psB)
    attn_l(2, psA)
    attn_o(2, psA)
    attn_l(3, psB)
    attn_o(3, psB)

    def _smm47(b):
        def go():
            extra = [v_gnegB]
            extra += blk_oevac[b - 1]
            attn_smm(b, psB, extra)
            attn_chain_pre(b, psB)
        return go
    def _l47(b):
        def go():
            attn_l(b, psB)
        return go
    def _o47(b):
        def go():
            attn_o(b, psB)
        return go
    attn_rest = [_smm47(4), None, _l47(4), None, _o47(4), None,
                 _smm47(5), None, _l47(5), None, _o47(5), None,
                 _smm47(6), _l47(6), _o47(6),
                 _smm47(7), _l47(7), _o47(7)]

    def flat_oevac(rng):
        w = []
        for b in rng:
            w.extend(blk_oevac[b])
        return w

    stream_free = {0: None, 1: None}
    ln_state = {}

    def emit_p4(half, slots=None):
        nonlocal tile_i
        hofs = half * TH
        entry_cc = ("cC", S.n("cC"))
        slot_i = [0]

        def hit_slot():
            if slots and slot_i[0] < len(slots):
                cb = slots[slot_i[0]]
                if cb is not None:
                    cb()
            slot_i[0] += 1

        wo_ld = {}
        def emit_wo_load(g):
            par = g % 2
            buf = wsA if par == 0 else wsB
            sem = "s0" if par == 0 else "s1"
            waits = [("tC", p2_t_end)]
            if stream_free[par] is not None:
                waits.append(stream_free[par])
            def wodma(e, g=g, buf=buf):
                src = wo[:, g * 512:(g + 1) * 512].rearrange(
                    "(c p) f -> p c f", p=128)
                dst = buf[:, :].rearrange("p (c f) -> p c f", c=8)
                return e.dma_start(out=dst, in_=src)
            S.op("gpsimd", wodma, wait=waits, inc=sem, k=16)
            wo_ld[g] = (sem, S.n(sem))

        emit_wo_load(0)
        emit_wo_load(1)
        a1_gate = flat_oevac(range(4) if half == 0 else range(4, 8))
        if half == 0:
            a1_gate = a1_gate + [v_gnegB]
        for wc in range(WC):
            hit_slot()
            g = wc // 4
            par = g % 2
            buf = wsA if par == 0 else wsB
            bank = tile_i % 4
            waits = [wo_ld[g]] + a1_gate
            if bank in bank_free:
                waits.append(bank_free[bank])
            for hec in range(WC):
                def womm(e, wc=wc, hec=hec, bank=bank, buf=buf, hofs=hofs,
                         first=(hec == 0), last=(hec == WC - 1)):
                    return e.matmul(
                        pbank(bank),
                        buf[:, hec * 512 + (wc % 4) * 128:
                            hec * 512 + (wc % 4) * 128 + 128],
                        otb[:, hec * 1024 + hofs:hec * 1024 + hofs + 512],
                        start=first, stop=last)
                S.op("tensor", womm, wait=waits,
                     inc="tC" if hec == WC - 1 else None)
                waits = []
            t_grp = S.n("tC")
            if wc % 4 == 3:
                stream_free[par] = ("tC", t_grp)
            def woevac(e, wc=wc, bank=bank):
                return e.activation(
                    a1b[:, wc * TH:(wc + 1) * TH], pbank(bank), AF.Identity,
                    bias=bos[:, wc:wc + 1], scale=1.0)
            S.op("scalar", woevac, wait=[("tC", t_grp)])
            bank_free[bank] = ("cC", S.n("cC"))
            tile_i += 1
        a1_done = ("cC", S.n("cC"))

        pw_ld = {}
        xj_ld = {}
        xj_free = {0: None, 1: None}
        b1, b2 = (2, 3) if half == 0 else (4, 5)
        st = {}
        stat_w1 = [bank_free[b1]] if b1 in bank_free else []
        stat_w2 = [bank_free[b2]] if b2 in bank_free else []
        mm2_done = []
        pend_stats = []

        def emit_pw_load(g):
            par = g % 2
            buf = wsA if par == 0 else wsB
            sem = "s0" if par == 0 else "s1"
            waits = []
            if stream_free[par] is not None:
                waits.append(stream_free[par])
            def pwdma(e, g=g, buf=buf):
                src = projw[:, g * 512:(g + 1) * 512].rearrange(
                    "(c p) f -> p c f", p=128)
                dst = buf[:, :].rearrange("p (c f) -> p c f", c=8)
                return e.dma_start(out=dst, in_=src)
            S.op("gpsimd", pwdma, wait=waits, inc=sem, k=16)
            pw_ld[g] = (sem, S.n(sem))

        def emit_xj_load(dmc):
            par = dmc % 2
            xcb = xjb0 if par == 0 else xjb1
            sem = "x0" if par == 0 else "x1"
            xw = []
            if xj_free[par] is not None:
                xw.append(xj_free[par])
            else:
                xw.append(("tC", conv_all_mm))
            def xdma(e, dmc=dmc, xcb=xcb, hofs=hofs):
                return e.dma_start(
                    out=xcb[:, :],
                    in_=xT[dmc * 128:(dmc + 1) * 128, 1 + hofs:1 + hofs + TH])
            S.op("gpsimd", xdma, wait=xw, inc=sem, k=16)
            xj_ld[dmc] = (sem, S.n(sem))

        emit_pw_load(0)
        emit_pw_load(1)
        emit_xj_load(0)
        emit_xj_load(1)
        res1_evac = []
        _sw = {4: [bank_free[4]] if 4 in bank_free else [],
               5: [bank_free[5]] if 5 in bank_free else []}

        def emit_ln1_stat(dmc):
            def sq(e, dmc=dmc):
                return e.activation(
                    sqtmp[:, (dmc % 2) * 512:(dmc % 2) * 512 + 512],
                    res1_ap(dmc), AF.Square)
            sqw = [("cC", res1_evac[dmc])]
            if dmc >= 2:
                sqw.append(("tC", mm2_done[dmc - 2]))
            S.op("scalar", sq, wait=sqw)
            c_sq = S.n("cC")
            def mm1(e, dmc=dmc):
                return e.matmul(pbank(4), ones128[:, :], res1_ap(dmc),
                                start=(dmc == 0), stop=(dmc == DC - 1),
                                skip_group_check=True)
            S.op("tensor", mm1, wait=_sw[4] + [("cC", res1_evac[dmc])],
                 inc="tC" if dmc == DC - 1 else None)
            _sw[4] = []
            def mm2(e, dmc=dmc):
                return e.matmul(pbank(5), ones128[:, :],
                                sqtmp[:, (dmc % 2) * 512:
                                      (dmc % 2) * 512 + 512],
                                start=(dmc == 0), stop=(dmc == DC - 1),
                                skip_group_check=True)
            S.op("tensor", mm2, wait=_sw[5] + [("cC", c_sq)], inc="tC")
            _sw[5] = []
            mm2_done.append(S.n("tC"))
        for dmc in range(DC):
            hit_slot()
            g = dmc // 4
            par = g % 2
            buf = wsA if par == 0 else wsB
            xcb = xjb0 if dmc % 2 == 0 else xjb1
            bank = tile_i % 4
            waits = [pw_ld[g], a1_done]
            if bank in bank_free:
                waits.append(bank_free[bank])
            for wc in range(WC):
                def pmm(e, dmc=dmc, wc=wc, bank=bank, buf=buf,
                        first=(wc == 0)):
                    return e.matmul(
                        pbank(bank),
                        buf[:, wc * 512 + (dmc % 4) * 128:
                            wc * 512 + (dmc % 4) * 128 + 128],
                        a1b[:, wc * TH:(wc + 1) * TH],
                        start=first, stop=False)
                S.op("tensor", pmm, wait=waits)
                waits = []
            def pxadd(e, dmc=dmc, bank=bank, xcb=xcb):
                return e.matmul(pbank(bank), idb[:, :], xcb[:, :],
                                start=False, stop=True)
            S.op("tensor", pxadd, wait=[xj_ld[dmc], sd_idb], inc="tC")
            t_grp = S.n("tC")
            xj_free[dmc % 2] = ("tC", t_grp)
            if dmc % 4 == 3:
                stream_free[par] = ("tC", t_grp)
            pw_ = [("tC", t_grp)]
            if half == 1 and dmc < 2:
                pw_.extend(ln_state[0]["applied"])
            def pevac(e, dmc=dmc, bank=bank):
                return e.activation(
                    res1_ap(dmc), pbank(bank), AF.Identity,
                    bias=projb[:, dmc:dmc + 1], scale=1.0)
            S.op("scalar", pevac, wait=pw_)
            bank_free[bank] = ("cC", S.n("cC"))
            c_evac = S.n("cC")
            res1_evac.append(c_evac)
            tile_i += 1
            if dmc % 4 == 3 and dmc // 4 + 2 < 8:
                emit_pw_load(dmc // 4 + 2)
            if dmc + 2 < DC:
                emit_xj_load(dmc + 2)
            if half == 1:
                pend_stats.append(dmc)
                if len(pend_stats) > 4:
                    emit_ln1_stat(pend_stats.pop(0))
        if half == 1:
            while pend_stats:
                emit_ln1_stat(pend_stats.pop(0))
            st["b1"], st["b2"], st["t_stats"] = 4, 5, S.n("tC")
            st["res1_evac"] = res1_evac
            ln_state[half] = st
            return
        b1, b2 = (0, 1) if half == 0 else (4, 5)
        stat_w1 = [bank_free[b1]] if b1 in bank_free else []
        stat_w2 = [bank_free[b2]] if b2 in bank_free else []
        for dmc in range(DC):
            def sq(e, dmc=dmc):
                return e.activation(
                    sqtmp[:, (dmc % 2) * 512:(dmc % 2) * 512 + 512],
                    res1_ap(dmc), AF.Square)
            sqw = [("cC", res1_evac[dmc])]
            if dmc >= 2:
                sqw.append(("tC", mm2_done[dmc - 2]))
            S.op("scalar", sq, wait=sqw)
            c_sq = S.n("cC")
            def mm1(e, dmc=dmc, b1=b1):
                return e.matmul(pbank(b1), ones128[:, :], res1_ap(dmc),
                                start=(dmc == 0), stop=(dmc == DC - 1),
                                skip_group_check=True)
            S.op("tensor", mm1,
                 wait=stat_w1 + [("cC", res1_evac[dmc])],
                 inc="tC" if dmc == DC - 1 else None)
            stat_w1 = []
            def mm2(e, dmc=dmc, b2=b2):
                return e.matmul(pbank(b2), ones128[:, :],
                                sqtmp[:, (dmc % 2) * 512:
                                      (dmc % 2) * 512 + 512],
                                start=(dmc == 0), stop=(dmc == DC - 1),
                                skip_group_check=True)
            S.op("tensor", mm2, wait=stat_w2 + [("cC", c_sq)], inc="tC")
            stat_w2 = []
            mm2_done.append(S.n("tC"))
        st["b1"], st["b2"], st["t_stats"] = b1, b2, S.n("tC")
        st["res1_evac"] = res1_evac
        ln_state[half] = st
    def emit_ln_chain(st, nm_t, rstd_t, make_bf=False):
        b1, b2, t_stats = st["b1"], st["b2"], st["t_stats"]
        cw_ = [("tC", t_stats)]
        if "extra_chain_wait" in st:
            cw_.append(st["extra_chain_wait"])
        S.op("scalar", lambda e: e.activation(
            nm_t[:, :], pbank(b1), AF.Copy, bias=0.0, scale=-1.0 / D),
            wait=cw_)
        S.op("scalar", lambda e: e.activation(
            tmp1[:, :], pbank(b1), AF.Square, bias=0.0, scale=1.0 / D))
        S.op("scalar", lambda e: e.activation(
            tmp2[:, :], pbank(b2), AF.Copy, bias=1e-5, scale=1.0 / D))
        c_st = S.n("cC")
        bank_free[b1] = ("cC", c_st)
        bank_free[b2] = ("cC", c_st)
        S.op("vector", lambda e: e.tensor_sub(tmp2[:, :], tmp2[:, :],
                                              tmp1[:, :]),
             wait=[("cC", c_st)])
        v_r = S.n("vC")
        S.op("scalar", lambda e: e.activation(
            tmp1[:, :], tmp2[:, :], AF.Ln), wait=[("vC", v_r)])
        S.op("scalar", lambda e, rstd_t=rstd_t: e.activation(
            rstd_t[:, :], tmp1[:, :], AF.Exp, bias=0.0, scale=-0.5))
        c_rstd = S.n("cC")
        if make_bf:
            S.op("vector", lambda e, nm_t=nm_t: e.tensor_copy(
                nmb[:, :], nm_t[:, :]), wait=[("cC", c_rstd)])
            S.op("vector", lambda e, rstd_t=rstd_t: e.tensor_copy(
                rstdb[:, :], rstd_t[:, :]))
            return ("vC", S.n("vC"))
        return ("cC", c_rstd)

    def p45_bank(i):
        return i % 4

    def _x1_grp(half, g):
        if half == 0:
            return x1bA[:, g * 2048:(g + 1) * 2048]
        if g < 4:
            return x1bB_lo[:, g * 2048:(g + 1) * 2048]
        return x1bB_hi[:, (g - 4) * 2048:(g - 3) * 2048]

    def emit_ln1_apply(half):
        st = ln_state[half]
        gate = emit_ln_chain(st, negmean, rstd, make_bf=True)
        for g in range(8):
            r = res1_grp(g).rearrange("p (c f) -> p c f", c=4)
            xo = _x1_grp(half, g).rearrange("p (c f) -> p c f", c=4)
            eng = "gpsimd" if (half == 1 and g % 2 == 1) else "vector"
            S.op(eng, lambda e, r=r: e.tensor_add(
                r, r, bc3(nmb[:, :], 4)), wait=[gate])
            S.op(eng, lambda e, r=r, xo=xo: e.tensor_mul(
                xo, r, bc3(rstdb[:, :], 4)))
        st["applied"] = [("vC", S.n("vC")), ("gC", S.n("gC"))]

    def emit_ht(half):
        nonlocal tile_i
        for erc in range(ERC):
            bank = p45_bank(tile_i)
            waits = [g_at] + ln_state[half]["applied"]
            if bank in bank_free:
                waits.append(bank_free[bank])
            for dc in range(DC):
                def hmm(e, erc=erc, dc=dc, bank=bank, half=half,
                        first=(dc == 0), last=(dc == DC - 1)):
                    return e.matmul(
                        pbank(bank),
                        atb[:, dc * 256 + erc * 128:dc * 256 + erc * 128 + 128],
                        x1b_ap(half, dc),
                        start=first, stop=last)
                S.op("tensor", hmm, wait=waits,
                     inc="tC" if dc == DC - 1 else None)
                waits = []
            t_grp = S.n("tC")
            def hevac(e, erc=erc, bank=bank, half=half):
                return e.tensor_scalar_mul(
                    hb[:, erc * 1024 + half * TH:erc * 1024 + half * TH + TH],
                    pbank(bank), probs[:, erc:erc + 1])
            S.op("vector", hevac, wait=[("tC", t_grp)])
            bank_free[bank] = ("vC", S.n("vC"))
            tile_i += 1

    emit_p4(0, slots=attn_rest)
    p3_done = [("cC", S.n("cC")), ("vC", S.n("vC")), ("gC", S.n("gC")),
               ("tC", S.n("tC"))]
    emit_ln1_apply(0)
    emit_p4(1)
    p4_t_end = ("tC", S.n("tC"))
    def atdma(e):
        src = At[:, :].rearrange("(c p) f -> p c f", p=128)
        dst = atb[:, :].rearrange("p (c f) -> p c f", c=32)
        return e.dma_start(out=dst, in_=src)
    S.op("gpsimd", atdma, wait=list(p3_done) + [g_wv], inc="w2", k=16)
    g_at = ("w2", S.n("w2"))
    emit_ht(0)
    hb_done_h0 = ("vC", S.n("vC"))
    emit_ln1_apply(1)

    wt_ld = {}
    def emit_wt_load(pg):
        par = pg % 2
        buf = wsF0 if par == 0 else wsF1
        sem = "s0" if par == 0 else "s1"
        waits = [] if pg >= 2 else list(wt_first_gate)
        if stream_free[par] is not None:
            waits.append(stream_free[par])
        def wtdma(e, pg=pg, buf=buf):
            src = Wt[:, pg * 256:(pg + 1) * 256].rearrange(
                "(c p) f -> p c f", p=128)
            dst = buf[:, :].rearrange("p (c f) -> p c f", c=32)
            return e.dma_start(out=dst, in_=src)
        S.op("gpsimd", wtdma, wait=waits, inc=sem, k=16)
        wt_ld[pg] = (sem, S.n(sem))

    t_p4done = ("tC", S.n("tC"))
    wt_first_gate = [t_p4done]
    emit_wt_load(0)
    emit_wt_load(1)
    emit_ht(1)
    t_ht_end = ("tC", S.n("tC"))
    hb_done = ("vC", S.n("vC"))
    for erc in range(ERC):
        def bmdma2(e, erc=erc):
            return e.dma_start(
                out=Bmb[:, erc * D:(erc + 1) * D],
                in_=Bm[erc * 128:(erc + 1) * 128, :])
        S.op("gpsimd", bmdma2, wait=[t_ht_end], inc="w2", k=16)
    g_bm = ("w2", S.n("w2"))
    ln2 = {0: dict(b1=4, b2=5, mm2_done=[]), 1: dict(b1=6, b2=7, mm2_done=[])}
    for hset in ln2.values():
        for bk in (hset["b1"], hset["b2"]):
            hset.setdefault("w", {})[bk] = (
                [bank_free[bk]] if bk in bank_free else [])
    pend_stats2 = []
    res2_writes = {}

    def emit_stats2(oc, half, c_evac):
        hs = ln2[half]
        def sq2(e, oc=oc, half=half):
            return e.activation(
                sqtmp[:, half * 512:half * 512 + 512],
                res2_ap(oc)[:, half * TH:half * TH + TH], AF.Square)
        sqw = [("cC", c_evac)]
        if oc >= 1:
            sqw.append(("tC", hs["mm2_done"][oc - 1]))
        S.op("scalar", sq2, wait=sqw)
        c_sq = S.n("cC")
        def mm1(e, oc=oc, half=half, b1=hs["b1"]):
            return e.matmul(
                pbank(b1), ones128[:, :],
                res2_ap(oc)[:, half * TH:half * TH + TH],
                start=(oc == 0), stop=(oc == DC - 1),
                skip_group_check=True)
        S.op("tensor", mm1, wait=hs["w"][hs["b1"]] + [("cC", c_evac)],
             inc="tC" if oc == DC - 1 else None)
        hs["w"][hs["b1"]] = []
        def mm2(e, oc=oc, half=half, b2=hs["b2"]):
            return e.matmul(
                pbank(b2), ones128[:, :],
                sqtmp[:, half * 512:half * 512 + 512],
                start=(oc == 0), stop=(oc == DC - 1),
                skip_group_check=True)
        S.op("tensor", mm2, wait=hs["w"][hs["b2"]] + [("cC", c_sq)],
             inc="tC")
        hs["w"][hs["b2"]] = []
        hs["mm2_done"].append(S.n("tC"))

    for pg in range(16):
        par = pg % 2
        buf = wsF0 if par == 0 else wsF1
        for half in range(2):
            for sub in range(2):
                oc = pg * 2 + sub
                bank = p45_bank(tile_i)
                waits = [wt_ld[pg], g_bm,
                         hb_done_h0 if half == 0 else hb_done]
                if bank in bank_free:
                    waits.append(bank_free[bank])
                for dc in range(DC):
                    def fmm(e, dc=dc, bank=bank, buf=buf, half=half,
                            sub=sub, first=(dc == 0)):
                        return e.matmul(
                            pbank(bank),
                            buf[:, dc * 256 + sub * 128:
                                dc * 256 + sub * 128 + 128],
                            x1b_ap(half, dc),
                            start=first, stop=False)
                    S.op("tensor", fmm, wait=waits)
                    waits = []
                for erc in range(ERC):
                    def bmm(e, oc=oc, erc=erc, bank=bank, half=half):
                        return e.matmul(
                            pbank(bank),
                            Bmb[:, erc * D + oc * 128:erc * D + oc * 128 + 128],
                            hb[:, erc * 1024 + half * TH:
                               erc * 1024 + half * TH + TH],
                            start=False, stop=False)
                    S.op("tensor", bmm)
                def fxadd(e, oc=oc, bank=bank, half=half):
                    return e.matmul(pbank(bank), idb[:, :],
                                    x1b_ap(half, oc),
                                    start=False, stop=True)
                S.op("tensor", fxadd, wait=[sd_idb], inc="tC")
                t_grp = S.n("tC")
                def fevac(e, oc=oc, bank=bank, half=half):
                    return e.activation(
                        res2_ap(oc)[:, half * TH:half * TH + TH],
                        pbank(bank), AF.Identity,
                        bias=ffnb[:, oc:oc + 1], scale=1.0)
                S.op("scalar", fevac, wait=[("tC", t_grp)])
                bank_free[bank] = ("cC", S.n("cC"))
                c_evac = S.n("cC")
                res2_writes[(oc, half)] = c_evac
                tile_i += 1
                pend_stats2.append((oc, half, c_evac))
                if len(pend_stats2) > 4:
                    emit_stats2(*pend_stats2.pop(0))
        stream_free[par] = ("tC", S.n("tC"))
        if pg + 2 < 16:
            emit_wt_load(pg + 2)
    while pend_stats2:
        emit_stats2(*pend_stats2.pop(0))

    ob_free = [None, None, None, None]
    apply_specs = []
    for half in range(2):
        hs = ln2[half]
        st = dict(b1=hs["b1"], b2=hs["b2"], t_stats=S.n("tC"))
        nm_t = negmean if half == 0 else nm2
        rs_t = rstd if half == 0 else rstd2
        gate = emit_ln_chain(st, nm_t, rs_t, make_bf=False)
        apply_specs.append((half, nm_t, rs_t, gate))
    for dc in range(DC):
        for half, nm_t, rs_t, gate in apply_specs:
            slot = (dc % 2) * 2 + half
            osem = f"o{slot}"
            hofs = half * TH
            eng = "vector" if half == 0 else "gpsimd"
            ww = [gate]
            if ob_free[slot] is not None:
                ww.append(ob_free[slot])
            def app1(e, dc=dc, slot=slot, hofs=hofs, nm_t=nm_t):
                return e.tensor_add(ob[slot][:, :],
                                    res2_ap(dc)[:, hofs:hofs + TH],
                                    nm_t[:, :])
            def app2(e, slot=slot, rs_t=rs_t):
                return e.tensor_mul(ob[slot][:, :], ob[slot][:, :],
                                    rs_t[:, :])
            if eng == "vector":
                S.op("vector", app1, wait=ww)
                S.op("vector", app2)
                v_ap = ("vC", S.n("vC"))
            else:
                S.op("gpsimd", app1, wait=ww, inc="gC")
                S.op("gpsimd", app2, inc="gC")
                v_ap = ("gC", S.n("gC"))
            def sdma(e, dc=dc, slot=slot, hofs=hofs):
                return e.dma_start(
                    out=outT[dc * 128:(dc + 1) * 128, hofs:hofs + TH],
                    in_=ob[slot][:, :])
            S.op("sync", sdma, wait=[v_ap], inc=osem, k=16)
            ob_free[slot] = (osem, S.n(osem))

    if tap_outs:
        wait_all = [("tC", S.n("tC")), ("vC", S.n("vC")), ("cC", S.n("cC"))]
        tap_srcs = dict(
            qkvb=qkvb, qhb=qhb, khb=khb, vtb=vtb, otb=otb, gneg=gneg,
            a1b=a1b, hbt=hb, m1all=m1all, pb0=pb0, pb1=pb1, linv0=linv0,
            rstd=rstd, x1bA=x1bA, res1=res1,
        )
        for tname, ap_d in tap_outs.items():
            src = tap_srcs[tname]
            S.op("sync", lambda e, d=ap_d, s=src: e.dma_start(
                out=d, in_=s[:, :]), wait=wait_all, inc="o0", k=16)

    fin = [(f"o{i}", S.n(f"o{i}")) for i in range(4)]
    S.op("sync", lambda e: e.nop(), wait=fin)

    with ExitStack() as stx:
        sems = {name: stx.enter_context(nc.semaphore(name)) for name in SEMS}
        blk = stx.enter_context(nc.Block())
        for eng in Sched.ENGS:
            ops = S.q[eng]
            if not ops:
                continue
            getattr(blk, eng)(lambda e, ops=ops: [f(e, sems) for f in ops])
    return nc


def stage_inputs(inputs):
    f32 = np.float32
    x = np.asarray(inputs["L0_attn_inputs"], f32)
    N, L, _ = x.shape
    per_core = []
    for c in range(8):
        n, s = divmod(c, 2)
        t0 = s * 1024
        xTh = np.zeros((D, T + 2), f32)
        lo = max(t0 - 1, 0)
        hi = min(t0 + T + 1, L)
        xTh[:, lo - (t0 - 1):hi - (t0 - 1)] = x[n, lo:hi, :].T
        per_core.append(np.ascontiguousarray(xTh))

    shared = {
        "cw": np.ascontiguousarray(np.asarray(inputs["L0_attn_conv_w"], f32)),
        "wq": np.ascontiguousarray(np.asarray(inputs["L0_attn_wq"], f32).reshape(W1, W1)),
        "wk": np.ascontiguousarray(np.asarray(inputs["L0_attn_wk"], f32).reshape(W1, W1)),
        "wv": np.ascontiguousarray(np.asarray(inputs["L0_attn_wv"], f32).reshape(W1, W1)),
        "wo": np.ascontiguousarray(np.asarray(inputs["L0_attn_wo"], f32).reshape(W1, W1)),
        "projw": np.ascontiguousarray(np.asarray(inputs["L0_attn_proj_w"], f32)),
        "Wt": np.ascontiguousarray(np.asarray(inputs["L0_ffn_W"], f32).T),
        "At": np.ascontiguousarray(np.asarray(inputs["L0_ffn_A"], f32).reshape(256, D).T),
        "Bm": np.ascontiguousarray(np.asarray(inputs["L0_ffn_B"], f32).transpose(0, 2, 1).reshape(256, D)),
        "convb": np.ascontiguousarray(np.asarray(inputs["L0_attn_conv_b"], f32).reshape(QC, 128).T),
        "bq": np.ascontiguousarray(np.asarray(inputs["L0_attn_bq"], f32).reshape(WC, 128).T),
        "bk": np.ascontiguousarray(np.asarray(inputs["L0_attn_bk"], f32).reshape(WC, 128).T),
        "bo": np.ascontiguousarray(np.asarray(inputs["L0_attn_bo"], f32).reshape(WC, 128).T),
        "projb": np.ascontiguousarray(np.asarray(inputs["L0_attn_proj_b"], f32).reshape(DC, 128).T),
        "ffnb": np.ascontiguousarray(np.asarray(inputs["L0_ffn_b"], f32).reshape(DC, 128).T),
        "probs": np.ascontiguousarray(
            np.repeat(np.asarray(inputs["L0_ffn_probs"], f32), 16).reshape(ERC, 128).T),
        "bv": np.ascontiguousarray(np.asarray(inputs["L0_attn_bv"], f32).reshape(1, W1)),
        "ones128": np.ones((128, 128), BF16N),
        "ones1": np.ones((1, 128), BF16N),
        "mask": np.ascontiguousarray(
            np.tile(np.triu(np.ones((128, 128), np.float32)), (1, 16)).astype(BF16N)),
        "idf": np.eye(128, dtype=f32),
        "idb": np.eye(128, dtype=BF16N),
    }
    in_maps = [{"xT": per_core[c], **shared} for c in range(8)]
    return in_maps


def kernel(**inputs):
    in_maps = stage_inputs(inputs)
    nc = build_nc()
    res = run_bass_kernel_spmd(nc, in_maps, core_ids=list(range(8)))
    out = np.empty((4, 2048, D), np.float32)
    for c in range(8):
        n, s = divmod(c, 2)
        out[n, s * 1024:(s + 1) * 1024, :] = res.results[c]["outT"].T
    return out


# version 14
# speedup vs baseline: 1.0284x; 1.0284x over previous
import numpy as np
import ml_dtypes
from contextlib import ExitStack
import concourse.bass as bass
import concourse.mybir as mybir
from concourse.bass_utils import run_bass_kernel_spmd

BF16N = ml_dtypes.bfloat16
F32 = mybir.dt.float32
BF16 = mybir.dt.bfloat16
AF = mybir.ActivationFunctionType
ALU = mybir.AluOpType
AX = mybir.AxisListType

D = 4096
DC = 32
W1 = 1024
WC = 8
QC = 24
T = 1024
TH = 512
H = 16
NB = 8
ERC = 2

OFF_ONES128 = 0
OFF_MASK = 256
OFF_IDF32 = 4352
OFF_CONVB = 4864
OFF_BQS = 4992
OFF_BK = 5024
OFF_BO = 5056
OFF_PROJB = 5088
OFF_FFNB = 5216
OFF_PROBS = 5344
OFF_ONES1 = 5376
OFF_BVF = 5632
OFF_BVB = 9728
OFF_M1ALL = 11776
OFF_NEGG8 = 11808
OFF_NEGROWF = 11840
OFF_NEGROWB = 11872
OFF_GNEG = 11904
OFF_SQTMP = 11936
OFF_STATS = 14016
OFF_HBOX = 22208
OFF_LINV = 26304
REG0 = 42688
REG1 = 108352
REG2 = 157504
OFF_SPARE = 206656
OFF_WK7 = 210752


import dataclasses


def bc3(ap2d, reps):
    return dataclasses.replace(ap2d, ap=[ap2d.ap[0], [0, reps], ap2d.ap[1]])


class Sched:
    ENGS = ("sync", "gpsimd", "tensor", "scalar", "vector")

    def __init__(self):
        self.q = {e: [] for e in self.ENGS}
        self.cnt = {}
        self.hw = {e: {} for e in self.ENGS}

    def n(self, sem):
        return self.cnt.get(sem, 0)

    def op(self, eng, fn, wait=(), inc=None, k=1):
        if eng == "vector":
            wait = list(wait) + [("vC", self.n("vC"))]
            if inc is None:
                inc = "vC"
        elif eng == "gpsimd":
            wait = list(wait) + [("gC", self.n("gC"))]
            if inc is None and k == 1:
                inc = "gC"
        elif eng == "scalar":
            wait = list(wait) + [("cC", self.n("cC"))]
            if inc is None:
                inc = "cC"
        waits = []
        for s, v in wait:
            if v is None or v <= 0:
                continue
            if self.hw[eng].get(s, 0) >= v:
                continue
            self.hw[eng][s] = v
            waits.append((s, v))
        if inc is not None:
            self.cnt[inc] = self.cnt.get(inc, 0) + k

        def run(e, sems, fn=fn, waits=waits, inc=inc, k=k):
            for s, v in waits:
                e.wait_ge(sems[s], v)
            inst = fn(e)
            if inc is not None:
                inst.then_inc(sems[inc], k)

        self.q[eng].append(run)
        return self.cnt.get(inc, 0) if inc is not None else None


def build_nc(taps=(), stop_after='full'):
    nc = bass.Bass("TRN2", target_bir_lowering=False)
    dt = lambda name, shape, dtype=F32: nc.dram_tensor(
        name, list(shape), dtype, kind="ExternalInput").ap()

    xT = dt("xT", (D, T + 2))
    cw = dt("cw", (3, D, 3 * W1))
    wq = dt("wq", (W1, W1))
    wk = dt("wk", (W1, W1))
    wv = dt("wv", (W1, W1))
    wo = dt("wo", (W1, W1))
    projw = dt("projw", (W1, D))
    Wt = dt("Wt", (D, D))
    At = dt("At", (D, 2 * 128))
    Bm = dt("Bm", (2 * 128, D))
    convb_d = dt("convb", (128, QC))
    bq_d = dt("bq", (128, WC))
    bk_d = dt("bk", (128, WC))
    bo_d = dt("bo", (128, WC))
    projb_d = dt("projb", (128, DC))
    ffnb_d = dt("ffnb", (128, DC))
    probs_d = dt("probs", (128, ERC))
    bv_d = dt("bv", (1, W1))
    ones128_d = dt("ones128", (128, 128), BF16)
    ones1_d = dt("ones1", (1, 128), BF16)
    mask_d = dt("mask", (128, 2048), BF16)
    idf_d = dt("idf", (128, 128), F32)
    idb_d = dt("idb", (128, 128), BF16)

    outT = nc.dram_tensor("outT", [D, T], F32, kind="ExternalOutput").ap()
    tap_outs = {}
    for tname, shape, tdt in taps:
        tap_outs[tname] = nc.dram_tensor(
            "tap_" + tname, list(shape), tdt, kind="ExternalOutput").ap()

    base = (nc._sbuf_addr_for_side("left") + 63) // 64 * 64

    def sb(name, off, shape, dtype):
        return nc.alloc_sbuf_tensor_at(name, list(shape), dtype,
                                       offset=base + off)

    ones128 = sb("ones128_s", OFF_ONES128, (128, 128), BF16)
    maskt = sb("maskt", OFF_MASK, (128, 2048), BF16)
    idf = sb("idf_s", OFF_IDF32, (128, 128), F32)
    convb = sb("convb_s", OFF_CONVB, (128, QC), F32)
    bqs = sb("bqs", OFF_BQS, (128, WC), F32)
    bks = sb("bks", OFF_BK, (128, WC), F32)
    bos = sb("bos", OFF_BO, (128, WC), F32)
    projb = sb("projb_s", OFF_PROJB, (128, DC), F32)
    ffnb = sb("ffnb_s", OFF_FFNB, (128, DC), F32)
    probs = sb("probs_s", OFF_PROBS, (128, ERC), F32)
    ones1 = sb("ones1_s", OFF_ONES1, (1, 128), BF16)
    bvf = sb("bvf", OFF_BVF, (1, W1), F32)
    bvb = sb("bvb", OFF_BVB, (1, W1), BF16)
    m1all = sb("m1all", OFF_M1ALL, (128, NB), F32)
    negg8 = sb("negg8", OFF_NEGG8, (NB, 1), F32)
    negrowf = sb("negrowf", OFF_NEGROWF, (1, NB), F32)
    negrowb = sb("negrowb", OFF_NEGROWB, (1, NB), BF16)
    gneg = sb("gneg", OFF_GNEG, (128, NB), F32)
    sqtmp = sb("sqtmp", OFF_SQTMP, (128, 1024), BF16)
    negmean = sb("negmean", OFF_STATS, (128, TH), F32)
    tmp1 = sb("tmp1", OFF_STATS + 2048, (128, TH), F32)
    tmp2 = sb("tmp2", OFF_STATS + 4096, (128, TH), F32)
    rstd = sb("rstd", OFF_STATS + 6144, (128, TH), F32)
    nm2 = sb("nm2", OFF_LINV + 8192, (128, TH), F32)
    rstd2 = sb("rstd2", OFF_LINV + 10240, (128, TH), F32)
    nmb = sb("nmb", OFF_HBOX, (128, TH), BF16)
    rstdb = sb("rstdb", OFF_HBOX + 1024, (128, TH), BF16)
    idb = sb("idb", OFF_BVF, (128, 128), BF16)
    hb = sb("hb", OFF_SPARE, (128, 2048), BF16)
    linv0 = sb("linv0", OFF_LINV, (128, 2048), F32)
    linv1 = sb("linv1", OFF_LINV + 8192, (128, 2048), F32)
    atb = sb("atb", OFF_LINV, (128, 32 * 256), BF16)
    ob = [sb(f"ob{i}", OFF_LINV + i * 2048, (128, TH), F32) for i in range(4)]
    wqpre = [sb(f"wqpre{c}", OFF_LINV + c * 2048, (128, 1024), BF16)
             for c in range(8)]
    wk_offs = [OFF_STATS, OFF_STATS + 2048, OFF_STATS + 4096, OFF_STATS + 6144,
               OFF_HBOX, OFF_HBOX + 2048, OFF_SQTMP, OFF_WK7]
    wkpre = [sb(f"wkpre{c}", wk_offs[c], (128, 1024), BF16) for c in range(8)]
    xb = sb("xb", REG0, (128, DC * (T + 2)), BF16)
    wvb = sb("wvb", REG0 + 32768, (128, 8 * 1024), BF16)
    otb = sb("otb", REG0, (128, 8 * 1024), BF16)
    x1bA = sb("x1bA", REG0 + 16384, (128, DC * TH), BF16)
    x1bB_lo = sb("x1bB_lo", REG0, (128, 16 * TH), BF16)
    x1bB_hi = sb("x1bB_hi", REG0 + 49152, (128, 16 * TH), BF16)
    qkvb = sb("qkvb", REG1, (128, QC * 1024), BF16)
    pb0 = sb("pb0", REG1 + 16384, (128, 2048), BF16)
    pb1 = sb("pb1", REG1 + 20480, (128, 2048), BF16)
    wsA = sb("wsA", REG1, (128, 4096), BF16)
    wsB = sb("wsB", REG1 + 8192, (128, 4096), BF16)
    Bmb = sb("Bmb", OFF_LINV, (128, 2 * D), BF16)
    wsF0 = sb("wsF0", REG1, (128, 8192), BF16)
    wsF1 = sb("wsF1", REG1 + 16384, (128, 8192), BF16)
    res2_m = sb("res2_m", REG1 + 32768, (128, 8 * 1024), BF16)
    cwb0 = sb("cwb0", REG2, (128, 3 * 4096), BF16)
    cwb1 = sb("cwb1", REG2 + 24576, (128, 3 * 4096), BF16)
    qhb = sb("qhb", REG2, (128, 8 * 1024), BF16)
    khb = sb("khb", REG2 + 16384, (128, 8 * 1024), BF16)
    vtb = sb("vtb", REG2 + 32768, (128, 8 * 1024), BF16)
    res1lo = sb("res1lo", REG1 + 32768, (128, 8 * TH), BF16)
    res1hi = sb("res1hi", REG2, (128, 24 * TH), BF16)
    a1b = sb("a1b", REG1 + 24576, (128, WC * TH), BF16)
    xjb0 = sb("xjb0", REG0 + 53248, (128, TH), BF16)
    xjb1 = sb("xjb1", REG0 + 54272, (128, TH), BF16)
    res2_lo = sb("res2_lo", REG2, (128, 16 * 1024), BF16)
    res2_hi = sb("res2_hi", REG2 + 32768, (128, 8 * 1024), BF16)

    psA = nc.alloc_psum_tensor("psA", [128, 2048], F32)
    psB = nc.alloc_psum_tensor("psB", [128, 2048], F32)

    def pbank(k):
        t = psA if k < 4 else psB
        return t[:, (k % 4) * 512:(k % 4 + 1) * 512]

    def pset(p):
        return psA if p == 0 else psB

    S = Sched()
    SEMS = ("xS", "c0", "c1", "w2", "s0", "s1", "sD", "gC",
            "x0", "x1", "o0", "o1", "o2", "o3", "tC", "vC", "cC")

    def res1_ap(c):
        if c < 8:
            return res1lo[:, c * TH:(c + 1) * TH]
        return res1hi[:, (c - 8) * TH:(c - 7) * TH]

    def res1_grp(g):
        if g < 2:
            return res1lo[:, g * 2048:(g + 1) * 2048]
        return res1hi[:, (g - 2) * 2048:(g - 1) * 2048]

    def x1b_ap(half, c):
        if half == 0:
            return x1bA[:, c * TH:(c + 1) * TH]
        if c < 16:
            return x1bB_lo[:, c * TH:(c + 1) * TH]
        return x1bB_hi[:, (c - 16) * TH:(c - 15) * TH]

    def res2_ap(c):
        if c < 16:
            return res2_lo[:, c * 1024:(c + 1) * 1024]
        if c < 24:
            return res2_m[:, (c - 16) * 1024:(c - 15) * 1024]
        return res2_hi[:, (c - 24) * 1024:(c - 23) * 1024]

    for sbuf_t, dram_ap in (
            (ones128, ones128_d), (maskt, mask_d), (idf, idf_d),
            (convb, convb_d), (bqs, bq_d), (bks, bk_d), (bos, bo_d),
            (projb, projb_d), (ffnb, ffnb_d), (probs, probs_d),
            (ones1, ones1_d), (bvf, bv_d)):
        S.op("sync", lambda e, o=sbuf_t, i=dram_ap: e.dma_start(
            out=o[:, :], in_=i), inc="sD", k=16)
    n_const = S.n("sD")
    S.op("vector", lambda e: e.tensor_scalar_mul(bqs[:, :], bqs[:, :], 0.125),
         wait=[("sD", n_const)])
    S.op("vector", lambda e: e.tensor_copy(bvb[:, :], bvf[:, :]))
    v_prep = S.n("vC")
    S.op("sync", lambda e: e.dma_start(out=idb[:, :], in_=idb_d),
         wait=[("vC", v_prep)], inc="sD", k=16)
    sd_idb = ("sD", S.n("sD"))

    for i in range(4):
        def xldma(e, i=i):
            src = xT[i * 8 * 128:(i + 1) * 8 * 128, :].rearrange(
                "(c p) f -> p c f", p=128)
            dst = xb[:, i * 8 * (T + 2):(i + 1) * 8 * (T + 2)].rearrange(
                "p (c f) -> p c f", c=8)
            return e.dma_start(out=dst, in_=src)
        S.op("gpsimd", xldma, inc="xS", k=16)
    g_xb = ("xS", S.n("xS"))

    conv_w_load_done = {}
    conv_last_mm_using_buf = {}

    def emit_conv_load(oc):
        par = oc % 2
        buf = cwb0 if par == 0 else cwb1
        sem = "c0" if par == 0 else "c1"
        waits = []
        if oc >= 2:
            waits.append(("tC", conv_last_mm_using_buf[oc - 2]))
        for w in range(3):
            for hh in range(2):
                def cwdma(e, oc=oc, w=w, hh=hh, buf=buf):
                    src = cw[w, hh * 2048:(hh + 1) * 2048,
                             oc * 128:(oc + 1) * 128].rearrange(
                        "(c p) f -> p c f", p=128)
                    dst = buf[:, w * 4096 + hh * 2048:
                              w * 4096 + (hh + 1) * 2048].rearrange(
                        "p (c f) -> p c f", c=16)
                    return e.dma_start(out=dst, in_=src)
                S.op("gpsimd", cwdma, wait=waits, inc=sem, k=16)
                waits = []
        conv_w_load_done[oc] = (sem, S.n(sem))

    def emit_pre_load(i):
        if i < 8:
            wt, dst = wq, wqpre[i]
            srcsl = wq[i * 128:(i + 1) * 128, :]
        else:
            wt, dst = wk, wkpre[i - 8]
            srcsl = wk[(i - 8) * 128:(i - 8 + 1) * 128, :]
        def pldma(e, srcsl=srcsl, dst=dst):
            return e.dma_start(out=dst[:, :], in_=srcsl)
        S.op("gpsimd", pldma, inc="w2", k=16)

    emit_conv_load(0)
    emit_conv_load(1)

    bank_free = {}
    tile_i = 0
    first_tensor = [("sD", n_const)]
    first_scalar = [("sD", n_const)]
    for oc in range(QC):
        par = oc % 2
        buf = cwb0 if par == 0 else cwb1
        for tokc in range(2):
            bank = tile_i % 8
            waits = [conv_w_load_done[oc], g_xb] + first_tensor
            first_tensor = []
            if bank in bank_free:
                waits.append(bank_free[bank])
            for dc in range(DC):
                for w in range(3):
                    def cmm(e, oc=oc, tokc=tokc, dc=dc, w=w, buf=buf,
                            bank=bank, first=(dc == 0 and w == 0),
                            last=(dc == DC - 1 and w == 2)):
                        return e.matmul(
                            pbank(bank),
                            buf[:, (w * 32 + dc) * 128:(w * 32 + dc + 1) * 128],
                            xb[:, dc * (T + 2) + tokc * 512 + w:
                               dc * (T + 2) + tokc * 512 + w + 512],
                            start=first, stop=last)
                    S.op("tensor", cmm, wait=waits,
                         inc="tC" if (dc == DC - 1 and w == 2) else None)
                    waits = []
            t_grp = S.n("tC")
            if tile_i % 3 != 2:
                def cevac(e, oc=oc, tokc=tokc, bank=bank):
                    return e.activation(
                        qkvb[:, oc * 1024 + tokc * 512:
                             oc * 1024 + tokc * 512 + 512],
                        pbank(bank), AF.Identity,
                        bias=convb[:, oc:oc + 1], scale=1.0)
                S.op("scalar", cevac, wait=[("tC", t_grp)] + first_scalar)
                first_scalar = []
                bank_free[bank] = ("cC", S.n("cC"))
            else:
                def cevac(e, oc=oc, tokc=tokc, bank=bank):
                    return e.tensor_scalar_add(
                        qkvb[:, oc * 1024 + tokc * 512:
                             oc * 1024 + tokc * 512 + 512],
                        pbank(bank), convb[:, oc:oc + 1])
                S.op("vector", cevac, wait=[("tC", t_grp)])
                bank_free[bank] = ("vC", S.n("vC"))
            tile_i += 1
        conv_last_mm_using_buf[oc] = S.n("tC")
        if oc + 2 < QC:
            emit_conv_load(oc + 2)
        if oc < 16:
            emit_pre_load(oc)
    conv_all_mm = S.n("tC")
    conv_all_evac = [("cC", S.n("cC")), ("vC", S.n("vC"))]
    g_pre = ("w2", S.n("w2"))

    def wvdma(e):
        src = wv[:, :].rearrange("(c p) f -> p c f", p=128)
        dst = wvb[:, :].rearrange("p (c f) -> p c f", c=8)
        return e.dma_start(out=dst, in_=src)
    S.op("gpsimd", wvdma, wait=[("tC", conv_all_mm), g_pre], inc="w2", k=16)
    g_wv = ("w2", S.n("w2"))

    def emit_qk(wchunks, qoff, outb, bias_col, scale):
        nonlocal tile_i
        for hc in range(WC):
            for tokc in range(2):
                bank = tile_i % 8
                waits = [g_pre] + conv_all_evac
                if bank in bank_free:
                    waits.append(bank_free[bank])
                for dc in range(8):
                    def qmm(e, hc=hc, tokc=tokc, dc=dc, bank=bank,
                            qoff=qoff, wcs=wchunks,
                            first=(dc == 0), last=(dc == 7)):
                        return e.matmul(
                            pbank(bank),
                            wcs[dc][:, hc * 128:hc * 128 + 128],
                            qkvb[:, (qoff + dc) * 1024 + tokc * 512:
                                 (qoff + dc) * 1024 + tokc * 512 + 512],
                            start=first, stop=last)
                    S.op("tensor", qmm, wait=waits,
                         inc="tC" if dc == 7 else None)
                    waits = []
                t_grp = S.n("tC")
                def qevac(e, hc=hc, tokc=tokc, bank=bank, outb=outb,
                          bias_col=bias_col, scale=scale):
                    return e.activation(
                        outb[:, hc * 1024 + tokc * 512:
                             hc * 1024 + tokc * 512 + 512],
                        pbank(bank), AF.Identity,
                        bias=bias_col[:, hc:hc + 1], scale=scale)
                S.op("scalar", qevac, wait=[("tC", t_grp), ("vC", v_prep)])
                bank_free[bank] = ("cC", S.n("cC"))
                tile_i += 1

    emit_qk(wqpre, 0, qhb, bqs, 0.125)
    emit_qk(wkpre, 8, khb, bks, 1.0)

    for tokt in range(8):
        for hec in range(2):
            bank = tile_i % 8
            waits = [g_wv] + conv_all_evac
            if bank in bank_free:
                waits.append(bank_free[bank])
            for dc in range(8):
                def vmm(e, tokt=tokt, hec=hec, dc=dc, bank=bank,
                        first=(dc == 0)):
                    return e.matmul(
                        pbank(bank),
                        qkvb[:, (16 + dc) * 1024 + tokt * 128:
                             (16 + dc) * 1024 + tokt * 128 + 128],
                        wvb[:, dc * 1024 + hec * 512:
                            dc * 1024 + hec * 512 + 512],
                        start=first, stop=False)
                S.op("tensor", vmm, wait=waits)
                waits = []
            def vbias(e, hec=hec, bank=bank):
                return e.matmul(pbank(bank), ones1[:, :],
                                bvb[0:1, hec * 512:hec * 512 + 512],
                                start=False, stop=True)
            S.op("tensor", vbias, wait=[("vC", v_prep)], inc="tC")
            t_grp = S.n("tC")
            def vevac(e, tokt=tokt, hec=hec, bank=bank):
                return e.tensor_copy(
                    vtb[:, tokt * 1024 + hec * 512:
                        tokt * 1024 + hec * 512 + 512],
                    pbank(bank))
            S.op("vector", vevac, wait=[("tC", t_grp)])
            bank_free[bank] = ("vC", S.n("vC"))
            tile_i += 1
    p2_done = [("cC", S.n("cC")), ("vC", S.n("vC"))]

    def scol(h):
        return (h % 2) * 1024 + (h // 2) * 128

    def qk_slice(buf, h, b):
        return buf[(h % 2) * 64:(h % 2) * 64 + 64,
                   (h // 2) * 1024 + b * 128:(h // 2) * 1024 + b * 128 + 128]

    p2_t_end = S.n("tC")
    pre_m1 = {}
    blk_oevac = {}
    blk_state = {}

    def emit_pre(b, ps, extra=()):
        waits = list(p2_done) + list(extra)
        horder = list(range(0, H, 2)) + list(range(1, H, 2))
        for i, h in enumerate(horder):
            def smm(e, b=b, h=h, ps=ps):
                return e.matmul(
                    ps[:, scol(h):scol(h) + 128],
                    qk_slice(khb, h, b), qk_slice(qhb, h, b),
                    start=True, stop=True)
            S.op("tensor", smm, wait=waits,
                 inc="tC" if i == H - 1 else None)
            waits = []
        t_grp = S.n("tC")
        def m1red(e, b=b, ps=ps):
            return e.reduce_max(m1all[:, b:b + 1], ps[:, :], axis=AX.X)
        S.op("vector", m1red, wait=[("tC", t_grp)])
        pre_m1[b] = S.n("vC")

    def emit_finish(blo, bhi, psm, coloff):
        nb_ = bhi - blo
        v_pre = S.n("vC")
        S.op("tensor", lambda e, psm=psm: e.transpose(
            psm[0:nb_, 0:128], m1all[:, blo:bhi], idf[:, :]),
            wait=[("vC", v_pre)], inc="tC")
        t1 = S.n("tC")
        S.op("vector", lambda e, psm=psm: e.reduce_max(
            negg8[0:nb_, :], psm[0:nb_, 0:128], axis=AX.X, negate=True),
            wait=[("tC", t1)])
        v8 = S.n("vC")
        S.op("tensor", lambda e, psm=psm: e.transpose(
            psm[0:1, 256:256 + nb_], negg8[0:nb_, :], idf[0:nb_, 0:nb_]),
            wait=[("vC", v8)], inc="tC")
        t2 = S.n("tC")
        S.op("vector", lambda e, psm=psm: e.tensor_copy(
            negrowf[:, 0:nb_], psm[0:1, 256:256 + nb_]), wait=[("tC", t2)])
        S.op("vector", lambda e: e.tensor_copy(
            negrowb[:, 0:nb_], negrowf[:, 0:nb_]))
        vr = S.n("vC")
        for j in range(nb_):
            def bcast(e, j=j, psm=psm):
                return e.matmul(psm[:, coloff + j:coloff + j + 1], ones1[:, :],
                                negrowb[0:1, j:j + 1], start=True, stop=True)
            S.op("tensor", bcast, wait=[("vC", vr)],
                 inc="tC" if j == nb_ - 1 else None)
        tb = S.n("tC")
        S.op("vector", lambda e, psm=psm: e.tensor_copy(
            gneg[:, blo:bhi], psm[:, coloff:coloff + nb_]), wait=[("tC", tb)])
        return ("vC", S.n("vC"))

    def attn_smm(b, ps, extra=()):
        waits = list(extra)
        horder = list(range(0, H, 2)) + list(range(1, H, 2))
        for i, h in enumerate(horder):
            def smm(e, b=b, h=h, ps=ps):
                return e.matmul(
                    ps[:, scol(h):scol(h) + 128],
                    qk_slice(khb, h, b), qk_slice(qhb, h, b),
                    start=True, stop=True)
            S.op("tensor", smm, wait=waits,
                 inc="tC" if i == H - 1 else None)
            waits = []
        blk_state[b] = dict(smm=S.n("tC"))

    def attn_chain_pre(b, ps):
        pb = pb0 if b % 2 == 0 else pb1
        lv = linv0 if b % 2 == 0 else linv1
        st = blk_state[b]
        S.op("scalar", lambda e, b=b, ps=ps, pb=pb: e.activation(
            pb[:, :], ps[:, :], AF.Exp, bias=gneg[:, b:b + 1], scale=1.0),
            wait=[("tC", st["smm"])])
        st["exp"] = S.n("cC")
        S.op("vector", lambda e, pb=pb: e.tensor_mul(
            pb[:, :], pb[:, :], maskt[:, :]),
            wait=[("cC", st["exp"])])
        st["mask"] = ("vC", S.n("vC"))
        st["pb"], st["lv"] = pb, lv

    def attn_l(b, ps):
        st = blk_state[b]
        pb, lv = st["pb"], st["lv"]
        for j in range(4):
            def lmm(e, j=j, ps=ps, pb=pb):
                return e.matmul(ps[:, j * 512:(j + 1) * 512], ones128[:, :],
                                pb[:, j * 512:(j + 1) * 512],
                                start=True, stop=True)
            S.op("tensor", lmm, wait=[st["mask"]],
                 inc="tC" if j == 3 else None)
        st["lmm"] = S.n("tC")
        S.op("scalar", lambda e, ps=ps, lv=lv: e.activation(
            lv[:, :], ps[:, :], AF.Ln), wait=[("tC", st["lmm"])])
        S.op("scalar", lambda e, lv=lv: e.activation(
            lv[:, :], lv[:, :], AF.Exp, bias=0.0, scale=-1.0))
        st["recip"] = ("cC", S.n("cC"))
        S.op("vector", lambda e, pb=pb, lv=lv: e.tensor_mul(
            pb[:, :], pb[:, :], lv[:, :]), wait=[st["recip"]])
        st["pnorm"] = ("vC", S.n("vC"))

    def attn_o(b, ps):
        st = blk_state[b]
        pb = st["pb"]
        for h in range(H):
            def omm(e, b=b, h=h, ps=ps, pb=pb):
                out = ps[(h % 2) * 64:(h % 2) * 64 + 64,
                         (h // 8) * 512 + ((h // 2) % 4) * 128:
                         (h // 8) * 512 + ((h // 2) % 4) * 128 + 128]
                return e.matmul(
                    out,
                    vtb[:, b * 1024 + h * 64:b * 1024 + h * 64 + 64],
                    pb[:, scol(h):scol(h) + 128],
                    start=True, stop=True)
            S.op("tensor", omm, wait=[st["pnorm"]],
                 inc="tC" if h == H - 1 else None)
        st["omm"] = S.n("tC")
        for g in range(2):
            for bot in range(2):
                def oev(e, b=b, g=g, bot=bot, ps=ps):
                    p0 = bot * 64
                    src = ps[p0:p0 + 64, g * 512:(g + 1) * 512].rearrange(
                        "p (c f) -> p c f", c=4)
                    dst = otb[p0:p0 + 64, g * 4096:(g + 1) * 4096].rearrange(
                        "p (c f) -> p c f", c=4)[:, :, b * 128:(b + 1) * 128]
                    return e.activation(dst, src, AF.Copy)
                S.op("scalar", oev, wait=[("tC", st["omm"])])
        blk_oevac[b] = [("cC", S.n("cC")), ("vC", st["pnorm"][1])]

    for b in range(NB):
        ps = pset(b % 2)
        extra = [("vC", pre_m1[b - 2])] if b >= 2 else []
        emit_pre(b, ps, extra)
    v_gnegA = emit_finish(0, 8, psA, 256 + 8)
    v_gnegB = v_gnegA
    attn_smm(0, psA, [v_gnegA, ("vC", pre_m1[6])])
    attn_chain_pre(0, psA)
    attn_smm(1, psB, [v_gnegA, ("vC", pre_m1[7])])
    attn_chain_pre(1, psB)
    attn_l(0, psA)
    attn_o(0, psA)
    attn_smm(2, psA, blk_oevac[0])
    attn_chain_pre(2, psA)
    attn_l(1, psB)
    attn_o(1, psB)
    attn_smm(3, psB, blk_oevac[1])
    attn_chain_pre(3, psB)
    attn_l(2, psA)
    attn_o(2, psA)
    attn_l(3, psB)
    attn_o(3, psB)

    def _smm47(b):
        def go():
            extra = [v_gnegB]
            extra += blk_oevac[b - 1]
            attn_smm(b, psB, extra)
            attn_chain_pre(b, psB)
        return go
    def _l47(b):
        def go():
            attn_l(b, psB)
        return go
    def _o47(b):
        def go():
            attn_o(b, psB)
        return go
    attn_rest = [_smm47(4), None, _l47(4), None, _o47(4), None,
                 _smm47(5), None, _l47(5), None, _o47(5), None,
                 _smm47(6), _l47(6), _o47(6),
                 _smm47(7), _l47(7), _o47(7)]

    def flat_oevac(rng):
        w = []
        for b in rng:
            w.extend(blk_oevac[b])
        return w

    stream_free = {0: None, 1: None}
    ln_state = {}

    def emit_p4(half, slots=None):
        nonlocal tile_i
        hofs = half * TH
        entry_cc = ("cC", S.n("cC"))
        slot_i = [0]

        def hit_slot():
            if slots and slot_i[0] < len(slots):
                cb = slots[slot_i[0]]
                if cb is not None:
                    cb()
            slot_i[0] += 1

        wo_ld = {}
        def emit_wo_load(g):
            par = g % 2
            buf = wsA if par == 0 else wsB
            sem = "s0" if par == 0 else "s1"
            waits = [("tC", p2_t_end)]
            if stream_free[par] is not None:
                waits.append(stream_free[par])
            def wodma(e, g=g, buf=buf):
                src = wo[:, g * 512:(g + 1) * 512].rearrange(
                    "(c p) f -> p c f", p=128)
                dst = buf[:, :].rearrange("p (c f) -> p c f", c=8)
                return e.dma_start(out=dst, in_=src)
            S.op("gpsimd", wodma, wait=waits, inc=sem, k=16)
            wo_ld[g] = (sem, S.n(sem))

        emit_wo_load(0)
        emit_wo_load(1)
        a1_gate = flat_oevac(range(4) if half == 0 else range(4, 8))
        if half == 0:
            a1_gate = a1_gate + [v_gnegB]
        for wc in range(WC):
            hit_slot()
            g = wc // 4
            par = g % 2
            buf = wsA if par == 0 else wsB
            bank = tile_i % 4
            waits = [wo_ld[g]] + a1_gate
            if bank in bank_free:
                waits.append(bank_free[bank])
            for hec in range(WC):
                def womm(e, wc=wc, hec=hec, bank=bank, buf=buf, hofs=hofs,
                         first=(hec == 0), last=(hec == WC - 1)):
                    return e.matmul(
                        pbank(bank),
                        buf[:, hec * 512 + (wc % 4) * 128:
                            hec * 512 + (wc % 4) * 128 + 128],
                        otb[:, hec * 1024 + hofs:hec * 1024 + hofs + 512],
                        start=first, stop=last)
                S.op("tensor", womm, wait=waits,
                     inc="tC" if hec == WC - 1 else None)
                waits = []
            t_grp = S.n("tC")
            if wc % 4 == 3:
                stream_free[par] = ("tC", t_grp)
            def woevac(e, wc=wc, bank=bank):
                return e.activation(
                    a1b[:, wc * TH:(wc + 1) * TH], pbank(bank), AF.Identity,
                    bias=bos[:, wc:wc + 1], scale=1.0)
            S.op("scalar", woevac, wait=[("tC", t_grp)])
            bank_free[bank] = ("cC", S.n("cC"))
            tile_i += 1
        a1_done = ("cC", S.n("cC"))

        pw_ld = {}
        xj_ld = {}
        xj_free = {0: None, 1: None}
        b1, b2 = (2, 3) if half == 0 else (4, 5)
        st = {}
        stat_w1 = [bank_free[b1]] if b1 in bank_free else []
        stat_w2 = [bank_free[b2]] if b2 in bank_free else []
        mm2_done = []
        pend_stats = []

        def emit_pw_load(g):
            par = g % 2
            buf = wsA if par == 0 else wsB
            sem = "s0" if par == 0 else "s1"
            waits = []
            if stream_free[par] is not None:
                waits.append(stream_free[par])
            def pwdma(e, g=g, buf=buf):
                src = projw[:, g * 512:(g + 1) * 512].rearrange(
                    "(c p) f -> p c f", p=128)
                dst = buf[:, :].rearrange("p (c f) -> p c f", c=8)
                return e.dma_start(out=dst, in_=src)
            S.op("gpsimd", pwdma, wait=waits, inc=sem, k=16)
            pw_ld[g] = (sem, S.n(sem))

        def emit_xj_load(dmc):
            par = dmc % 2
            xcb = xjb0 if par == 0 else xjb1
            sem = "x0" if par == 0 else "x1"
            xw = []
            if xj_free[par] is not None:
                xw.append(xj_free[par])
            else:
                xw.append(("tC", conv_all_mm))
            def xdma(e, dmc=dmc, xcb=xcb, hofs=hofs):
                return e.dma_start(
                    out=xcb[:, :],
                    in_=xT[dmc * 128:(dmc + 1) * 128, 1 + hofs:1 + hofs + TH])
            S.op("gpsimd", xdma, wait=xw, inc=sem, k=16)
            xj_ld[dmc] = (sem, S.n(sem))

        emit_pw_load(0)
        emit_pw_load(1)
        emit_xj_load(0)
        emit_xj_load(1)
        res1_evac = []
        _sw = {4: [bank_free[4]] if 4 in bank_free else [],
               5: [bank_free[5]] if 5 in bank_free else []}

        def emit_ln1_stat(dmc):
            def sq(e, dmc=dmc):
                return e.activation(
                    sqtmp[:, (dmc % 2) * 512:(dmc % 2) * 512 + 512],
                    res1_ap(dmc), AF.Square)
            sqw = [("cC", res1_evac[dmc])]
            if dmc >= 2:
                sqw.append(("tC", mm2_done[dmc - 2]))
            S.op("scalar", sq, wait=sqw)
            c_sq = S.n("cC")
            def mm1(e, dmc=dmc):
                return e.matmul(pbank(4), ones128[:, :], res1_ap(dmc),
                                start=(dmc == 0), stop=(dmc == DC - 1),
                                skip_group_check=True)
            S.op("tensor", mm1, wait=_sw[4] + [("cC", res1_evac[dmc])],
                 inc="tC" if dmc == DC - 1 else None)
            _sw[4] = []
            def mm2(e, dmc=dmc):
                return e.matmul(pbank(5), ones128[:, :],
                                sqtmp[:, (dmc % 2) * 512:
                                      (dmc % 2) * 512 + 512],
                                start=(dmc == 0), stop=(dmc == DC - 1),
                                skip_group_check=True)
            S.op("tensor", mm2, wait=_sw[5] + [("cC", c_sq)], inc="tC")
            _sw[5] = []
            mm2_done.append(S.n("tC"))
        for dmc in range(DC):
            hit_slot()
            g = dmc // 4
            par = g % 2
            buf = wsA if par == 0 else wsB
            xcb = xjb0 if dmc % 2 == 0 else xjb1
            bank = tile_i % 4
            waits = [pw_ld[g], a1_done]
            if bank in bank_free:
                waits.append(bank_free[bank])
            for wc in range(WC):
                def pmm(e, dmc=dmc, wc=wc, bank=bank, buf=buf,
                        first=(wc == 0)):
                    return e.matmul(
                        pbank(bank),
                        buf[:, wc * 512 + (dmc % 4) * 128:
                            wc * 512 + (dmc % 4) * 128 + 128],
                        a1b[:, wc * TH:(wc + 1) * TH],
                        start=first, stop=False)
                S.op("tensor", pmm, wait=waits)
                waits = []
            def pxadd(e, dmc=dmc, bank=bank, xcb=xcb):
                return e.matmul(pbank(bank), idb[:, :], xcb[:, :],
                                start=False, stop=True)
            S.op("tensor", pxadd, wait=[xj_ld[dmc], sd_idb], inc="tC")
            t_grp = S.n("tC")
            xj_free[dmc % 2] = ("tC", t_grp)
            if dmc % 4 == 3:
                stream_free[par] = ("tC", t_grp)
            pw_ = [("tC", t_grp)]
            if half == 1 and dmc < 2:
                pw_.extend(ln_state[0]["applied"])
            def pevac(e, dmc=dmc, bank=bank):
                return e.activation(
                    res1_ap(dmc), pbank(bank), AF.Identity,
                    bias=projb[:, dmc:dmc + 1], scale=1.0)
            S.op("scalar", pevac, wait=pw_)
            bank_free[bank] = ("cC", S.n("cC"))
            c_evac = S.n("cC")
            res1_evac.append(c_evac)
            tile_i += 1
            if dmc % 4 == 3 and dmc // 4 + 2 < 8:
                emit_pw_load(dmc // 4 + 2)
            if dmc + 2 < DC:
                emit_xj_load(dmc + 2)
            if half == 1:
                pend_stats.append(dmc)
                if len(pend_stats) > 4:
                    emit_ln1_stat(pend_stats.pop(0))
        if half == 1:
            while pend_stats:
                emit_ln1_stat(pend_stats.pop(0))
            st["b1"], st["b2"], st["t_stats"] = 4, 5, S.n("tC")
            st["res1_evac"] = res1_evac
            ln_state[half] = st
            return
        b1, b2 = (0, 1) if half == 0 else (4, 5)
        stat_w1 = [bank_free[b1]] if b1 in bank_free else []
        stat_w2 = [bank_free[b2]] if b2 in bank_free else []
        for dmc in range(DC):
            def sq(e, dmc=dmc):
                return e.activation(
                    sqtmp[:, (dmc % 2) * 512:(dmc % 2) * 512 + 512],
                    res1_ap(dmc), AF.Square)
            sqw = [("cC", res1_evac[dmc])]
            if dmc >= 2:
                sqw.append(("tC", mm2_done[dmc - 2]))
            S.op("scalar", sq, wait=sqw)
            c_sq = S.n("cC")
            def mm1(e, dmc=dmc, b1=b1):
                return e.matmul(pbank(b1), ones128[:, :], res1_ap(dmc),
                                start=(dmc == 0), stop=(dmc == DC - 1),
                                skip_group_check=True)
            S.op("tensor", mm1,
                 wait=stat_w1 + [("cC", res1_evac[dmc])],
                 inc="tC" if dmc == DC - 1 else None)
            stat_w1 = []
            def mm2(e, dmc=dmc, b2=b2):
                return e.matmul(pbank(b2), ones128[:, :],
                                sqtmp[:, (dmc % 2) * 512:
                                      (dmc % 2) * 512 + 512],
                                start=(dmc == 0), stop=(dmc == DC - 1),
                                skip_group_check=True)
            S.op("tensor", mm2, wait=stat_w2 + [("cC", c_sq)], inc="tC")
            stat_w2 = []
            mm2_done.append(S.n("tC"))
        st["b1"], st["b2"], st["t_stats"] = b1, b2, S.n("tC")
        st["res1_evac"] = res1_evac
        ln_state[half] = st
    def emit_ln_chain(st, nm_t, rstd_t, make_bf=False):
        b1, b2, t_stats = st["b1"], st["b2"], st["t_stats"]
        cw_ = [("tC", t_stats)]
        if "extra_chain_wait" in st:
            cw_.append(st["extra_chain_wait"])
        nm_out = nmb if make_bf else nm_t
        rs_out = rstdb if make_bf else rstd_t
        S.op("scalar", lambda e, nm_out=nm_out: e.activation(
            nm_out[:, :], pbank(b1), AF.Copy, bias=0.0, scale=-1.0 / D),
            wait=cw_)
        S.op("scalar", lambda e: e.activation(
            tmp1[:, :], pbank(b1), AF.Square, bias=0.0, scale=1.0 / D))
        S.op("scalar", lambda e: e.activation(
            tmp2[:, :], pbank(b2), AF.Copy, bias=1e-5, scale=1.0 / D))
        c_st = S.n("cC")
        bank_free[b1] = ("cC", c_st)
        bank_free[b2] = ("cC", c_st)
        S.op("vector", lambda e: e.tensor_sub(tmp2[:, :], tmp2[:, :],
                                              tmp1[:, :]),
             wait=[("cC", c_st)])
        v_r = S.n("vC")
        S.op("scalar", lambda e: e.activation(
            tmp1[:, :], tmp2[:, :], AF.Ln), wait=[("vC", v_r)])
        S.op("scalar", lambda e, rs_out=rs_out: e.activation(
            rs_out[:, :], tmp1[:, :], AF.Exp, bias=0.0, scale=-0.5))
        return ("cC", S.n("cC"))

    def p45_bank(i):
        return i % 4

    def _x1_grp(half, g):
        if half == 0:
            return x1bA[:, g * 2048:(g + 1) * 2048]
        if g < 4:
            return x1bB_lo[:, g * 2048:(g + 1) * 2048]
        return x1bB_hi[:, (g - 4) * 2048:(g - 3) * 2048]

    def emit_ln1_apply(half):
        st = ln_state[half]
        gate = emit_ln_chain(st, negmean, rstd, make_bf=True)
        for g in range(8):
            r = res1_grp(g).rearrange("p (c f) -> p c f", c=4)
            xo = _x1_grp(half, g).rearrange("p (c f) -> p c f", c=4)
            eng = "gpsimd" if (half == 1 and g % 2 == 1) else "vector"
            S.op(eng, lambda e, r=r: e.tensor_add(
                r, r, bc3(nmb[:, :], 4)), wait=[gate])
            S.op(eng, lambda e, r=r, xo=xo: e.tensor_mul(
                xo, r, bc3(rstdb[:, :], 4)))
        st["applied"] = [("vC", S.n("vC")), ("gC", S.n("gC"))]

    def emit_ht(half):
        nonlocal tile_i
        for erc in range(ERC):
            bank = p45_bank(tile_i)
            waits = [g_at] + ln_state[half]["applied"]
            if bank in bank_free:
                waits.append(bank_free[bank])
            for dc in range(DC):
                def hmm(e, erc=erc, dc=dc, bank=bank, half=half,
                        first=(dc == 0), last=(dc == DC - 1)):
                    return e.matmul(
                        pbank(bank),
                        atb[:, dc * 256 + erc * 128:dc * 256 + erc * 128 + 128],
                        x1b_ap(half, dc),
                        start=first, stop=last)
                S.op("tensor", hmm, wait=waits,
                     inc="tC" if dc == DC - 1 else None)
                waits = []
            t_grp = S.n("tC")
            def hevac(e, erc=erc, bank=bank, half=half):
                return e.tensor_scalar_mul(
                    hb[:, erc * 1024 + half * TH:erc * 1024 + half * TH + TH],
                    pbank(bank), probs[:, erc:erc + 1])
            S.op("vector", hevac, wait=[("tC", t_grp)])
            bank_free[bank] = ("vC", S.n("vC"))
            tile_i += 1

    wt_ld = {}
    def emit_wt_load(pg):
        par = pg % 2
        buf = wsF0 if par == 0 else wsF1
        sem = "s0" if par == 0 else "s1"
        waits = [] if pg >= 2 else list(wt_first_gate)
        if stream_free[par] is not None:
            waits.append(stream_free[par])
        def wtdma(e, pg=pg, buf=buf):
            src = Wt[:, pg * 256:(pg + 1) * 256].rearrange(
                "(c p) f -> p c f", p=128)
            dst = buf[:, :].rearrange("p (c f) -> p c f", c=32)
            return e.dma_start(out=dst, in_=src)
        S.op("gpsimd", wtdma, wait=waits, inc=sem, k=16)
        wt_ld[pg] = (sem, S.n(sem))

    emit_p4(0, slots=attn_rest)
    p3_done = [("cC", S.n("cC")), ("vC", S.n("vC")), ("gC", S.n("gC")),
               ("tC", S.n("tC"))]
    emit_ln1_apply(0)
    emit_p4(1)
    p4_t_end = ("tC", S.n("tC"))
    def atdma(e):
        src = At[:, :].rearrange("(c p) f -> p c f", p=128)
        dst = atb[:, :].rearrange("p (c f) -> p c f", c=32)
        return e.dma_start(out=dst, in_=src)
    S.op("gpsimd", atdma, wait=list(p3_done) + [g_wv], inc="w2", k=16)
    g_at = ("w2", S.n("w2"))
    emit_ht(0)
    hb_done_h0 = ("vC", S.n("vC"))
    t_p4done = ("tC", S.n("tC"))
    wt_first_gate = [t_p4done]
    emit_wt_load(0)
    emit_wt_load(1)
    emit_ln1_apply(1)


    emit_ht(1)
    t_ht_end = ("tC", S.n("tC"))
    hb_done = ("vC", S.n("vC"))
    for erc in range(ERC):
        def bmdma2(e, erc=erc):
            return e.dma_start(
                out=Bmb[:, erc * D:(erc + 1) * D],
                in_=Bm[erc * 128:(erc + 1) * 128, :])
        S.op("gpsimd", bmdma2, wait=[t_ht_end], inc="w2", k=16)
    g_bm = ("w2", S.n("w2"))
    ln2 = {0: dict(b1=4, b2=5, mm2_done=[]), 1: dict(b1=6, b2=7, mm2_done=[])}
    for hset in ln2.values():
        for bk in (hset["b1"], hset["b2"]):
            hset.setdefault("w", {})[bk] = (
                [bank_free[bk]] if bk in bank_free else [])
    pend_stats2 = []
    res2_writes = {}

    def emit_stats2(oc, half, c_evac):
        hs = ln2[half]
        def sq2(e, oc=oc, half=half):
            return e.activation(
                sqtmp[:, half * 512:half * 512 + 512],
                res2_ap(oc)[:, half * TH:half * TH + TH], AF.Square)
        sqw = [("cC", c_evac)]
        if oc >= 1:
            sqw.append(("tC", hs["mm2_done"][oc - 1]))
        S.op("scalar", sq2, wait=sqw)
        c_sq = S.n("cC")
        def mm1(e, oc=oc, half=half, b1=hs["b1"]):
            return e.matmul(
                pbank(b1), ones128[:, :],
                res2_ap(oc)[:, half * TH:half * TH + TH],
                start=(oc == 0), stop=(oc == DC - 1),
                skip_group_check=True)
        S.op("tensor", mm1, wait=hs["w"][hs["b1"]] + [("cC", c_evac)],
             inc="tC" if oc == DC - 1 else None)
        hs["w"][hs["b1"]] = []
        def mm2(e, oc=oc, half=half, b2=hs["b2"]):
            return e.matmul(
                pbank(b2), ones128[:, :],
                sqtmp[:, half * 512:half * 512 + 512],
                start=(oc == 0), stop=(oc == DC - 1),
                skip_group_check=True)
        S.op("tensor", mm2, wait=hs["w"][hs["b2"]] + [("cC", c_sq)],
             inc="tC")
        hs["w"][hs["b2"]] = []
        hs["mm2_done"].append(S.n("tC"))

    for pg in range(16):
        par = pg % 2
        buf = wsF0 if par == 0 else wsF1
        for half in range(2):
            for sub in range(2):
                oc = pg * 2 + sub
                bank = p45_bank(tile_i)
                waits = [wt_ld[pg], g_bm,
                         hb_done_h0 if half == 0 else hb_done]
                if bank in bank_free:
                    waits.append(bank_free[bank])
                for dc in range(DC):
                    def fmm(e, dc=dc, bank=bank, buf=buf, half=half,
                            sub=sub, first=(dc == 0)):
                        return e.matmul(
                            pbank(bank),
                            buf[:, dc * 256 + sub * 128:
                                dc * 256 + sub * 128 + 128],
                            x1b_ap(half, dc),
                            start=first, stop=False)
                    S.op("tensor", fmm, wait=waits)
                    waits = []
                for erc in range(ERC):
                    def bmm(e, oc=oc, erc=erc, bank=bank, half=half):
                        return e.matmul(
                            pbank(bank),
                            Bmb[:, erc * D + oc * 128:erc * D + oc * 128 + 128],
                            hb[:, erc * 1024 + half * TH:
                               erc * 1024 + half * TH + TH],
                            start=False, stop=False)
                    S.op("tensor", bmm)
                def fxadd(e, oc=oc, bank=bank, half=half):
                    return e.matmul(pbank(bank), idb[:, :],
                                    x1b_ap(half, oc),
                                    start=False, stop=True)
                S.op("tensor", fxadd, wait=[sd_idb], inc="tC")
                t_grp = S.n("tC")
                def fevac(e, oc=oc, bank=bank, half=half):
                    return e.activation(
                        res2_ap(oc)[:, half * TH:half * TH + TH],
                        pbank(bank), AF.Identity,
                        bias=ffnb[:, oc:oc + 1], scale=1.0)
                S.op("scalar", fevac, wait=[("tC", t_grp)])
                bank_free[bank] = ("cC", S.n("cC"))
                c_evac = S.n("cC")
                res2_writes[(oc, half)] = c_evac
                tile_i += 1
                pend_stats2.append((oc, half, c_evac))
                if len(pend_stats2) > 4:
                    emit_stats2(*pend_stats2.pop(0))
        stream_free[par] = ("tC", S.n("tC"))
        if pg + 2 < 16:
            emit_wt_load(pg + 2)
    while pend_stats2:
        emit_stats2(*pend_stats2.pop(0))

    ob_free = [None, None, None, None]
    apply_specs = []
    for half in range(2):
        hs = ln2[half]
        st = dict(b1=hs["b1"], b2=hs["b2"], t_stats=S.n("tC"))
        nm_t = negmean if half == 0 else nm2
        rs_t = rstd if half == 0 else rstd2
        gate = emit_ln_chain(st, nm_t, rs_t, make_bf=False)
        apply_specs.append((half, nm_t, rs_t, gate))
    for dc in range(DC):
        for half, nm_t, rs_t, gate in apply_specs:
            slot = (dc % 2) * 2 + half
            osem = f"o{slot}"
            hofs = half * TH
            eng = "vector" if half == 0 else "gpsimd"
            ww = [gate]
            if ob_free[slot] is not None:
                ww.append(ob_free[slot])
            def app1(e, dc=dc, slot=slot, hofs=hofs, nm_t=nm_t):
                return e.tensor_add(ob[slot][:, :],
                                    res2_ap(dc)[:, hofs:hofs + TH],
                                    nm_t[:, :])
            def app2(e, slot=slot, rs_t=rs_t):
                return e.tensor_mul(ob[slot][:, :], ob[slot][:, :],
                                    rs_t[:, :])
            if eng == "vector":
                S.op("vector", app1, wait=ww)
                S.op("vector", app2)
                v_ap = ("vC", S.n("vC"))
            else:
                S.op("gpsimd", app1, wait=ww, inc="gC")
                S.op("gpsimd", app2, inc="gC")
                v_ap = ("gC", S.n("gC"))
            def sdma(e, dc=dc, slot=slot, hofs=hofs):
                return e.dma_start(
                    out=outT[dc * 128:(dc + 1) * 128, hofs:hofs + TH],
                    in_=ob[slot][:, :])
            S.op("sync", sdma, wait=[v_ap], inc=osem, k=16)
            ob_free[slot] = (osem, S.n(osem))

    if tap_outs:
        wait_all = [("tC", S.n("tC")), ("vC", S.n("vC")), ("cC", S.n("cC"))]
        tap_srcs = dict(
            qkvb=qkvb, qhb=qhb, khb=khb, vtb=vtb, otb=otb, gneg=gneg,
            a1b=a1b, hbt=hb, m1all=m1all, pb0=pb0, pb1=pb1, linv0=linv0,
            rstd=rstd, x1bA=x1bA, res1=res1,
        )
        for tname, ap_d in tap_outs.items():
            src = tap_srcs[tname]
            S.op("sync", lambda e, d=ap_d, s=src: e.dma_start(
                out=d, in_=s[:, :]), wait=wait_all, inc="o0", k=16)

    fin = [(f"o{i}", S.n(f"o{i}")) for i in range(4)]
    S.op("sync", lambda e: e.nop(), wait=fin)

    with ExitStack() as stx:
        sems = {name: stx.enter_context(nc.semaphore(name)) for name in SEMS}
        blk = stx.enter_context(nc.Block())
        for eng in Sched.ENGS:
            ops = S.q[eng]
            if not ops:
                continue
            getattr(blk, eng)(lambda e, ops=ops: [f(e, sems) for f in ops])
    return nc


def stage_inputs(inputs):
    f32 = np.float32
    x = np.asarray(inputs["L0_attn_inputs"], f32)
    N, L, _ = x.shape
    per_core = []
    for c in range(8):
        n, s = divmod(c, 2)
        t0 = s * 1024
        xTh = np.zeros((D, T + 2), f32)
        lo = max(t0 - 1, 0)
        hi = min(t0 + T + 1, L)
        xTh[:, lo - (t0 - 1):hi - (t0 - 1)] = x[n, lo:hi, :].T
        per_core.append(np.ascontiguousarray(xTh))

    shared = {
        "cw": np.ascontiguousarray(np.asarray(inputs["L0_attn_conv_w"], f32)),
        "wq": np.ascontiguousarray(np.asarray(inputs["L0_attn_wq"], f32).reshape(W1, W1)),
        "wk": np.ascontiguousarray(np.asarray(inputs["L0_attn_wk"], f32).reshape(W1, W1)),
        "wv": np.ascontiguousarray(np.asarray(inputs["L0_attn_wv"], f32).reshape(W1, W1)),
        "wo": np.ascontiguousarray(np.asarray(inputs["L0_attn_wo"], f32).reshape(W1, W1)),
        "projw": np.ascontiguousarray(np.asarray(inputs["L0_attn_proj_w"], f32)),
        "Wt": np.ascontiguousarray(np.asarray(inputs["L0_ffn_W"], f32).T),
        "At": np.ascontiguousarray(np.asarray(inputs["L0_ffn_A"], f32).reshape(256, D).T),
        "Bm": np.ascontiguousarray(np.asarray(inputs["L0_ffn_B"], f32).transpose(0, 2, 1).reshape(256, D)),
        "convb": np.ascontiguousarray(np.asarray(inputs["L0_attn_conv_b"], f32).reshape(QC, 128).T),
        "bq": np.ascontiguousarray(np.asarray(inputs["L0_attn_bq"], f32).reshape(WC, 128).T),
        "bk": np.ascontiguousarray(np.asarray(inputs["L0_attn_bk"], f32).reshape(WC, 128).T),
        "bo": np.ascontiguousarray(np.asarray(inputs["L0_attn_bo"], f32).reshape(WC, 128).T),
        "projb": np.ascontiguousarray(np.asarray(inputs["L0_attn_proj_b"], f32).reshape(DC, 128).T),
        "ffnb": np.ascontiguousarray(np.asarray(inputs["L0_ffn_b"], f32).reshape(DC, 128).T),
        "probs": np.ascontiguousarray(
            np.repeat(np.asarray(inputs["L0_ffn_probs"], f32), 16).reshape(ERC, 128).T),
        "bv": np.ascontiguousarray(np.asarray(inputs["L0_attn_bv"], f32).reshape(1, W1)),
        "ones128": np.ones((128, 128), BF16N),
        "ones1": np.ones((1, 128), BF16N),
        "mask": np.ascontiguousarray(
            np.tile(np.triu(np.ones((128, 128), np.float32)), (1, 16)).astype(BF16N)),
        "idf": np.eye(128, dtype=f32),
        "idb": np.eye(128, dtype=BF16N),
    }
    in_maps = [{"xT": per_core[c], **shared} for c in range(8)]
    return in_maps


def kernel(**inputs):
    in_maps = stage_inputs(inputs)
    nc = build_nc()
    res = run_bass_kernel_spmd(nc, in_maps, core_ids=list(range(8)))
    out = np.empty((4, 2048, D), np.float32)
    for c in range(8):
        n, s = divmod(c, 2)
        out[n, s * 1024:(s + 1) * 1024, :] = res.results[c]["outT"].T
    return out


# version 15
# speedup vs baseline: 1.0407x; 1.0120x over previous
import numpy as np
import ml_dtypes
from contextlib import ExitStack
import concourse.bass as bass
import concourse.mybir as mybir
from concourse.bass_utils import run_bass_kernel_spmd

BF16N = ml_dtypes.bfloat16
F32 = mybir.dt.float32
BF16 = mybir.dt.bfloat16
AF = mybir.ActivationFunctionType
ALU = mybir.AluOpType
AX = mybir.AxisListType

D = 4096
DC = 32
W1 = 1024
WC = 8
QC = 24
T = 1024
TH = 512
H = 16
NB = 8
ERC = 2

OFF_ONES128 = 0
OFF_MASK = 256
OFF_IDF32 = 4352
OFF_CONVB = 4864
OFF_BQS = 4992
OFF_BK = 5024
OFF_BO = 5056
OFF_PROJB = 5088
OFF_FFNB = 5216
OFF_PROBS = 5344
OFF_ONES1 = 5376
OFF_BVF = 5632
OFF_BVB = 9728
OFF_M1ALL = 11776
OFF_NEGG8 = 11808
OFF_NEGROWF = 11840
OFF_NEGROWB = 11872
OFF_GNEG = 11904
OFF_SQTMP = 11936
OFF_STATS = 14016
OFF_HBOX = 22208
OFF_LINV = 26304
REG0 = 42688
REG1 = 108352
REG2 = 157504
OFF_SPARE = 206656
OFF_WK7 = 210752


import dataclasses


def bc3(ap2d, reps):
    return dataclasses.replace(ap2d, ap=[ap2d.ap[0], [0, reps], ap2d.ap[1]])


class Sched:
    ENGS = ("sync", "gpsimd", "tensor", "scalar", "vector")

    def __init__(self):
        self.q = {e: [] for e in self.ENGS}
        self.cnt = {}
        self.hw = {e: {} for e in self.ENGS}

    def n(self, sem):
        return self.cnt.get(sem, 0)

    def op(self, eng, fn, wait=(), inc=None, k=1):
        if eng == "vector":
            wait = list(wait) + [("vC", self.n("vC"))]
            if inc is None:
                inc = "vC"
        elif eng == "gpsimd":
            wait = list(wait) + [("gC", self.n("gC"))]
            if inc is None and k == 1:
                inc = "gC"
        elif eng == "scalar":
            wait = list(wait) + [("cC", self.n("cC"))]
            if inc is None:
                inc = "cC"
        waits = []
        for s, v in wait:
            if v is None or v <= 0:
                continue
            if self.hw[eng].get(s, 0) >= v:
                continue
            self.hw[eng][s] = v
            waits.append((s, v))
        if inc is not None:
            self.cnt[inc] = self.cnt.get(inc, 0) + k

        def run(e, sems, fn=fn, waits=waits, inc=inc, k=k):
            for s, v in waits:
                e.wait_ge(sems[s], v)
            inst = fn(e)
            if inc is not None:
                inst.then_inc(sems[inc], k)

        self.q[eng].append(run)
        return self.cnt.get(inc, 0) if inc is not None else None


def build_nc(taps=(), stop_after='full'):
    nc = bass.Bass("TRN2", target_bir_lowering=False)
    dt = lambda name, shape, dtype=F32: nc.dram_tensor(
        name, list(shape), dtype, kind="ExternalInput").ap()

    xT = dt("xT", (D, T + 2))
    cw = dt("cw", (3, D, 3 * W1))
    wq = dt("wq", (W1, W1))
    wk = dt("wk", (W1, W1))
    wv = dt("wv", (W1, W1))
    wo = dt("wo", (W1, W1))
    projw = dt("projw", (W1, D))
    Wt = dt("Wt", (D, D))
    At = dt("At", (D, 2 * 128))
    Bm = dt("Bm", (2 * 128, D))
    convb_d = dt("convb", (128, QC))
    bq_d = dt("bq", (128, WC))
    bk_d = dt("bk", (128, WC))
    bo_d = dt("bo", (128, WC))
    projb_d = dt("projb", (128, DC))
    ffnb_d = dt("ffnb", (128, DC))
    probs_d = dt("probs", (128, ERC))
    bv_d = dt("bv", (1, W1))
    ones128_d = dt("ones128", (128, 128), BF16)
    ones1_d = dt("ones1", (1, 128), BF16)
    mask_d = dt("mask", (128, 2048), BF16)
    idf_d = dt("idf", (128, 128), F32)
    idb_d = dt("idb", (128, 128), BF16)

    outT = nc.dram_tensor("outT", [D, T], F32, kind="ExternalOutput").ap()
    tap_outs = {}
    for tname, shape, tdt in taps:
        tap_outs[tname] = nc.dram_tensor(
            "tap_" + tname, list(shape), tdt, kind="ExternalOutput").ap()

    base = (nc._sbuf_addr_for_side("left") + 63) // 64 * 64

    def sb(name, off, shape, dtype):
        return nc.alloc_sbuf_tensor_at(name, list(shape), dtype,
                                       offset=base + off)

    ones128 = sb("ones128_s", OFF_ONES128, (128, 128), BF16)
    maskt = sb("maskt", OFF_MASK, (128, 2048), BF16)
    idf = sb("idf_s", OFF_IDF32, (128, 128), F32)
    convb = sb("convb_s", OFF_CONVB, (128, QC), F32)
    bqs = sb("bqs", OFF_BQS, (128, WC), F32)
    bks = sb("bks", OFF_BK, (128, WC), F32)
    bos = sb("bos", OFF_BO, (128, WC), F32)
    projb = sb("projb_s", OFF_PROJB, (128, DC), F32)
    ffnb = sb("ffnb_s", OFF_FFNB, (128, DC), F32)
    probs = sb("probs_s", OFF_PROBS, (128, ERC), F32)
    ones1 = sb("ones1_s", OFF_ONES1, (1, 128), BF16)
    bvf = sb("bvf", OFF_BVF, (1, W1), F32)
    bvb = sb("bvb", OFF_BVB, (1, W1), BF16)
    m1all = sb("m1all", OFF_M1ALL, (128, NB), F32)
    negg8 = sb("negg8", OFF_NEGG8, (NB, 1), F32)
    negrowf = sb("negrowf", OFF_NEGROWF, (1, NB), F32)
    negrowb = sb("negrowb", OFF_NEGROWB, (1, NB), BF16)
    gneg = sb("gneg", OFF_GNEG, (128, NB), F32)
    sqtmp = sb("sqtmp", OFF_SQTMP, (128, 1024), BF16)
    negmean = sb("negmean", OFF_STATS, (128, TH), F32)
    tmp1 = sb("tmp1", OFF_STATS + 2048, (128, TH), F32)
    tmp2 = sb("tmp2", OFF_STATS + 4096, (128, TH), F32)
    rstd = sb("rstd", OFF_STATS + 6144, (128, TH), F32)
    nm2 = sb("nm2", OFF_LINV + 8192, (128, TH), F32)
    rstd2 = sb("rstd2", OFF_LINV + 10240, (128, TH), F32)
    nmb = sb("nmb", OFF_HBOX, (128, TH), BF16)
    rstdb = sb("rstdb", OFF_HBOX + 1024, (128, TH), BF16)
    idb = sb("idb", OFF_BVF, (128, 128), BF16)
    hb = sb("hb", OFF_SPARE, (128, 2048), BF16)
    linv0 = sb("linv0", OFF_LINV, (128, 2048), F32)
    linv1 = sb("linv1", OFF_LINV + 8192, (128, 2048), F32)
    atb = sb("atb", OFF_LINV, (128, 32 * 256), BF16)
    ob = [sb(f"ob{i}", OFF_LINV + i * 2048, (128, TH), F32) for i in range(4)]
    wqpre = [sb(f"wqpre{c}", OFF_LINV + c * 2048, (128, 1024), BF16)
             for c in range(8)]
    wk_offs = [OFF_STATS, OFF_STATS + 2048, OFF_STATS + 4096, OFF_STATS + 6144,
               OFF_HBOX, OFF_HBOX + 2048, OFF_SQTMP, OFF_WK7]
    wkpre = [sb(f"wkpre{c}", wk_offs[c], (128, 1024), BF16) for c in range(8)]
    xb = sb("xb", REG0, (128, DC * (T + 2)), BF16)
    wvb = sb("wvb", REG0 + 32768, (128, 8 * 1024), BF16)
    otb = sb("otb", REG0, (128, 8 * 1024), BF16)
    x1bA = sb("x1bA", REG0 + 16384, (128, DC * TH), BF16)
    x1bB_lo = sb("x1bB_lo", REG0, (128, 16 * TH), BF16)
    x1bB_hi = sb("x1bB_hi", REG0 + 49152, (128, 16 * TH), BF16)
    qkvb = sb("qkvb", REG1, (128, QC * 1024), BF16)
    pb0 = sb("pb0", REG1 + 16384, (128, 2048), BF16)
    pb1 = sb("pb1", REG1 + 20480, (128, 2048), BF16)
    wsA = sb("wsA", REG1, (128, 4096), BF16)
    wsB = sb("wsB", REG1 + 8192, (128, 4096), BF16)
    Bmb = sb("Bmb", OFF_LINV, (128, 2 * D), BF16)
    wsF0 = sb("wsF0", REG1, (128, 8192), BF16)
    wsF1 = sb("wsF1", REG1 + 16384, (128, 8192), BF16)
    res2_m = sb("res2_m", REG1 + 32768, (128, 8 * 1024), BF16)
    cwb0 = sb("cwb0", REG2, (128, 3 * 4096), BF16)
    cwb1 = sb("cwb1", REG2 + 24576, (128, 3 * 4096), BF16)
    qhb = sb("qhb", REG2, (128, 8 * 1024), BF16)
    khb = sb("khb", REG2 + 16384, (128, 8 * 1024), BF16)
    vtb = sb("vtb", REG2 + 32768, (128, 8 * 1024), BF16)
    res1lo = sb("res1lo", REG1 + 32768, (128, 8 * TH), BF16)
    res1hi = sb("res1hi", REG2, (128, 24 * TH), BF16)
    a1b = sb("a1b", REG1 + 24576, (128, WC * TH), BF16)
    xjb0 = sb("xjb0", REG0 + 53248, (128, TH), BF16)
    xjb1 = sb("xjb1", REG0 + 54272, (128, TH), BF16)
    res2_lo = sb("res2_lo", REG2, (128, 16 * 1024), BF16)
    res2_hi = sb("res2_hi", REG2 + 32768, (128, 8 * 1024), BF16)

    psA = nc.alloc_psum_tensor("psA", [128, 2048], F32)
    psB = nc.alloc_psum_tensor("psB", [128, 2048], F32)

    def pbank(k):
        t = psA if k < 4 else psB
        return t[:, (k % 4) * 512:(k % 4 + 1) * 512]

    def pset(p):
        return psA if p == 0 else psB

    S = Sched()
    SEMS = ("xS", "c0", "c1", "w2", "s0", "s1", "s2", "s3", "sD", "gC",
            "x0", "x1", "o0", "o1", "o2", "o3", "tC", "vC", "cC")

    def res1_ap(c):
        if c < 8:
            return res1lo[:, c * TH:(c + 1) * TH]
        return res1hi[:, (c - 8) * TH:(c - 7) * TH]

    def res1_grp(g):
        if g < 2:
            return res1lo[:, g * 2048:(g + 1) * 2048]
        return res1hi[:, (g - 2) * 2048:(g - 1) * 2048]

    def x1b_ap(half, c):
        if half == 0:
            return x1bA[:, c * TH:(c + 1) * TH]
        if c < 16:
            return x1bB_lo[:, c * TH:(c + 1) * TH]
        return x1bB_hi[:, (c - 16) * TH:(c - 15) * TH]

    def res2_ap(c):
        if c < 16:
            return res2_lo[:, c * 1024:(c + 1) * 1024]
        if c < 24:
            return res2_m[:, (c - 16) * 1024:(c - 15) * 1024]
        return res2_hi[:, (c - 24) * 1024:(c - 23) * 1024]

    for sbuf_t, dram_ap in (
            (ones128, ones128_d), (maskt, mask_d), (idf, idf_d),
            (convb, convb_d), (bqs, bq_d), (bks, bk_d), (bos, bo_d),
            (projb, projb_d), (ffnb, ffnb_d), (probs, probs_d),
            (ones1, ones1_d), (bvf, bv_d)):
        S.op("sync", lambda e, o=sbuf_t, i=dram_ap: e.dma_start(
            out=o[:, :], in_=i), inc="sD", k=16)
    n_const = S.n("sD")
    S.op("vector", lambda e: e.tensor_scalar_mul(bqs[:, :], bqs[:, :], 0.125),
         wait=[("sD", n_const)])
    S.op("vector", lambda e: e.tensor_copy(bvb[:, :], bvf[:, :]))
    v_prep = S.n("vC")
    S.op("sync", lambda e: e.dma_start(out=idb[:, :], in_=idb_d),
         wait=[("vC", v_prep)], inc="sD", k=16)
    sd_idb = ("sD", S.n("sD"))

    for i in range(4):
        def xldma(e, i=i):
            src = xT[i * 8 * 128:(i + 1) * 8 * 128, :].rearrange(
                "(c p) f -> p c f", p=128)
            dst = xb[:, i * 8 * (T + 2):(i + 1) * 8 * (T + 2)].rearrange(
                "p (c f) -> p c f", c=8)
            return e.dma_start(out=dst, in_=src)
        S.op("gpsimd", xldma, inc="xS", k=16)
    g_xb = ("xS", S.n("xS"))

    conv_w_load_done = {}
    conv_last_mm_using_buf = {}

    def emit_conv_load(oc):
        par = oc % 2
        buf = cwb0 if par == 0 else cwb1
        sem = "c0" if par == 0 else "c1"
        waits = []
        if oc >= 2:
            waits.append(("tC", conv_last_mm_using_buf[oc - 2]))
        for w in range(3):
            for hh in range(2):
                def cwdma(e, oc=oc, w=w, hh=hh, buf=buf):
                    src = cw[w, hh * 2048:(hh + 1) * 2048,
                             oc * 128:(oc + 1) * 128].rearrange(
                        "(c p) f -> p c f", p=128)
                    dst = buf[:, w * 4096 + hh * 2048:
                              w * 4096 + (hh + 1) * 2048].rearrange(
                        "p (c f) -> p c f", c=16)
                    return e.dma_start(out=dst, in_=src)
                S.op("gpsimd", cwdma, wait=waits, inc=sem, k=16)
                waits = []
        conv_w_load_done[oc] = (sem, S.n(sem))

    def emit_pre_load(i):
        if i < 8:
            wt, dst = wq, wqpre[i]
            srcsl = wq[i * 128:(i + 1) * 128, :]
        else:
            wt, dst = wk, wkpre[i - 8]
            srcsl = wk[(i - 8) * 128:(i - 8 + 1) * 128, :]
        def pldma(e, srcsl=srcsl, dst=dst):
            return e.dma_start(out=dst[:, :], in_=srcsl)
        S.op("gpsimd", pldma, inc="w2", k=16)

    emit_conv_load(0)
    emit_conv_load(1)

    bank_free = {}
    tile_i = 0
    first_tensor = [("sD", n_const)]
    first_scalar = [("sD", n_const)]
    for oc in range(QC):
        par = oc % 2
        buf = cwb0 if par == 0 else cwb1
        for tokc in range(2):
            bank = tile_i % 8
            waits = [conv_w_load_done[oc], g_xb] + first_tensor
            first_tensor = []
            if bank in bank_free:
                waits.append(bank_free[bank])
            for dc in range(DC):
                for w in range(3):
                    def cmm(e, oc=oc, tokc=tokc, dc=dc, w=w, buf=buf,
                            bank=bank, first=(dc == 0 and w == 0),
                            last=(dc == DC - 1 and w == 2)):
                        return e.matmul(
                            pbank(bank),
                            buf[:, (w * 32 + dc) * 128:(w * 32 + dc + 1) * 128],
                            xb[:, dc * (T + 2) + tokc * 512 + w:
                               dc * (T + 2) + tokc * 512 + w + 512],
                            start=first, stop=last)
                    S.op("tensor", cmm, wait=waits,
                         inc="tC" if (dc == DC - 1 and w == 2) else None)
                    waits = []
            t_grp = S.n("tC")
            if tile_i % 3 != 2:
                def cevac(e, oc=oc, tokc=tokc, bank=bank):
                    return e.activation(
                        qkvb[:, oc * 1024 + tokc * 512:
                             oc * 1024 + tokc * 512 + 512],
                        pbank(bank), AF.Identity,
                        bias=convb[:, oc:oc + 1], scale=1.0)
                S.op("scalar", cevac, wait=[("tC", t_grp)] + first_scalar)
                first_scalar = []
                bank_free[bank] = ("cC", S.n("cC"))
            else:
                def cevac(e, oc=oc, tokc=tokc, bank=bank):
                    return e.tensor_scalar_add(
                        qkvb[:, oc * 1024 + tokc * 512:
                             oc * 1024 + tokc * 512 + 512],
                        pbank(bank), convb[:, oc:oc + 1])
                S.op("vector", cevac, wait=[("tC", t_grp)])
                bank_free[bank] = ("vC", S.n("vC"))
            tile_i += 1
        conv_last_mm_using_buf[oc] = S.n("tC")
        if oc + 2 < QC:
            emit_conv_load(oc + 2)
        if oc < 16:
            emit_pre_load(oc)
    conv_all_mm = S.n("tC")
    conv_all_evac = [("cC", S.n("cC")), ("vC", S.n("vC"))]
    g_pre = ("w2", S.n("w2"))

    def wvdma(e):
        src = wv[:, :].rearrange("(c p) f -> p c f", p=128)
        dst = wvb[:, :].rearrange("p (c f) -> p c f", c=8)
        return e.dma_start(out=dst, in_=src)
    S.op("gpsimd", wvdma, wait=[("tC", conv_all_mm), g_pre], inc="w2", k=16)
    g_wv = ("w2", S.n("w2"))

    def emit_qk(wchunks, qoff, outb, bias_col, scale):
        nonlocal tile_i
        for hc in range(WC):
            for tokc in range(2):
                bank = tile_i % 8
                waits = [g_pre] + conv_all_evac
                if bank in bank_free:
                    waits.append(bank_free[bank])
                for dc in range(8):
                    def qmm(e, hc=hc, tokc=tokc, dc=dc, bank=bank,
                            qoff=qoff, wcs=wchunks,
                            first=(dc == 0), last=(dc == 7)):
                        return e.matmul(
                            pbank(bank),
                            wcs[dc][:, hc * 128:hc * 128 + 128],
                            qkvb[:, (qoff + dc) * 1024 + tokc * 512:
                                 (qoff + dc) * 1024 + tokc * 512 + 512],
                            start=first, stop=last)
                    S.op("tensor", qmm, wait=waits,
                         inc="tC" if dc == 7 else None)
                    waits = []
                t_grp = S.n("tC")
                def qevac(e, hc=hc, tokc=tokc, bank=bank, outb=outb,
                          bias_col=bias_col, scale=scale):
                    return e.activation(
                        outb[:, hc * 1024 + tokc * 512:
                             hc * 1024 + tokc * 512 + 512],
                        pbank(bank), AF.Identity,
                        bias=bias_col[:, hc:hc + 1], scale=scale)
                S.op("scalar", qevac, wait=[("tC", t_grp), ("vC", v_prep)])
                bank_free[bank] = ("cC", S.n("cC"))
                tile_i += 1

    emit_qk(wqpre, 0, qhb, bqs, 0.125)
    emit_qk(wkpre, 8, khb, bks, 1.0)

    for tokt in range(8):
        for hec in range(2):
            bank = tile_i % 8
            waits = [g_wv] + conv_all_evac
            if bank in bank_free:
                waits.append(bank_free[bank])
            for dc in range(8):
                def vmm(e, tokt=tokt, hec=hec, dc=dc, bank=bank,
                        first=(dc == 0)):
                    return e.matmul(
                        pbank(bank),
                        qkvb[:, (16 + dc) * 1024 + tokt * 128:
                             (16 + dc) * 1024 + tokt * 128 + 128],
                        wvb[:, dc * 1024 + hec * 512:
                            dc * 1024 + hec * 512 + 512],
                        start=first, stop=False)
                S.op("tensor", vmm, wait=waits)
                waits = []
            def vbias(e, hec=hec, bank=bank):
                return e.matmul(pbank(bank), ones1[:, :],
                                bvb[0:1, hec * 512:hec * 512 + 512],
                                start=False, stop=True)
            S.op("tensor", vbias, wait=[("vC", v_prep)], inc="tC")
            t_grp = S.n("tC")
            def vevac(e, tokt=tokt, hec=hec, bank=bank):
                return e.tensor_copy(
                    vtb[:, tokt * 1024 + hec * 512:
                        tokt * 1024 + hec * 512 + 512],
                    pbank(bank))
            S.op("vector", vevac, wait=[("tC", t_grp)])
            bank_free[bank] = ("vC", S.n("vC"))
            tile_i += 1
    p2_done = [("cC", S.n("cC")), ("vC", S.n("vC"))]

    def scol(h):
        return (h % 2) * 1024 + (h // 2) * 128

    def qk_slice(buf, h, b):
        return buf[(h % 2) * 64:(h % 2) * 64 + 64,
                   (h // 2) * 1024 + b * 128:(h // 2) * 1024 + b * 128 + 128]

    p2_t_end = S.n("tC")
    pre_m1 = {}
    blk_oevac = {}
    blk_state = {}

    def emit_pre(b, ps, extra=()):
        waits = list(p2_done) + list(extra)
        horder = list(range(0, H, 2)) + list(range(1, H, 2))
        for i, h in enumerate(horder):
            def smm(e, b=b, h=h, ps=ps):
                return e.matmul(
                    ps[:, scol(h):scol(h) + 128],
                    qk_slice(khb, h, b), qk_slice(qhb, h, b),
                    start=True, stop=True)
            S.op("tensor", smm, wait=waits,
                 inc="tC" if i == H - 1 else None)
            waits = []
        t_grp = S.n("tC")
        def m1red(e, b=b, ps=ps):
            return e.reduce_max(m1all[:, b:b + 1], ps[:, :], axis=AX.X)
        S.op("vector", m1red, wait=[("tC", t_grp)])
        pre_m1[b] = S.n("vC")

    def emit_finish(blo, bhi, psm, coloff):
        nb_ = bhi - blo
        v_pre = S.n("vC")
        S.op("tensor", lambda e, psm=psm: e.transpose(
            psm[0:nb_, 0:128], m1all[:, blo:bhi], idf[:, :]),
            wait=[("vC", v_pre)], inc="tC")
        t1 = S.n("tC")
        S.op("vector", lambda e, psm=psm: e.reduce_max(
            negg8[0:nb_, :], psm[0:nb_, 0:128], axis=AX.X, negate=True),
            wait=[("tC", t1)])
        v8 = S.n("vC")
        S.op("tensor", lambda e, psm=psm: e.transpose(
            psm[0:1, 256:256 + nb_], negg8[0:nb_, :], idf[0:nb_, 0:nb_]),
            wait=[("vC", v8)], inc="tC")
        t2 = S.n("tC")
        S.op("vector", lambda e, psm=psm: e.tensor_copy(
            negrowf[:, 0:nb_], psm[0:1, 256:256 + nb_]), wait=[("tC", t2)])
        S.op("vector", lambda e: e.tensor_copy(
            negrowb[:, 0:nb_], negrowf[:, 0:nb_]))
        vr = S.n("vC")
        for j in range(nb_):
            def bcast(e, j=j, psm=psm):
                return e.matmul(psm[:, coloff + j:coloff + j + 1], ones1[:, :],
                                negrowb[0:1, j:j + 1], start=True, stop=True)
            S.op("tensor", bcast, wait=[("vC", vr)],
                 inc="tC" if j == nb_ - 1 else None)
        tb = S.n("tC")
        S.op("vector", lambda e, psm=psm: e.tensor_copy(
            gneg[:, blo:bhi], psm[:, coloff:coloff + nb_]), wait=[("tC", tb)])
        return ("vC", S.n("vC"))

    def attn_smm(b, ps, extra=()):
        waits = list(extra)
        horder = list(range(0, H, 2)) + list(range(1, H, 2))
        for i, h in enumerate(horder):
            def smm(e, b=b, h=h, ps=ps):
                return e.matmul(
                    ps[:, scol(h):scol(h) + 128],
                    qk_slice(khb, h, b), qk_slice(qhb, h, b),
                    start=True, stop=True)
            S.op("tensor", smm, wait=waits,
                 inc="tC" if i == H - 1 else None)
            waits = []
        blk_state[b] = dict(smm=S.n("tC"))

    def attn_chain_pre(b, ps):
        pb = pb0 if b % 2 == 0 else pb1
        lv = linv0 if b % 2 == 0 else linv1
        st = blk_state[b]
        S.op("scalar", lambda e, b=b, ps=ps, pb=pb: e.activation(
            pb[:, :], ps[:, :], AF.Exp, bias=gneg[:, b:b + 1], scale=1.0),
            wait=[("tC", st["smm"])])
        st["exp"] = S.n("cC")
        S.op("vector", lambda e, pb=pb: e.tensor_mul(
            pb[:, :], pb[:, :], maskt[:, :]),
            wait=[("cC", st["exp"])])
        st["mask"] = ("vC", S.n("vC"))
        st["pb"], st["lv"] = pb, lv

    def attn_l(b, ps):
        st = blk_state[b]
        pb, lv = st["pb"], st["lv"]
        for j in range(4):
            def lmm(e, j=j, ps=ps, pb=pb):
                return e.matmul(ps[:, j * 512:(j + 1) * 512], ones128[:, :],
                                pb[:, j * 512:(j + 1) * 512],
                                start=True, stop=True)
            S.op("tensor", lmm, wait=[st["mask"]],
                 inc="tC" if j == 3 else None)
        st["lmm"] = S.n("tC")
        S.op("scalar", lambda e, ps=ps, lv=lv: e.activation(
            lv[:, :], ps[:, :], AF.Ln), wait=[("tC", st["lmm"])])
        S.op("scalar", lambda e, lv=lv: e.activation(
            lv[:, :], lv[:, :], AF.Exp, bias=0.0, scale=-1.0))
        st["recip"] = ("cC", S.n("cC"))
        S.op("vector", lambda e, pb=pb, lv=lv: e.tensor_mul(
            pb[:, :], pb[:, :], lv[:, :]), wait=[st["recip"]])
        st["pnorm"] = ("vC", S.n("vC"))

    def attn_o(b, ps):
        st = blk_state[b]
        pb = st["pb"]
        for h in range(H):
            def omm(e, b=b, h=h, ps=ps, pb=pb):
                out = ps[(h % 2) * 64:(h % 2) * 64 + 64,
                         (h // 8) * 512 + ((h // 2) % 4) * 128:
                         (h // 8) * 512 + ((h // 2) % 4) * 128 + 128]
                return e.matmul(
                    out,
                    vtb[:, b * 1024 + h * 64:b * 1024 + h * 64 + 64],
                    pb[:, scol(h):scol(h) + 128],
                    start=True, stop=True)
            S.op("tensor", omm, wait=[st["pnorm"]],
                 inc="tC" if h == H - 1 else None)
        st["omm"] = S.n("tC")
        for g in range(2):
            for bot in range(2):
                def oev(e, b=b, g=g, bot=bot, ps=ps):
                    p0 = bot * 64
                    src = ps[p0:p0 + 64, g * 512:(g + 1) * 512].rearrange(
                        "p (c f) -> p c f", c=4)
                    dst = otb[p0:p0 + 64, g * 4096:(g + 1) * 4096].rearrange(
                        "p (c f) -> p c f", c=4)[:, :, b * 128:(b + 1) * 128]
                    return e.activation(dst, src, AF.Copy)
                S.op("scalar", oev, wait=[("tC", st["omm"])])
        blk_oevac[b] = [("cC", S.n("cC")), ("vC", st["pnorm"][1])]

    for b in range(NB):
        ps = pset(b % 2)
        extra = [("vC", pre_m1[b - 2])] if b >= 2 else []
        emit_pre(b, ps, extra)
    v_gnegA = emit_finish(0, 8, psA, 256 + 8)
    v_gnegB = v_gnegA
    attn_smm(0, psA, [v_gnegA, ("vC", pre_m1[6])])
    attn_chain_pre(0, psA)
    attn_smm(1, psB, [v_gnegA, ("vC", pre_m1[7])])
    attn_chain_pre(1, psB)
    attn_l(0, psA)
    attn_o(0, psA)
    attn_smm(2, psA, blk_oevac[0])
    attn_chain_pre(2, psA)
    attn_l(1, psB)
    attn_o(1, psB)
    attn_smm(3, psB, blk_oevac[1])
    attn_chain_pre(3, psB)
    attn_l(2, psA)
    attn_o(2, psA)
    attn_l(3, psB)
    attn_o(3, psB)

    def _smm47(b):
        def go():
            extra = [v_gnegB]
            extra += blk_oevac[b - 1]
            attn_smm(b, psB, extra)
            attn_chain_pre(b, psB)
        return go
    def _l47(b):
        def go():
            attn_l(b, psB)
        return go
    def _o47(b):
        def go():
            attn_o(b, psB)
        return go
    attn_rest = [_smm47(4), None, _l47(4), None, _o47(4), None,
                 _smm47(5), None, _l47(5), None, _o47(5), None,
                 _smm47(6), _l47(6), _o47(6),
                 _smm47(7), _l47(7), _o47(7)]

    def flat_oevac(rng):
        w = []
        for b in rng:
            w.extend(blk_oevac[b])
        return w

    stream_free = {0: None, 1: None}
    slot_free = {}
    ln_state = {}

    def emit_p4(half, slots=None):
        nonlocal tile_i
        hofs = half * TH
        entry_cc = ("cC", S.n("cC"))
        slot_i = [0]

        def hit_slot():
            if slots and slot_i[0] < len(slots):
                cb = slots[slot_i[0]]
                if cb is not None:
                    cb()
            slot_i[0] += 1

        def _wslot(g):
            t = wsA if g % 2 == 0 else wsB
            half_off = 2048 * ((g // 2) % 2)
            return (t[:, half_off:half_off + 2048],
                    ("s0", "s1", "s2", "s3")[g % 4])

        wo_ld = {}
        def emit_wo_load(g):
            buf, sem = _wslot(g)
            waits = [("tC", p2_t_end)]
            if g % 4 in slot_free:
                waits.append(slot_free[g % 4])
            def wodma(e, g=g, buf=buf):
                src = wo[:, g * 256:(g + 1) * 256].rearrange(
                    "(c p) f -> p c f", p=128)
                dst = buf.rearrange("p (c f) -> p c f", c=8)
                return e.dma_start(out=dst, in_=src)
            S.op("gpsimd", wodma, wait=waits, inc=sem, k=16)
            wo_ld[g] = (sem, S.n(sem))

        for g in range(4):
            emit_wo_load(g)
        a1_gate = flat_oevac(range(4) if half == 0 else range(4, 8))
        if half == 0:
            a1_gate = a1_gate + [v_gnegB]
        for wc in range(WC):
            hit_slot()
            g = wc // 2
            buf, _ = _wslot(g)
            bank = tile_i % 4
            waits = [wo_ld[g]] + a1_gate
            if bank in bank_free:
                waits.append(bank_free[bank])
            for hec in range(WC):
                def womm(e, wc=wc, hec=hec, bank=bank, buf=buf, hofs=hofs,
                         first=(hec == 0), last=(hec == WC - 1)):
                    return e.matmul(
                        pbank(bank),
                        buf[:, hec * 256 + (wc % 2) * 128:
                            hec * 256 + (wc % 2) * 128 + 128],
                        otb[:, hec * 1024 + hofs:hec * 1024 + hofs + 512],
                        start=first, stop=last)
                S.op("tensor", womm, wait=waits,
                     inc="tC" if hec == WC - 1 else None)
                waits = []
            t_grp = S.n("tC")
            if wc % 2 == 1:
                slot_free[g % 4] = ("tC", t_grp)
            def woevac(e, wc=wc, bank=bank):
                return e.activation(
                    a1b[:, wc * TH:(wc + 1) * TH], pbank(bank), AF.Identity,
                    bias=bos[:, wc:wc + 1], scale=1.0)
            S.op("scalar", woevac, wait=[("tC", t_grp)])
            bank_free[bank] = ("cC", S.n("cC"))
            tile_i += 1
        a1_done = ("cC", S.n("cC"))

        pw_ld = {}
        xj_ld = {}
        xj_free = {0: None, 1: None}
        b1, b2 = (2, 3) if half == 0 else (4, 5)
        st = {}
        stat_w1 = [bank_free[b1]] if b1 in bank_free else []
        stat_w2 = [bank_free[b2]] if b2 in bank_free else []
        mm2_done = []
        pend_stats = []

        def emit_pw_load(g):
            buf, sem = _wslot(g)
            waits = []
            if g % 4 in slot_free:
                waits.append(slot_free[g % 4])
            def pwdma(e, g=g, buf=buf):
                src = projw[:, g * 256:(g + 1) * 256].rearrange(
                    "(c p) f -> p c f", p=128)
                dst = buf.rearrange("p (c f) -> p c f", c=8)
                return e.dma_start(out=dst, in_=src)
            S.op("gpsimd", pwdma, wait=waits, inc=sem, k=16)
            pw_ld[g] = (sem, S.n(sem))

        def emit_xj_load(dmc):
            par = dmc % 2
            xcb = xjb0 if par == 0 else xjb1
            sem = "x0" if par == 0 else "x1"
            xw = []
            if xj_free[par] is not None:
                xw.append(xj_free[par])
            else:
                xw.append(("tC", conv_all_mm))
            def xdma(e, dmc=dmc, xcb=xcb, hofs=hofs):
                return e.dma_start(
                    out=xcb[:, :],
                    in_=xT[dmc * 128:(dmc + 1) * 128, 1 + hofs:1 + hofs + TH])
            S.op("gpsimd", xdma, wait=xw, inc=sem, k=16)
            xj_ld[dmc] = (sem, S.n(sem))

        for g in range(4):
            emit_pw_load(g)
        emit_xj_load(0)
        emit_xj_load(1)
        res1_evac = []
        _sw = {4: [bank_free[4]] if 4 in bank_free else [],
               5: [bank_free[5]] if 5 in bank_free else []}

        def emit_ln1_stat(dmc):
            def sq(e, dmc=dmc):
                return e.activation(
                    sqtmp[:, (dmc % 2) * 512:(dmc % 2) * 512 + 512],
                    res1_ap(dmc), AF.Square)
            sqw = [("cC", res1_evac[dmc])]
            if dmc >= 2:
                sqw.append(("tC", mm2_done[dmc - 2]))
            S.op("scalar", sq, wait=sqw)
            c_sq = S.n("cC")
            def mm1(e, dmc=dmc):
                return e.matmul(pbank(4), ones128[:, :], res1_ap(dmc),
                                start=(dmc == 0), stop=(dmc == DC - 1),
                                skip_group_check=True)
            S.op("tensor", mm1, wait=_sw[4] + [("cC", res1_evac[dmc])],
                 inc="tC" if dmc == DC - 1 else None)
            _sw[4] = []
            def mm2(e, dmc=dmc):
                return e.matmul(pbank(5), ones128[:, :],
                                sqtmp[:, (dmc % 2) * 512:
                                      (dmc % 2) * 512 + 512],
                                start=(dmc == 0), stop=(dmc == DC - 1),
                                skip_group_check=True)
            S.op("tensor", mm2, wait=_sw[5] + [("cC", c_sq)], inc="tC")
            _sw[5] = []
            mm2_done.append(S.n("tC"))
        for dmc in range(DC):
            hit_slot()
            g = dmc // 2
            buf, _ = _wslot(g)
            xcb = xjb0 if dmc % 2 == 0 else xjb1
            bank = tile_i % 4
            waits = [pw_ld[g], a1_done]
            if bank in bank_free:
                waits.append(bank_free[bank])
            for wc in range(WC):
                def pmm(e, dmc=dmc, wc=wc, bank=bank, buf=buf,
                        first=(wc == 0)):
                    return e.matmul(
                        pbank(bank),
                        buf[:, wc * 256 + (dmc % 2) * 128:
                            wc * 256 + (dmc % 2) * 128 + 128],
                        a1b[:, wc * TH:(wc + 1) * TH],
                        start=first, stop=False)
                S.op("tensor", pmm, wait=waits)
                waits = []
            def pxadd(e, dmc=dmc, bank=bank, xcb=xcb):
                return e.matmul(pbank(bank), idb[:, :], xcb[:, :],
                                start=False, stop=True)
            S.op("tensor", pxadd, wait=[xj_ld[dmc], sd_idb], inc="tC")
            t_grp = S.n("tC")
            xj_free[dmc % 2] = ("tC", t_grp)
            if dmc % 2 == 1:
                slot_free[g % 4] = ("tC", t_grp)
            if dmc == DC - 1:
                stream_free[0] = stream_free[1] = ("tC", t_grp)
            pw_ = [("tC", t_grp)]
            if half == 1 and dmc < 2:
                pw_.extend(ln_state[0]["applied"])
            def pevac(e, dmc=dmc, bank=bank):
                return e.activation(
                    res1_ap(dmc), pbank(bank), AF.Identity,
                    bias=projb[:, dmc:dmc + 1], scale=1.0)
            S.op("scalar", pevac, wait=pw_)
            bank_free[bank] = ("cC", S.n("cC"))
            c_evac = S.n("cC")
            res1_evac.append(c_evac)
            tile_i += 1
            if dmc % 2 == 1 and dmc // 2 + 4 < 16:
                emit_pw_load(dmc // 2 + 4)
            if dmc + 2 < DC:
                emit_xj_load(dmc + 2)
        b1, b2 = 0, 1
        stat_w1 = [bank_free[b1]] if b1 in bank_free else []
        stat_w2 = [bank_free[b2]] if b2 in bank_free else []
        for dmc in range(DC):
            def sq(e, dmc=dmc):
                return e.activation(
                    sqtmp[:, (dmc % 2) * 512:(dmc % 2) * 512 + 512],
                    res1_ap(dmc), AF.Square)
            sqw = [("cC", res1_evac[dmc])]
            if dmc >= 2:
                sqw.append(("tC", mm2_done[dmc - 2]))
            S.op("scalar", sq, wait=sqw)
            c_sq = S.n("cC")
            def mm1(e, dmc=dmc, b1=b1):
                return e.matmul(pbank(b1), ones128[:, :], res1_ap(dmc),
                                start=(dmc == 0), stop=(dmc == DC - 1),
                                skip_group_check=True)
            S.op("tensor", mm1,
                 wait=stat_w1 + [("cC", res1_evac[dmc])],
                 inc="tC" if dmc == DC - 1 else None)
            stat_w1 = []
            def mm2(e, dmc=dmc, b2=b2):
                return e.matmul(pbank(b2), ones128[:, :],
                                sqtmp[:, (dmc % 2) * 512:
                                      (dmc % 2) * 512 + 512],
                                start=(dmc == 0), stop=(dmc == DC - 1),
                                skip_group_check=True)
            S.op("tensor", mm2, wait=stat_w2 + [("cC", c_sq)], inc="tC")
            stat_w2 = []
            mm2_done.append(S.n("tC"))
        st["b1"], st["b2"], st["t_stats"] = b1, b2, S.n("tC")
        st["res1_evac"] = res1_evac
        ln_state[half] = st
    def emit_ln_chain(st, nm_t, rstd_t, make_bf=False):
        b1, b2, t_stats = st["b1"], st["b2"], st["t_stats"]
        cw_ = [("tC", t_stats)]
        if "extra_chain_wait" in st:
            cw_.append(st["extra_chain_wait"])
        nm_out = nmb if make_bf else nm_t
        rs_out = rstdb if make_bf else rstd_t
        S.op("scalar", lambda e, nm_out=nm_out: e.activation(
            nm_out[:, :], pbank(b1), AF.Copy, bias=0.0, scale=-1.0 / D),
            wait=cw_)
        S.op("scalar", lambda e: e.activation(
            tmp1[:, :], pbank(b1), AF.Square, bias=0.0, scale=1.0 / D))
        S.op("scalar", lambda e: e.activation(
            tmp2[:, :], pbank(b2), AF.Copy, bias=1e-5, scale=1.0 / D))
        c_st = S.n("cC")
        bank_free[b1] = ("cC", c_st)
        bank_free[b2] = ("cC", c_st)
        S.op("vector", lambda e: e.tensor_sub(tmp2[:, :], tmp2[:, :],
                                              tmp1[:, :]),
             wait=[("cC", c_st)])
        v_r = S.n("vC")
        S.op("scalar", lambda e: e.activation(
            tmp1[:, :], tmp2[:, :], AF.Ln), wait=[("vC", v_r)])
        S.op("scalar", lambda e, rs_out=rs_out: e.activation(
            rs_out[:, :], tmp1[:, :], AF.Exp, bias=0.0, scale=-0.5))
        return ("cC", S.n("cC"))

    def p45_bank(i):
        return i % 4

    def _x1_grp(half, g):
        if half == 0:
            return x1bA[:, g * 2048:(g + 1) * 2048]
        if g < 4:
            return x1bB_lo[:, g * 2048:(g + 1) * 2048]
        return x1bB_hi[:, (g - 4) * 2048:(g - 3) * 2048]

    def emit_ln1_apply(half):
        st = ln_state[half]
        gate = emit_ln_chain(st, negmean, rstd, make_bf=True)
        for g in range(8):
            r = res1_grp(g).rearrange("p (c f) -> p c f", c=4)
            xo = _x1_grp(half, g).rearrange("p (c f) -> p c f", c=4)
            eng = "gpsimd" if (half == 1 and g % 2 == 1) else "vector"
            S.op(eng, lambda e, r=r: e.tensor_add(
                r, r, bc3(nmb[:, :], 4)), wait=[gate])
            S.op(eng, lambda e, r=r, xo=xo: e.tensor_mul(
                xo, r, bc3(rstdb[:, :], 4)))
        st["applied"] = [("vC", S.n("vC")), ("gC", S.n("gC"))]

    def emit_ht(half):
        nonlocal tile_i
        for erc in range(ERC):
            bank = 2 + erc
            waits = [g_at] + ln_state[half]["applied"]
            if bank in bank_free:
                waits.append(bank_free[bank])
            for dc in range(DC):
                def hmm(e, erc=erc, dc=dc, bank=bank, half=half,
                        first=(dc == 0), last=(dc == DC - 1)):
                    return e.matmul(
                        pbank(bank),
                        atb[:, dc * 256 + erc * 128:dc * 256 + erc * 128 + 128],
                        x1b_ap(half, dc),
                        start=first, stop=last)
                S.op("tensor", hmm, wait=waits,
                     inc="tC" if dc == DC - 1 else None)
                waits = []
            t_grp = S.n("tC")
            def hevac(e, erc=erc, bank=bank, half=half):
                return e.tensor_scalar_mul(
                    hb[:, erc * 1024 + half * TH:erc * 1024 + half * TH + TH],
                    pbank(bank), probs[:, erc:erc + 1])
            S.op("vector", hevac, wait=[("tC", t_grp)])
            bank_free[bank] = ("vC", S.n("vC"))
            tile_i += 1

    wt_ld = {}
    def emit_wt_load(pg):
        par = pg % 2
        buf = wsF0 if par == 0 else wsF1
        sem = "s0" if par == 0 else "s1"
        waits = [] if pg >= 2 else list(wt_first_gate)
        if stream_free[par] is not None:
            waits.append(stream_free[par])
        def wtdma(e, pg=pg, buf=buf):
            src = Wt[:, pg * 256:(pg + 1) * 256].rearrange(
                "(c p) f -> p c f", p=128)
            dst = buf[:, :].rearrange("p (c f) -> p c f", c=32)
            return e.dma_start(out=dst, in_=src)
        S.op("gpsimd", wtdma, wait=waits, inc=sem, k=16)
        wt_ld[pg] = (sem, S.n(sem))

    emit_p4(0, slots=attn_rest)
    p3_done = [("cC", S.n("cC")), ("vC", S.n("vC")), ("gC", S.n("gC")),
               ("tC", S.n("tC"))]
    emit_ln1_apply(0)
    emit_p4(1)
    p4_t_end = ("tC", S.n("tC"))
    def atdma(e):
        src = At[:, :].rearrange("(c p) f -> p c f", p=128)
        dst = atb[:, :].rearrange("p (c f) -> p c f", c=32)
        return e.dma_start(out=dst, in_=src)
    S.op("gpsimd", atdma, wait=list(p3_done) + [g_wv], inc="w2", k=16)
    g_at = ("w2", S.n("w2"))
    emit_ht(0)
    hb_done_h0 = ("vC", S.n("vC"))
    t_p4done = ("tC", S.n("tC"))
    wt_first_gate = [t_p4done]
    emit_wt_load(0)
    emit_wt_load(1)
    emit_ln1_apply(1)


    emit_ht(1)
    t_ht_end = ("tC", S.n("tC"))
    hb_done = ("vC", S.n("vC"))
    for erc in range(ERC):
        def bmdma2(e, erc=erc):
            return e.dma_start(
                out=Bmb[:, erc * D:(erc + 1) * D],
                in_=Bm[erc * 128:(erc + 1) * 128, :])
        S.op("gpsimd", bmdma2, wait=[t_ht_end], inc="w2", k=16)
    g_bm = ("w2", S.n("w2"))
    ln2 = {0: dict(b1=4, b2=5, mm2_done=[]), 1: dict(b1=6, b2=7, mm2_done=[])}
    for hset in ln2.values():
        for bk in (hset["b1"], hset["b2"]):
            hset.setdefault("w", {})[bk] = (
                [bank_free[bk]] if bk in bank_free else [])
    pend_stats2 = []
    res2_writes = {}

    def emit_stats2(oc, half, c_evac):
        hs = ln2[half]
        def sq2(e, oc=oc, half=half):
            return e.activation(
                sqtmp[:, half * 512:half * 512 + 512],
                res2_ap(oc)[:, half * TH:half * TH + TH], AF.Square)
        sqw = [("cC", c_evac)]
        if oc >= 1:
            sqw.append(("tC", hs["mm2_done"][oc - 1]))
        S.op("scalar", sq2, wait=sqw)
        c_sq = S.n("cC")
        def mm1(e, oc=oc, half=half, b1=hs["b1"]):
            return e.matmul(
                pbank(b1), ones128[:, :],
                res2_ap(oc)[:, half * TH:half * TH + TH],
                start=(oc == 0), stop=(oc == DC - 1),
                skip_group_check=True)
        S.op("tensor", mm1, wait=hs["w"][hs["b1"]] + [("cC", c_evac)],
             inc="tC" if oc == DC - 1 else None)
        hs["w"][hs["b1"]] = []
        def mm2(e, oc=oc, half=half, b2=hs["b2"]):
            return e.matmul(
                pbank(b2), ones128[:, :],
                sqtmp[:, half * 512:half * 512 + 512],
                start=(oc == 0), stop=(oc == DC - 1),
                skip_group_check=True)
        S.op("tensor", mm2, wait=hs["w"][hs["b2"]] + [("cC", c_sq)],
             inc="tC")
        hs["w"][hs["b2"]] = []
        hs["mm2_done"].append(S.n("tC"))

    for pg in range(16):
        par = pg % 2
        buf = wsF0 if par == 0 else wsF1
        for half in range(2):
            for sub in range(2):
                oc = pg * 2 + sub
                bank = p45_bank(tile_i)
                waits = [wt_ld[pg], g_bm,
                         hb_done_h0 if half == 0 else hb_done]
                if bank in bank_free:
                    waits.append(bank_free[bank])
                for dc in range(DC):
                    def fmm(e, dc=dc, bank=bank, buf=buf, half=half,
                            sub=sub, first=(dc == 0)):
                        return e.matmul(
                            pbank(bank),
                            buf[:, dc * 256 + sub * 128:
                                dc * 256 + sub * 128 + 128],
                            x1b_ap(half, dc),
                            start=first, stop=False)
                    S.op("tensor", fmm, wait=waits)
                    waits = []
                for erc in range(ERC):
                    def bmm(e, oc=oc, erc=erc, bank=bank, half=half):
                        return e.matmul(
                            pbank(bank),
                            Bmb[:, erc * D + oc * 128:erc * D + oc * 128 + 128],
                            hb[:, erc * 1024 + half * TH:
                               erc * 1024 + half * TH + TH],
                            start=False, stop=False)
                    S.op("tensor", bmm)
                def fxadd(e, oc=oc, bank=bank, half=half):
                    return e.matmul(pbank(bank), idb[:, :],
                                    x1b_ap(half, oc),
                                    start=False, stop=True)
                S.op("tensor", fxadd, wait=[sd_idb], inc="tC")
                t_grp = S.n("tC")
                def fevac(e, oc=oc, bank=bank, half=half):
                    return e.activation(
                        res2_ap(oc)[:, half * TH:half * TH + TH],
                        pbank(bank), AF.Identity,
                        bias=ffnb[:, oc:oc + 1], scale=1.0)
                S.op("scalar", fevac, wait=[("tC", t_grp)])
                bank_free[bank] = ("cC", S.n("cC"))
                c_evac = S.n("cC")
                res2_writes[(oc, half)] = c_evac
                tile_i += 1
        stream_free[par] = ("tC", S.n("tC"))
        if pg + 2 < 16:
            emit_wt_load(pg + 2)
    for half in range(2):
        for oc in range(DC):
            emit_stats2(oc, half, res2_writes[(oc, half)])

    ob_free = [None, None, None, None]
    apply_specs = []
    for half in range(2):
        hs = ln2[half]
        st = dict(b1=hs["b1"], b2=hs["b2"], t_stats=S.n("tC"))
        nm_t = negmean if half == 0 else nm2
        rs_t = rstd if half == 0 else rstd2
        gate = emit_ln_chain(st, nm_t, rs_t, make_bf=False)
        apply_specs.append((half, nm_t, rs_t, gate))
    for dc in range(DC):
        for half, nm_t, rs_t, gate in apply_specs:
            slot = (dc % 2) * 2 + half
            osem = f"o{slot}"
            hofs = half * TH
            eng = "vector" if half == 0 else "gpsimd"
            ww = [gate]
            if ob_free[slot] is not None:
                ww.append(ob_free[slot])
            def app1(e, dc=dc, slot=slot, hofs=hofs, nm_t=nm_t):
                return e.tensor_add(ob[slot][:, :],
                                    res2_ap(dc)[:, hofs:hofs + TH],
                                    nm_t[:, :])
            def app2(e, slot=slot, rs_t=rs_t):
                return e.tensor_mul(ob[slot][:, :], ob[slot][:, :],
                                    rs_t[:, :])
            if eng == "vector":
                S.op("vector", app1, wait=ww)
                S.op("vector", app2)
                v_ap = ("vC", S.n("vC"))
            else:
                S.op("gpsimd", app1, wait=ww, inc="gC")
                S.op("gpsimd", app2, inc="gC")
                v_ap = ("gC", S.n("gC"))
            def sdma(e, dc=dc, slot=slot, hofs=hofs):
                return e.dma_start(
                    out=outT[dc * 128:(dc + 1) * 128, hofs:hofs + TH],
                    in_=ob[slot][:, :])
            S.op("sync", sdma, wait=[v_ap], inc=osem, k=16)
            ob_free[slot] = (osem, S.n(osem))

    if tap_outs:
        wait_all = [("tC", S.n("tC")), ("vC", S.n("vC")), ("cC", S.n("cC"))]
        tap_srcs = dict(
            qkvb=qkvb, qhb=qhb, khb=khb, vtb=vtb, otb=otb, gneg=gneg,
            a1b=a1b, hbt=hb, m1all=m1all, pb0=pb0, pb1=pb1, linv0=linv0,
            rstd=rstd, x1bA=x1bA, res1=res1,
        )
        for tname, ap_d in tap_outs.items():
            src = tap_srcs[tname]
            S.op("sync", lambda e, d=ap_d, s=src: e.dma_start(
                out=d, in_=s[:, :]), wait=wait_all, inc="o0", k=16)

    fin = [(f"o{i}", S.n(f"o{i}")) for i in range(4)]
    S.op("sync", lambda e: e.nop(), wait=fin)

    with ExitStack() as stx:
        sems = {name: stx.enter_context(nc.semaphore(name)) for name in SEMS}
        blk = stx.enter_context(nc.Block())
        for eng in Sched.ENGS:
            ops = S.q[eng]
            if not ops:
                continue
            getattr(blk, eng)(lambda e, ops=ops: [f(e, sems) for f in ops])
    return nc


def stage_inputs(inputs):
    f32 = np.float32
    x = np.asarray(inputs["L0_attn_inputs"], f32)
    N, L, _ = x.shape
    per_core = []
    for c in range(8):
        n, s = divmod(c, 2)
        t0 = s * 1024
        xTh = np.zeros((D, T + 2), f32)
        lo = max(t0 - 1, 0)
        hi = min(t0 + T + 1, L)
        xTh[:, lo - (t0 - 1):hi - (t0 - 1)] = x[n, lo:hi, :].T
        per_core.append(np.ascontiguousarray(xTh))

    shared = {
        "cw": np.ascontiguousarray(np.asarray(inputs["L0_attn_conv_w"], f32)),
        "wq": np.ascontiguousarray(np.asarray(inputs["L0_attn_wq"], f32).reshape(W1, W1)),
        "wk": np.ascontiguousarray(np.asarray(inputs["L0_attn_wk"], f32).reshape(W1, W1)),
        "wv": np.ascontiguousarray(np.asarray(inputs["L0_attn_wv"], f32).reshape(W1, W1)),
        "wo": np.ascontiguousarray(np.asarray(inputs["L0_attn_wo"], f32).reshape(W1, W1)),
        "projw": np.ascontiguousarray(np.asarray(inputs["L0_attn_proj_w"], f32)),
        "Wt": np.ascontiguousarray(np.asarray(inputs["L0_ffn_W"], f32).T),
        "At": np.ascontiguousarray(np.asarray(inputs["L0_ffn_A"], f32).reshape(256, D).T),
        "Bm": np.ascontiguousarray(np.asarray(inputs["L0_ffn_B"], f32).transpose(0, 2, 1).reshape(256, D)),
        "convb": np.ascontiguousarray(np.asarray(inputs["L0_attn_conv_b"], f32).reshape(QC, 128).T),
        "bq": np.ascontiguousarray(np.asarray(inputs["L0_attn_bq"], f32).reshape(WC, 128).T),
        "bk": np.ascontiguousarray(np.asarray(inputs["L0_attn_bk"], f32).reshape(WC, 128).T),
        "bo": np.ascontiguousarray(np.asarray(inputs["L0_attn_bo"], f32).reshape(WC, 128).T),
        "projb": np.ascontiguousarray(np.asarray(inputs["L0_attn_proj_b"], f32).reshape(DC, 128).T),
        "ffnb": np.ascontiguousarray(np.asarray(inputs["L0_ffn_b"], f32).reshape(DC, 128).T),
        "probs": np.ascontiguousarray(
            np.repeat(np.asarray(inputs["L0_ffn_probs"], f32), 16).reshape(ERC, 128).T),
        "bv": np.ascontiguousarray(np.asarray(inputs["L0_attn_bv"], f32).reshape(1, W1)),
        "ones128": np.ones((128, 128), BF16N),
        "ones1": np.ones((1, 128), BF16N),
        "mask": np.ascontiguousarray(
            np.tile(np.triu(np.ones((128, 128), np.float32)), (1, 16)).astype(BF16N)),
        "idf": np.eye(128, dtype=f32),
        "idb": np.eye(128, dtype=BF16N),
    }
    in_maps = [{"xT": per_core[c], **shared} for c in range(8)]
    return in_maps


def kernel(**inputs):
    in_maps = stage_inputs(inputs)
    nc = build_nc()
    res = run_bass_kernel_spmd(nc, in_maps, core_ids=list(range(8)))
    out = np.empty((4, 2048, D), np.float32)
    for c in range(8):
        n, s = divmod(c, 2)
        out[n, s * 1024:(s + 1) * 1024, :] = res.results[c]["outT"].T
    return out


# version 16
# speedup vs baseline: 1.0675x; 1.0257x over previous
import numpy as np
import ml_dtypes
from contextlib import ExitStack
import concourse.bass as bass
import concourse.mybir as mybir
from concourse.bass_utils import run_bass_kernel_spmd

BF16N = ml_dtypes.bfloat16
F32 = mybir.dt.float32
BF16 = mybir.dt.bfloat16
AF = mybir.ActivationFunctionType
ALU = mybir.AluOpType
AX = mybir.AxisListType

D = 4096
DC = 32
W1 = 1024
WC = 8
QC = 24
T = 1024
TH = 512
H = 16
NB = 8
ERC = 2

OFF_ONES128 = 0
OFF_MASK = 256
OFF_IDF32 = 4352
OFF_CONVB = 4864
OFF_BQS = 4992
OFF_BK = 5024
OFF_BO = 5056
OFF_PROJB = 5088
OFF_FFNB = 5216
OFF_PROBS = 5344
OFF_ONES1 = 5376
OFF_BVF = 5632
OFF_BVB = 9728
OFF_M1ALL = 11776
OFF_NEGG8 = 11808
OFF_NEGROWF = 11840
OFF_NEGROWB = 11872
OFF_GNEG = 11904
OFF_SQTMP = 11936
OFF_STATS = 14016
OFF_HBOX = 22208
OFF_LINV = 26304
REG0 = 42688
REG1 = 108352
REG2 = 157504
OFF_SPARE = 206656
OFF_WK7 = 210752


import dataclasses


def bc3(ap2d, reps):
    return dataclasses.replace(ap2d, ap=[ap2d.ap[0], [0, reps], ap2d.ap[1]])


class Sched:
    ENGS = ("sync", "gpsimd", "tensor", "scalar", "vector")

    def __init__(self):
        self.q = {e: [] for e in self.ENGS}
        self.cnt = {}
        self.hw = {e: {} for e in self.ENGS}

    def n(self, sem):
        return self.cnt.get(sem, 0)

    def op(self, eng, fn, wait=(), inc=None, k=1):
        if eng == "vector":
            wait = list(wait) + [("vC", self.n("vC"))]
            if inc is None:
                inc = "vC"
        elif eng == "gpsimd":
            wait = list(wait) + [("gC", self.n("gC"))]
            if inc is None and k == 1:
                inc = "gC"
        elif eng == "scalar":
            wait = list(wait) + [("cC", self.n("cC"))]
            if inc is None:
                inc = "cC"
        waits = []
        for s, v in wait:
            if v is None or v <= 0:
                continue
            if self.hw[eng].get(s, 0) >= v:
                continue
            self.hw[eng][s] = v
            waits.append((s, v))
        if inc is not None:
            self.cnt[inc] = self.cnt.get(inc, 0) + k

        def run(e, sems, fn=fn, waits=waits, inc=inc, k=k):
            for s, v in waits:
                e.wait_ge(sems[s], v)
            inst = fn(e)
            if inc is not None:
                inst.then_inc(sems[inc], k)

        self.q[eng].append(run)
        return self.cnt.get(inc, 0) if inc is not None else None


def build_nc(taps=(), stop_after='full'):
    nc = bass.Bass("TRN2", target_bir_lowering=False)
    dt = lambda name, shape, dtype=F32: nc.dram_tensor(
        name, list(shape), dtype, kind="ExternalInput").ap()

    xT = dt("xT", (D, T + 2))
    cw = dt("cw", (3, D, 3 * W1))
    wq = dt("wq", (W1, W1))
    wk = dt("wk", (W1, W1))
    wv = dt("wv", (W1, W1))
    wo = dt("wo", (W1, W1))
    projw = dt("projw", (W1, D))
    Wt = dt("Wt", (D, D))
    At = dt("At", (D, 2 * 128))
    Bm = dt("Bm", (2 * 128, D))
    convb_d = dt("convb", (128, QC))
    bq_d = dt("bq", (128, WC))
    bk_d = dt("bk", (128, WC))
    bo_d = dt("bo", (128, WC))
    projb_d = dt("projb", (128, DC))
    ffnb_d = dt("ffnb", (128, DC))
    probs_d = dt("probs", (128, ERC))
    bv_d = dt("bv", (1, W1))
    ones128_d = dt("ones128", (128, 128), BF16)
    ones1_d = dt("ones1", (1, 128), BF16)
    mask_d = dt("mask", (128, 2048), BF16)
    idf_d = dt("idf", (128, 128), F32)
    idb_d = dt("idb", (128, 128), BF16)

    outT = nc.dram_tensor("outT", [D, T], F32, kind="ExternalOutput").ap()
    tap_outs = {}
    for tname, shape, tdt in taps:
        tap_outs[tname] = nc.dram_tensor(
            "tap_" + tname, list(shape), tdt, kind="ExternalOutput").ap()

    base = (nc._sbuf_addr_for_side("left") + 63) // 64 * 64

    def sb(name, off, shape, dtype):
        return nc.alloc_sbuf_tensor_at(name, list(shape), dtype,
                                       offset=base + off)

    ones128 = sb("ones128_s", OFF_ONES128, (128, 128), BF16)
    maskt = sb("maskt", OFF_MASK, (128, 2048), BF16)
    idf = sb("idf_s", OFF_IDF32, (128, 128), F32)
    convb = sb("convb_s", OFF_CONVB, (128, QC), F32)
    bqs = sb("bqs", OFF_BQS, (128, WC), F32)
    bks = sb("bks", OFF_BK, (128, WC), F32)
    bos = sb("bos", OFF_BO, (128, WC), F32)
    projb = sb("projb_s", OFF_PROJB, (128, DC), F32)
    ffnb = sb("ffnb_s", OFF_FFNB, (128, DC), F32)
    probs = sb("probs_s", OFF_PROBS, (128, ERC), F32)
    ones1 = sb("ones1_s", OFF_ONES1, (1, 128), BF16)
    bvf = sb("bvf", OFF_BVF, (1, W1), F32)
    bvb = sb("bvb", OFF_BVB, (1, W1), BF16)
    m1all = sb("m1all", OFF_M1ALL, (128, NB), F32)
    negg8 = sb("negg8", OFF_NEGG8, (NB, 1), F32)
    negrowf = sb("negrowf", OFF_NEGROWF, (1, NB), F32)
    negrowb = sb("negrowb", OFF_NEGROWB, (1, NB), BF16)
    gneg = sb("gneg", OFF_GNEG, (128, NB), F32)
    sqtmp = sb("sqtmp", OFF_SQTMP, (128, 1024), BF16)
    negmean = sb("negmean", OFF_STATS, (128, TH), F32)
    tmp1 = sb("tmp1", OFF_STATS + 2048, (128, TH), F32)
    tmp2 = sb("tmp2", OFF_STATS + 4096, (128, TH), F32)
    rstd = sb("rstd", OFF_STATS + 6144, (128, TH), F32)
    nm2 = sb("nm2", OFF_LINV + 8192, (128, TH), F32)
    rstd2 = sb("rstd2", OFF_LINV + 10240, (128, TH), F32)
    nmb = sb("nmb", OFF_HBOX, (128, TH), BF16)
    rstdb = sb("rstdb", OFF_HBOX + 1024, (128, TH), BF16)
    idb = sb("idb", OFF_BVF, (128, 128), BF16)
    hb = sb("hb", OFF_SPARE, (128, 2048), BF16)
    linv0 = sb("linv0", OFF_LINV, (128, 2048), F32)
    linv1 = sb("linv1", OFF_LINV + 8192, (128, 2048), F32)
    atb = sb("atb", OFF_LINV, (128, 32 * 256), BF16)
    ob = [sb(f"ob{i}", OFF_LINV + i * 2048, (128, TH), F32) for i in range(4)]
    wqpre = [sb(f"wqpre{c}", OFF_LINV + c * 2048, (128, 1024), BF16)
             for c in range(8)]
    wk_offs = [OFF_STATS, OFF_STATS + 2048, OFF_STATS + 4096, OFF_STATS + 6144,
               OFF_HBOX, OFF_HBOX + 2048, OFF_SQTMP, OFF_WK7]
    wkpre = [sb(f"wkpre{c}", wk_offs[c], (128, 1024), BF16) for c in range(8)]
    xb = sb("xb", REG0, (128, DC * (T + 2)), BF16)
    wvb = sb("wvb", REG0 + 32768, (128, 8 * 1024), BF16)
    otb = sb("otb", REG0, (128, 8 * 1024), BF16)
    x1bA = sb("x1bA", REG0 + 16384, (128, DC * TH), BF16)
    x1bB_lo = sb("x1bB_lo", REG0, (128, 16 * TH), BF16)
    x1bB_hi = sb("x1bB_hi", REG0 + 49152, (128, 16 * TH), BF16)
    qkvb = sb("qkvb", REG1, (128, QC * 1024), BF16)
    pb0 = sb("pb0", REG1 + 16384, (128, 2048), BF16)
    pb1 = sb("pb1", REG1 + 20480, (128, 2048), BF16)
    wsA = sb("wsA", REG1, (128, 4096), BF16)
    wsB = sb("wsB", REG1 + 8192, (128, 4096), BF16)
    Bmb = sb("Bmb", OFF_LINV, (128, 2 * D), BF16)
    wsF0 = sb("wsF0", REG1, (128, 8192), BF16)
    sq4 = sb("sq4", REG1, (128, 2048), BF16)
    l2tmp1 = sb("l2tmp1", REG1 + 8192, (128, TH), F32)
    l2tmp2 = sb("l2tmp2", REG1 + 10240, (128, TH), F32)
    wsF1 = sb("wsF1", REG1 + 16384, (128, 8192), BF16)
    res2_m = sb("res2_m", REG1 + 32768, (128, 8 * 1024), BF16)
    cwb0 = sb("cwb0", REG2, (128, 3 * 4096), BF16)
    cwb1 = sb("cwb1", REG2 + 24576, (128, 3 * 4096), BF16)
    qhb = sb("qhb", REG2, (128, 8 * 1024), BF16)
    khb = sb("khb", REG2 + 16384, (128, 8 * 1024), BF16)
    vtb = sb("vtb", REG2 + 32768, (128, 8 * 1024), BF16)
    res1lo = sb("res1lo", REG1 + 32768, (128, 8 * TH), BF16)
    res1hi = sb("res1hi", REG2, (128, 24 * TH), BF16)
    a1b = sb("a1b", REG1 + 24576, (128, WC * TH), BF16)
    xjb0 = sb("xjb0", REG0 + 53248, (128, TH), BF16)
    xjb1 = sb("xjb1", REG0 + 54272, (128, TH), BF16)
    res2_lo = sb("res2_lo", REG2, (128, 16 * 1024), BF16)
    res2_hi = sb("res2_hi", REG2 + 32768, (128, 8 * 1024), BF16)

    psA = nc.alloc_psum_tensor("psA", [128, 2048], F32)
    psB = nc.alloc_psum_tensor("psB", [128, 2048], F32)

    def pbank(k):
        t = psA if k < 4 else psB
        return t[:, (k % 4) * 512:(k % 4 + 1) * 512]

    def pset(p):
        return psA if p == 0 else psB

    S = Sched()
    SEMS = ("xS", "c0", "c1", "w2", "s0", "s1", "s2", "s3", "sD", "gC",
            "x0", "x1", "o0", "o1", "o2", "o3", "tC", "vC", "cC")

    def res1_ap(c):
        if c < 8:
            return res1lo[:, c * TH:(c + 1) * TH]
        return res1hi[:, (c - 8) * TH:(c - 7) * TH]

    def res1_grp(g):
        if g < 2:
            return res1lo[:, g * 2048:(g + 1) * 2048]
        return res1hi[:, (g - 2) * 2048:(g - 1) * 2048]

    def x1b_ap(half, c):
        if half == 0:
            return x1bA[:, c * TH:(c + 1) * TH]
        if c < 16:
            return x1bB_lo[:, c * TH:(c + 1) * TH]
        return x1bB_hi[:, (c - 16) * TH:(c - 15) * TH]

    def res2_ap(c):
        if c < 16:
            return res2_lo[:, c * 1024:(c + 1) * 1024]
        if c < 24:
            return res2_m[:, (c - 16) * 1024:(c - 15) * 1024]
        return res2_hi[:, (c - 24) * 1024:(c - 23) * 1024]

    for sbuf_t, dram_ap in (
            (ones128, ones128_d), (maskt, mask_d), (idf, idf_d),
            (convb, convb_d), (bqs, bq_d), (bks, bk_d), (bos, bo_d),
            (projb, projb_d), (ffnb, ffnb_d), (probs, probs_d),
            (ones1, ones1_d), (bvf, bv_d)):
        S.op("sync", lambda e, o=sbuf_t, i=dram_ap: e.dma_start(
            out=o[:, :], in_=i), inc="sD", k=16)
    n_const = S.n("sD")
    S.op("vector", lambda e: e.tensor_scalar_mul(bqs[:, :], bqs[:, :], 0.125),
         wait=[("sD", n_const)])
    S.op("vector", lambda e: e.tensor_copy(bvb[:, :], bvf[:, :]))
    v_prep = S.n("vC")
    S.op("sync", lambda e: e.dma_start(out=idb[:, :], in_=idb_d),
         wait=[("vC", v_prep)], inc="sD", k=16)
    sd_idb = ("sD", S.n("sD"))

    for i in range(4):
        def xldma(e, i=i):
            src = xT[i * 8 * 128:(i + 1) * 8 * 128, :].rearrange(
                "(c p) f -> p c f", p=128)
            dst = xb[:, i * 8 * (T + 2):(i + 1) * 8 * (T + 2)].rearrange(
                "p (c f) -> p c f", c=8)
            return e.dma_start(out=dst, in_=src)
        S.op("gpsimd", xldma, inc="xS", k=16)
    g_xb = ("xS", S.n("xS"))

    conv_w_load_done = {}
    conv_last_mm_using_buf = {}

    def emit_conv_load(oc):
        par = oc % 2
        buf = cwb0 if par == 0 else cwb1
        sem = "c0" if par == 0 else "c1"
        waits = []
        if oc >= 2:
            waits.append(("tC", conv_last_mm_using_buf[oc - 2]))
        for w in range(3):
            for hh in range(2):
                def cwdma(e, oc=oc, w=w, hh=hh, buf=buf):
                    src = cw[w, hh * 2048:(hh + 1) * 2048,
                             oc * 128:(oc + 1) * 128].rearrange(
                        "(c p) f -> p c f", p=128)
                    dst = buf[:, w * 4096 + hh * 2048:
                              w * 4096 + (hh + 1) * 2048].rearrange(
                        "p (c f) -> p c f", c=16)
                    return e.dma_start(out=dst, in_=src)
                S.op("gpsimd", cwdma, wait=waits, inc=sem, k=16)
                waits = []
        conv_w_load_done[oc] = (sem, S.n(sem))

    def emit_pre_load(i):
        if i < 8:
            wt, dst = wq, wqpre[i]
            srcsl = wq[i * 128:(i + 1) * 128, :]
        else:
            wt, dst = wk, wkpre[i - 8]
            srcsl = wk[(i - 8) * 128:(i - 8 + 1) * 128, :]
        def pldma(e, srcsl=srcsl, dst=dst):
            return e.dma_start(out=dst[:, :], in_=srcsl)
        S.op("gpsimd", pldma, inc="w2", k=16)

    emit_conv_load(0)
    emit_conv_load(1)

    bank_free = {}
    tile_i = 0
    first_tensor = [("sD", n_const)]
    first_scalar = [("sD", n_const)]
    for oc in range(QC):
        par = oc % 2
        buf = cwb0 if par == 0 else cwb1
        for tokc in range(2):
            bank = tile_i % 8
            waits = [conv_w_load_done[oc], g_xb] + first_tensor
            first_tensor = []
            if bank in bank_free:
                waits.append(bank_free[bank])
            for dc in range(DC):
                for w in range(3):
                    def cmm(e, oc=oc, tokc=tokc, dc=dc, w=w, buf=buf,
                            bank=bank, first=(dc == 0 and w == 0),
                            last=(dc == DC - 1 and w == 2)):
                        return e.matmul(
                            pbank(bank),
                            buf[:, (w * 32 + dc) * 128:(w * 32 + dc + 1) * 128],
                            xb[:, dc * (T + 2) + tokc * 512 + w:
                               dc * (T + 2) + tokc * 512 + w + 512],
                            start=first, stop=last)
                    S.op("tensor", cmm, wait=waits,
                         inc="tC" if (dc == DC - 1 and w == 2) else None)
                    waits = []
            t_grp = S.n("tC")
            if tile_i % 3 != 2:
                def cevac(e, oc=oc, tokc=tokc, bank=bank):
                    return e.activation(
                        qkvb[:, oc * 1024 + tokc * 512:
                             oc * 1024 + tokc * 512 + 512],
                        pbank(bank), AF.Identity,
                        bias=convb[:, oc:oc + 1], scale=1.0)
                S.op("scalar", cevac, wait=[("tC", t_grp)] + first_scalar)
                first_scalar = []
                bank_free[bank] = ("cC", S.n("cC"))
            else:
                def cevac(e, oc=oc, tokc=tokc, bank=bank):
                    return e.tensor_scalar_add(
                        qkvb[:, oc * 1024 + tokc * 512:
                             oc * 1024 + tokc * 512 + 512],
                        pbank(bank), convb[:, oc:oc + 1])
                S.op("vector", cevac, wait=[("tC", t_grp)])
                bank_free[bank] = ("vC", S.n("vC"))
            tile_i += 1
        conv_last_mm_using_buf[oc] = S.n("tC")
        if oc + 2 < QC:
            emit_conv_load(oc + 2)
        if oc < 16:
            emit_pre_load(oc)
    conv_all_mm = S.n("tC")
    conv_all_evac = [("cC", S.n("cC")), ("vC", S.n("vC"))]
    g_pre = ("w2", S.n("w2"))

    def wvdma(e):
        src = wv[:, :].rearrange("(c p) f -> p c f", p=128)
        dst = wvb[:, :].rearrange("p (c f) -> p c f", c=8)
        return e.dma_start(out=dst, in_=src)
    S.op("gpsimd", wvdma, wait=[("tC", conv_all_mm), g_pre], inc="w2", k=16)
    g_wv = ("w2", S.n("w2"))

    def emit_qk(wchunks, qoff, outb, bias_col, scale):
        nonlocal tile_i
        for hc in range(WC):
            for tokc in range(2):
                bank = tile_i % 8
                waits = [g_pre] + conv_all_evac
                if bank in bank_free:
                    waits.append(bank_free[bank])
                for dc in range(8):
                    def qmm(e, hc=hc, tokc=tokc, dc=dc, bank=bank,
                            qoff=qoff, wcs=wchunks,
                            first=(dc == 0), last=(dc == 7)):
                        return e.matmul(
                            pbank(bank),
                            wcs[dc][:, hc * 128:hc * 128 + 128],
                            qkvb[:, (qoff + dc) * 1024 + tokc * 512:
                                 (qoff + dc) * 1024 + tokc * 512 + 512],
                            start=first, stop=last)
                    S.op("tensor", qmm, wait=waits,
                         inc="tC" if dc == 7 else None)
                    waits = []
                t_grp = S.n("tC")
                def qevac(e, hc=hc, tokc=tokc, bank=bank, outb=outb,
                          bias_col=bias_col, scale=scale):
                    return e.activation(
                        outb[:, hc * 1024 + tokc * 512:
                             hc * 1024 + tokc * 512 + 512],
                        pbank(bank), AF.Identity,
                        bias=bias_col[:, hc:hc + 1], scale=scale)
                S.op("scalar", qevac, wait=[("tC", t_grp), ("vC", v_prep)])
                bank_free[bank] = ("cC", S.n("cC"))
                tile_i += 1

    emit_qk(wqpre, 0, qhb, bqs, 0.125)
    emit_qk(wkpre, 8, khb, bks, 1.0)

    for tokt in range(8):
        for hec in range(2):
            bank = tile_i % 8
            waits = [g_wv] + conv_all_evac
            if bank in bank_free:
                waits.append(bank_free[bank])
            for dc in range(8):
                def vmm(e, tokt=tokt, hec=hec, dc=dc, bank=bank,
                        first=(dc == 0)):
                    return e.matmul(
                        pbank(bank),
                        qkvb[:, (16 + dc) * 1024 + tokt * 128:
                             (16 + dc) * 1024 + tokt * 128 + 128],
                        wvb[:, dc * 1024 + hec * 512:
                            dc * 1024 + hec * 512 + 512],
                        start=first, stop=False)
                S.op("tensor", vmm, wait=waits)
                waits = []
            def vbias(e, hec=hec, bank=bank):
                return e.matmul(pbank(bank), ones1[:, :],
                                bvb[0:1, hec * 512:hec * 512 + 512],
                                start=False, stop=True)
            S.op("tensor", vbias, wait=[("vC", v_prep)], inc="tC")
            t_grp = S.n("tC")
            def vevac(e, tokt=tokt, hec=hec, bank=bank):
                return e.tensor_copy(
                    vtb[:, tokt * 1024 + hec * 512:
                        tokt * 1024 + hec * 512 + 512],
                    pbank(bank))
            S.op("vector", vevac, wait=[("tC", t_grp)])
            bank_free[bank] = ("vC", S.n("vC"))
            tile_i += 1
    p2_done = [("cC", S.n("cC")), ("vC", S.n("vC"))]

    def scol(h):
        return (h % 2) * 1024 + (h // 2) * 128

    def qk_slice(buf, h, b):
        return buf[(h % 2) * 64:(h % 2) * 64 + 64,
                   (h // 2) * 1024 + b * 128:(h // 2) * 1024 + b * 128 + 128]

    p2_t_end = S.n("tC")
    pre_m1 = {}
    blk_oevac = {}
    blk_state = {}

    def emit_pre(b, ps, extra=()):
        waits = list(p2_done) + list(extra)
        horder = list(range(0, H, 2)) + list(range(1, H, 2))
        for i, h in enumerate(horder):
            def smm(e, b=b, h=h, ps=ps):
                return e.matmul(
                    ps[:, scol(h):scol(h) + 128],
                    qk_slice(khb, h, b), qk_slice(qhb, h, b),
                    start=True, stop=True)
            S.op("tensor", smm, wait=waits,
                 inc="tC" if i == H - 1 else None)
            waits = []
        t_grp = S.n("tC")
        def m1red(e, b=b, ps=ps):
            return e.reduce_max(m1all[:, b:b + 1], ps[:, :], axis=AX.X)
        S.op("vector", m1red, wait=[("tC", t_grp)])
        pre_m1[b] = S.n("vC")

    def emit_finish(blo, bhi, psm, coloff):
        nb_ = bhi - blo
        v_pre = S.n("vC")
        S.op("tensor", lambda e, psm=psm: e.transpose(
            psm[0:nb_, 0:128], m1all[:, blo:bhi], idf[:, :]),
            wait=[("vC", v_pre)], inc="tC")
        t1 = S.n("tC")
        S.op("vector", lambda e, psm=psm: e.reduce_max(
            negg8[0:nb_, :], psm[0:nb_, 0:128], axis=AX.X, negate=True),
            wait=[("tC", t1)])
        v8 = S.n("vC")
        S.op("tensor", lambda e, psm=psm: e.transpose(
            psm[0:1, 256:256 + nb_], negg8[0:nb_, :], idf[0:nb_, 0:nb_]),
            wait=[("vC", v8)], inc="tC")
        t2 = S.n("tC")
        S.op("vector", lambda e, psm=psm: e.tensor_copy(
            negrowf[:, 0:nb_], psm[0:1, 256:256 + nb_]), wait=[("tC", t2)])
        S.op("vector", lambda e: e.tensor_copy(
            negrowb[:, 0:nb_], negrowf[:, 0:nb_]))
        vr = S.n("vC")
        for j in range(nb_):
            def bcast(e, j=j, psm=psm):
                return e.matmul(psm[:, coloff + j:coloff + j + 1], ones1[:, :],
                                negrowb[0:1, j:j + 1], start=True, stop=True)
            S.op("tensor", bcast, wait=[("vC", vr)],
                 inc="tC" if j == nb_ - 1 else None)
        tb = S.n("tC")
        S.op("vector", lambda e, psm=psm: e.tensor_copy(
            gneg[:, blo:bhi], psm[:, coloff:coloff + nb_]), wait=[("tC", tb)])
        return ("vC", S.n("vC"))

    def attn_smm(b, ps, extra=()):
        waits = list(extra)
        horder = list(range(0, H, 2)) + list(range(1, H, 2))
        for i, h in enumerate(horder):
            def smm(e, b=b, h=h, ps=ps):
                return e.matmul(
                    ps[:, scol(h):scol(h) + 128],
                    qk_slice(khb, h, b), qk_slice(qhb, h, b),
                    start=True, stop=True)
            S.op("tensor", smm, wait=waits,
                 inc="tC" if i == H - 1 else None)
            waits = []
        blk_state[b] = dict(smm=S.n("tC"))

    def attn_chain_pre(b, ps):
        pb = pb0 if b % 2 == 0 else pb1
        lv = linv0 if b % 2 == 0 else linv1
        st = blk_state[b]
        S.op("scalar", lambda e, b=b, ps=ps, pb=pb: e.activation(
            pb[:, :], ps[:, :], AF.Exp, bias=gneg[:, b:b + 1], scale=1.0),
            wait=[("tC", st["smm"])])
        st["exp"] = S.n("cC")
        S.op("vector", lambda e, pb=pb: e.tensor_mul(
            pb[:, :], pb[:, :], maskt[:, :]),
            wait=[("cC", st["exp"])])
        st["mask"] = ("vC", S.n("vC"))
        st["pb"], st["lv"] = pb, lv

    def attn_l(b, ps):
        st = blk_state[b]
        pb, lv = st["pb"], st["lv"]
        for j in range(4):
            def lmm(e, j=j, ps=ps, pb=pb):
                return e.matmul(ps[:, j * 512:(j + 1) * 512], ones128[:, :],
                                pb[:, j * 512:(j + 1) * 512],
                                start=True, stop=True)
            S.op("tensor", lmm, wait=[st["mask"]],
                 inc="tC" if j == 3 else None)
        st["lmm"] = S.n("tC")
        S.op("scalar", lambda e, ps=ps, lv=lv: e.activation(
            lv[:, :], ps[:, :], AF.Ln), wait=[("tC", st["lmm"])])
        S.op("scalar", lambda e, lv=lv: e.activation(
            lv[:, :], lv[:, :], AF.Exp, bias=0.0, scale=-1.0))
        st["recip"] = ("cC", S.n("cC"))
        S.op("vector", lambda e, pb=pb, lv=lv: e.tensor_mul(
            pb[:, :], pb[:, :], lv[:, :]), wait=[st["recip"]])
        st["pnorm"] = ("vC", S.n("vC"))

    def attn_o(b, ps):
        st = blk_state[b]
        pb = st["pb"]
        for h in range(H):
            def omm(e, b=b, h=h, ps=ps, pb=pb):
                out = ps[(h % 2) * 64:(h % 2) * 64 + 64,
                         (h // 8) * 512 + ((h // 2) % 4) * 128:
                         (h // 8) * 512 + ((h // 2) % 4) * 128 + 128]
                return e.matmul(
                    out,
                    vtb[:, b * 1024 + h * 64:b * 1024 + h * 64 + 64],
                    pb[:, scol(h):scol(h) + 128],
                    start=True, stop=True)
            S.op("tensor", omm, wait=[st["pnorm"]],
                 inc="tC" if h == H - 1 else None)
        st["omm"] = S.n("tC")
        for g in range(2):
            for bot in range(2):
                def oev(e, b=b, g=g, bot=bot, ps=ps):
                    p0 = bot * 64
                    src = ps[p0:p0 + 64, g * 512:(g + 1) * 512].rearrange(
                        "p (c f) -> p c f", c=4)
                    dst = otb[p0:p0 + 64, g * 4096:(g + 1) * 4096].rearrange(
                        "p (c f) -> p c f", c=4)[:, :, b * 128:(b + 1) * 128]
                    return e.activation(dst, src, AF.Copy)
                S.op("scalar", oev, wait=[("tC", st["omm"])])
        blk_oevac[b] = [("cC", S.n("cC")), ("vC", st["pnorm"][1])]

    for b in range(NB):
        ps = pset(b % 2)
        extra = [("vC", pre_m1[b - 2])] if b >= 2 else []
        emit_pre(b, ps, extra)
    v_gnegA = emit_finish(0, 8, psA, 256 + 8)
    v_gnegB = v_gnegA
    attn_smm(0, psA, [v_gnegA, ("vC", pre_m1[6])])
    attn_chain_pre(0, psA)
    attn_smm(1, psB, [v_gnegA, ("vC", pre_m1[7])])
    attn_chain_pre(1, psB)
    attn_l(0, psA)
    attn_o(0, psA)
    attn_smm(2, psA, blk_oevac[0])
    attn_chain_pre(2, psA)
    attn_l(1, psB)
    attn_o(1, psB)
    attn_smm(3, psB, blk_oevac[1])
    attn_chain_pre(3, psB)
    attn_l(2, psA)
    attn_o(2, psA)
    attn_l(3, psB)
    attn_o(3, psB)

    def _smm47(b):
        def go():
            extra = [v_gnegB]
            extra += blk_oevac[b - 1]
            attn_smm(b, psB, extra)
            attn_chain_pre(b, psB)
        return go
    def _l47(b):
        def go():
            attn_l(b, psB)
        return go
    def _o47(b):
        def go():
            attn_o(b, psB)
        return go
    attn_rest = [_smm47(4), None, _l47(4), None, _o47(4), None,
                 _smm47(5), None, _l47(5), None, _o47(5), None,
                 _smm47(6), _l47(6), _o47(6),
                 _smm47(7), _l47(7), _o47(7)]

    def flat_oevac(rng):
        w = []
        for b in rng:
            w.extend(blk_oevac[b])
        return w

    stream_free = {0: None, 1: None}
    slot_free = {}
    ln_state = {}

    def emit_p4(half, slots=None):
        nonlocal tile_i
        hofs = half * TH
        entry_cc = ("cC", S.n("cC"))
        slot_i = [0]

        def hit_slot():
            if slots and slot_i[0] < len(slots):
                cb = slots[slot_i[0]]
                if cb is not None:
                    cb()
            slot_i[0] += 1

        def _wslot(g):
            t = wsA if g % 2 == 0 else wsB
            half_off = 2048 * ((g // 2) % 2)
            return (t[:, half_off:half_off + 2048],
                    ("s0", "s1", "s2", "s3")[g % 4])

        wo_ld = {}
        def emit_wo_load(g):
            buf, sem = _wslot(g)
            waits = [("tC", p2_t_end)]
            if g % 4 in slot_free:
                waits.append(slot_free[g % 4])
            def wodma(e, g=g, buf=buf):
                src = wo[:, g * 256:(g + 1) * 256].rearrange(
                    "(c p) f -> p c f", p=128)
                dst = buf.rearrange("p (c f) -> p c f", c=8)
                return e.dma_start(out=dst, in_=src)
            S.op("gpsimd", wodma, wait=waits, inc=sem, k=16)
            wo_ld[g] = (sem, S.n(sem))

        for g in range(4):
            emit_wo_load(g)
        a1_gate = flat_oevac(range(4) if half == 0 else range(4, 8))
        if half == 0:
            a1_gate = a1_gate + [v_gnegB]
        for wc in range(WC):
            hit_slot()
            g = wc // 2
            buf, _ = _wslot(g)
            bank = tile_i % 4
            waits = [wo_ld[g]] + a1_gate
            if bank in bank_free:
                waits.append(bank_free[bank])
            for hec in range(WC):
                def womm(e, wc=wc, hec=hec, bank=bank, buf=buf, hofs=hofs,
                         first=(hec == 0), last=(hec == WC - 1)):
                    return e.matmul(
                        pbank(bank),
                        buf[:, hec * 256 + (wc % 2) * 128:
                            hec * 256 + (wc % 2) * 128 + 128],
                        otb[:, hec * 1024 + hofs:hec * 1024 + hofs + 512],
                        start=first, stop=last)
                S.op("tensor", womm, wait=waits,
                     inc="tC" if hec == WC - 1 else None)
                waits = []
            t_grp = S.n("tC")
            if wc % 2 == 1:
                slot_free[g % 4] = ("tC", t_grp)
            def woevac(e, wc=wc, bank=bank):
                return e.activation(
                    a1b[:, wc * TH:(wc + 1) * TH], pbank(bank), AF.Identity,
                    bias=bos[:, wc:wc + 1], scale=1.0)
            S.op("scalar", woevac, wait=[("tC", t_grp)])
            bank_free[bank] = ("cC", S.n("cC"))
            tile_i += 1
        a1_done = ("cC", S.n("cC"))

        pw_ld = {}
        xj_ld = {}
        xj_free = {0: None, 1: None}
        b1, b2 = (2, 3) if half == 0 else (4, 5)
        st = {}
        stat_w1 = [bank_free[b1]] if b1 in bank_free else []
        stat_w2 = [bank_free[b2]] if b2 in bank_free else []
        mm2_done = []
        pend_stats = []

        def emit_pw_load(g):
            buf, sem = _wslot(g)
            waits = []
            if g % 4 in slot_free:
                waits.append(slot_free[g % 4])
            def pwdma(e, g=g, buf=buf):
                src = projw[:, g * 256:(g + 1) * 256].rearrange(
                    "(c p) f -> p c f", p=128)
                dst = buf.rearrange("p (c f) -> p c f", c=8)
                return e.dma_start(out=dst, in_=src)
            S.op("gpsimd", pwdma, wait=waits, inc=sem, k=16)
            pw_ld[g] = (sem, S.n(sem))

        def emit_xj_load(dmc):
            par = dmc % 2
            xcb = xjb0 if par == 0 else xjb1
            sem = "x0" if par == 0 else "x1"
            xw = []
            if xj_free[par] is not None:
                xw.append(xj_free[par])
            else:
                xw.append(("tC", conv_all_mm))
            def xdma(e, dmc=dmc, xcb=xcb, hofs=hofs):
                return e.dma_start(
                    out=xcb[:, :],
                    in_=xT[dmc * 128:(dmc + 1) * 128, 1 + hofs:1 + hofs + TH])
            S.op("gpsimd", xdma, wait=xw, inc=sem, k=16)
            xj_ld[dmc] = (sem, S.n(sem))

        for g in range(4):
            emit_pw_load(g)
        emit_xj_load(0)
        emit_xj_load(1)
        res1_evac = []
        _sw = {4: [bank_free[4]] if 4 in bank_free else [],
               5: [bank_free[5]] if 5 in bank_free else []}

        def emit_ln1_stat(dmc):
            def sq(e, dmc=dmc):
                return e.activation(
                    sqtmp[:, (dmc % 2) * 512:(dmc % 2) * 512 + 512],
                    res1_ap(dmc), AF.Square)
            sqw = [("cC", res1_evac[dmc])]
            if dmc >= 2:
                sqw.append(("tC", mm2_done[dmc - 2]))
            S.op("scalar", sq, wait=sqw)
            c_sq = S.n("cC")
            def mm1(e, dmc=dmc):
                return e.matmul(pbank(4), ones128[:, :], res1_ap(dmc),
                                start=(dmc == 0), stop=(dmc == DC - 1),
                                skip_group_check=True)
            S.op("tensor", mm1, wait=_sw[4] + [("cC", res1_evac[dmc])],
                 inc="tC" if dmc == DC - 1 else None)
            _sw[4] = []
            def mm2(e, dmc=dmc):
                return e.matmul(pbank(5), ones128[:, :],
                                sqtmp[:, (dmc % 2) * 512:
                                      (dmc % 2) * 512 + 512],
                                start=(dmc == 0), stop=(dmc == DC - 1),
                                skip_group_check=True)
            S.op("tensor", mm2, wait=_sw[5] + [("cC", c_sq)], inc="tC")
            _sw[5] = []
            mm2_done.append(S.n("tC"))
        for dmc in range(DC):
            hit_slot()
            g = dmc // 2
            buf, _ = _wslot(g)
            xcb = xjb0 if dmc % 2 == 0 else xjb1
            bank = tile_i % 4
            waits = [pw_ld[g], a1_done]
            if bank in bank_free:
                waits.append(bank_free[bank])
            for wc in range(WC):
                def pmm(e, dmc=dmc, wc=wc, bank=bank, buf=buf,
                        first=(wc == 0)):
                    return e.matmul(
                        pbank(bank),
                        buf[:, wc * 256 + (dmc % 2) * 128:
                            wc * 256 + (dmc % 2) * 128 + 128],
                        a1b[:, wc * TH:(wc + 1) * TH],
                        start=first, stop=False)
                S.op("tensor", pmm, wait=waits)
                waits = []
            def pxadd(e, dmc=dmc, bank=bank, xcb=xcb):
                return e.matmul(pbank(bank), idb[:, :], xcb[:, :],
                                start=False, stop=True)
            S.op("tensor", pxadd, wait=[xj_ld[dmc], sd_idb], inc="tC")
            t_grp = S.n("tC")
            xj_free[dmc % 2] = ("tC", t_grp)
            if dmc % 2 == 1:
                slot_free[g % 4] = ("tC", t_grp)
            if dmc == DC - 1:
                stream_free[0] = stream_free[1] = ("tC", t_grp)
            pw_ = [("tC", t_grp)]
            if half == 1 and dmc < 2:
                pw_.extend(ln_state[0]["applied"])
            def pevac(e, dmc=dmc, bank=bank):
                return e.activation(
                    res1_ap(dmc), pbank(bank), AF.Identity,
                    bias=projb[:, dmc:dmc + 1], scale=1.0)
            S.op("scalar", pevac, wait=pw_)
            bank_free[bank] = ("cC", S.n("cC"))
            c_evac = S.n("cC")
            res1_evac.append(c_evac)
            tile_i += 1
            if dmc % 2 == 1 and dmc // 2 + 4 < 16:
                emit_pw_load(dmc // 2 + 4)
            if dmc + 2 < DC:
                emit_xj_load(dmc + 2)
        b1, b2 = 0, 1
        stat_w1 = [bank_free[b1]] if b1 in bank_free else []
        stat_w2 = [bank_free[b2]] if b2 in bank_free else []
        for dmc in range(DC):
            def sq(e, dmc=dmc):
                return e.activation(
                    sqtmp[:, (dmc % 2) * 512:(dmc % 2) * 512 + 512],
                    res1_ap(dmc), AF.Square)
            sqw = [("cC", res1_evac[dmc])]
            if dmc >= 2:
                sqw.append(("tC", mm2_done[dmc - 2]))
            S.op("scalar", sq, wait=sqw)
            c_sq = S.n("cC")
            def mm1(e, dmc=dmc, b1=b1):
                return e.matmul(pbank(b1), ones128[:, :], res1_ap(dmc),
                                start=(dmc == 0), stop=(dmc == DC - 1),
                                skip_group_check=True)
            S.op("tensor", mm1,
                 wait=stat_w1 + [("cC", res1_evac[dmc])],
                 inc="tC" if dmc == DC - 1 else None)
            stat_w1 = []
            def mm2(e, dmc=dmc, b2=b2):
                return e.matmul(pbank(b2), ones128[:, :],
                                sqtmp[:, (dmc % 2) * 512:
                                      (dmc % 2) * 512 + 512],
                                start=(dmc == 0), stop=(dmc == DC - 1),
                                skip_group_check=True)
            S.op("tensor", mm2, wait=stat_w2 + [("cC", c_sq)], inc="tC")
            stat_w2 = []
            mm2_done.append(S.n("tC"))
        st["b1"], st["b2"], st["t_stats"] = b1, b2, S.n("tC")
        st["res1_evac"] = res1_evac
        ln_state[half] = st
    def emit_ln_chain(st, nm_t, rstd_t, make_bf=False, tmps=None):
        t1_, t2_ = tmps if tmps else (tmp1, tmp2)
        b1, b2, t_stats = st["b1"], st["b2"], st["t_stats"]
        cw_ = [("tC", t_stats)]
        if "extra_chain_wait" in st:
            cw_.append(st["extra_chain_wait"])
        nm_out = nmb if make_bf else nm_t
        rs_out = rstdb if make_bf else rstd_t
        S.op("scalar", lambda e, nm_out=nm_out: e.activation(
            nm_out[:, :], pbank(b1), AF.Copy, bias=0.0, scale=-1.0 / D),
            wait=cw_)
        S.op("scalar", lambda e, t1_=t1_: e.activation(
            t1_[:, :], pbank(b1), AF.Square, bias=0.0, scale=1.0 / D))
        S.op("scalar", lambda e, t2_=t2_: e.activation(
            t2_[:, :], pbank(b2), AF.Copy, bias=1e-5, scale=1.0 / D))
        c_st = S.n("cC")
        bank_free[b1] = ("cC", c_st)
        bank_free[b2] = ("cC", c_st)
        S.op("vector", lambda e, t1_=t1_, t2_=t2_: e.tensor_sub(
            t2_[:, :], t2_[:, :], t1_[:, :]), wait=[("cC", c_st)])
        v_r = S.n("vC")
        S.op("scalar", lambda e, t1_=t1_, t2_=t2_: e.activation(
            t1_[:, :], t2_[:, :], AF.Ln), wait=[("vC", v_r)])
        S.op("scalar", lambda e, rs_out=rs_out, t1_=t1_: e.activation(
            rs_out[:, :], t1_[:, :], AF.Exp, bias=0.0, scale=-0.5))
        return ("cC", S.n("cC"))

    def p45_bank(i):
        return i % 4

    def _x1_grp(half, g):
        if half == 0:
            return x1bA[:, g * 2048:(g + 1) * 2048]
        if g < 4:
            return x1bB_lo[:, g * 2048:(g + 1) * 2048]
        return x1bB_hi[:, (g - 4) * 2048:(g - 3) * 2048]

    def emit_ln1_apply(half):
        st = ln_state[half]
        gate = emit_ln_chain(st, negmean, rstd, make_bf=True)
        for g in range(8):
            r = res1_grp(g).rearrange("p (c f) -> p c f", c=4)
            xo = _x1_grp(half, g).rearrange("p (c f) -> p c f", c=4)
            eng = "gpsimd" if (half == 1 and g % 2 == 1) else "vector"
            S.op(eng, lambda e, r=r: e.tensor_add(
                r, r, bc3(nmb[:, :], 4)), wait=[gate])
            S.op(eng, lambda e, r=r, xo=xo: e.tensor_mul(
                xo, r, bc3(rstdb[:, :], 4)))
        st["applied"] = [("vC", S.n("vC")), ("gC", S.n("gC"))]

    def emit_ht(half):
        nonlocal tile_i
        for erc in range(ERC):
            bank = 2 + erc
            waits = [g_at] + ln_state[half]["applied"]
            if bank in bank_free:
                waits.append(bank_free[bank])
            for dc in range(DC):
                def hmm(e, erc=erc, dc=dc, bank=bank, half=half,
                        first=(dc == 0), last=(dc == DC - 1)):
                    return e.matmul(
                        pbank(bank),
                        atb[:, dc * 256 + erc * 128:dc * 256 + erc * 128 + 128],
                        x1b_ap(half, dc),
                        start=first, stop=last)
                S.op("tensor", hmm, wait=waits,
                     inc="tC" if dc == DC - 1 else None)
                waits = []
            t_grp = S.n("tC")
            def hevac(e, erc=erc, bank=bank, half=half):
                return e.tensor_scalar_mul(
                    hb[:, erc * 1024 + half * TH:erc * 1024 + half * TH + TH],
                    pbank(bank), probs[:, erc:erc + 1])
            S.op("vector", hevac, wait=[("tC", t_grp)])
            bank_free[bank] = ("vC", S.n("vC"))
            tile_i += 1

    wt_ld = {}
    def emit_wt_load(pg):
        par = pg % 2
        buf = wsF0 if par == 0 else wsF1
        sem = "s0" if par == 0 else "s1"
        waits = [] if pg >= 2 else list(wt_first_gate)
        if stream_free[par] is not None:
            waits.append(stream_free[par])
        def wtdma(e, pg=pg, buf=buf):
            src = Wt[:, pg * 256:(pg + 1) * 256].rearrange(
                "(c p) f -> p c f", p=128)
            dst = buf[:, :].rearrange("p (c f) -> p c f", c=32)
            return e.dma_start(out=dst, in_=src)
        S.op("gpsimd", wtdma, wait=waits, inc=sem, k=16)
        wt_ld[pg] = (sem, S.n(sem))

    emit_p4(0, slots=attn_rest)
    p3_done = [("cC", S.n("cC")), ("vC", S.n("vC")), ("gC", S.n("gC")),
               ("tC", S.n("tC"))]
    emit_ln1_apply(0)
    emit_p4(1)
    p4_t_end = ("tC", S.n("tC"))
    def atdma(e):
        src = At[:, :].rearrange("(c p) f -> p c f", p=128)
        dst = atb[:, :].rearrange("p (c f) -> p c f", c=32)
        return e.dma_start(out=dst, in_=src)
    S.op("gpsimd", atdma, wait=list(p3_done) + [g_wv], inc="w2", k=16)
    g_at = ("w2", S.n("w2"))
    emit_ht(0)
    hb_done_h0 = ("vC", S.n("vC"))
    t_p4done = ("tC", S.n("tC"))
    wt_first_gate = [t_p4done]
    emit_wt_load(0)
    emit_wt_load(1)
    emit_ln1_apply(1)


    emit_ht(1)
    t_ht_end = ("tC", S.n("tC"))
    hb_done = ("vC", S.n("vC"))
    for erc in range(ERC):
        def bmdma2(e, erc=erc):
            return e.dma_start(
                out=Bmb[:, erc * D:(erc + 1) * D],
                in_=Bm[erc * 128:(erc + 1) * 128, :])
        S.op("gpsimd", bmdma2, wait=[t_ht_end], inc="w2", k=16)
    g_bm = ("w2", S.n("w2"))
    ln2 = {0: dict(b1=4, b2=5, mm2_done=[]), 1: dict(b1=6, b2=7, mm2_done=[])}
    for hset in ln2.values():
        for bk in (hset["b1"], hset["b2"]):
            hset.setdefault("w", {})[bk] = (
                [bank_free[bk]] if bk in bank_free else [])
    pend_stats2 = []
    res2_writes = {}

    stats2_slot_tc = {}
    def emit_stats2(oc, half, c_evac):
        hs = ln2[half]
        slot = (2 * oc + half) % 4
        def sq2(e, oc=oc, half=half, slot=slot):
            return e.activation(
                sq4[:, slot * 512:slot * 512 + 512],
                res2_ap(oc)[:, half * TH:half * TH + TH], AF.Square)
        sqw = [("cC", c_evac), t_ffn_end]
        prev = stats2_slot_tc.get(slot)
        if prev is not None:
            sqw.append(("tC", prev))
        S.op("scalar", sq2, wait=sqw)
        c_sq = S.n("cC")
        def mm1(e, oc=oc, half=half, b1=hs["b1"]):
            return e.matmul(
                pbank(b1), ones128[:, :],
                res2_ap(oc)[:, half * TH:half * TH + TH],
                start=(oc == 0), stop=(oc == DC - 1),
                skip_group_check=True)
        S.op("tensor", mm1, wait=hs["w"][hs["b1"]] + [("cC", c_evac)],
             inc="tC" if oc == DC - 1 else None)
        hs["w"][hs["b1"]] = []
        def mm2(e, oc=oc, half=half, b2=hs["b2"], slot=slot):
            return e.matmul(
                pbank(b2), ones128[:, :],
                sq4[:, slot * 512:slot * 512 + 512],
                start=(oc == 0), stop=(oc == DC - 1),
                skip_group_check=True)
        S.op("tensor", mm2, wait=hs["w"][hs["b2"]] + [("cC", c_sq)],
             inc="tC")
        hs["w"][hs["b2"]] = []
        stats2_slot_tc[slot] = S.n("tC")
        hs["mm2_done"].append(S.n("tC"))

    for pg in range(16):
        par = pg % 2
        buf = wsF0 if par == 0 else wsF1
        for half in range(2):
            for sub in range(2):
                oc = pg * 2 + sub
                bank = p45_bank(tile_i)
                waits = [wt_ld[pg], g_bm,
                         hb_done_h0 if half == 0 else hb_done]
                if bank in bank_free:
                    waits.append(bank_free[bank])
                for dc in range(DC):
                    def fmm(e, dc=dc, bank=bank, buf=buf, half=half,
                            sub=sub, first=(dc == 0)):
                        return e.matmul(
                            pbank(bank),
                            buf[:, dc * 256 + sub * 128:
                                dc * 256 + sub * 128 + 128],
                            x1b_ap(half, dc),
                            start=first, stop=False)
                    S.op("tensor", fmm, wait=waits)
                    waits = []
                for erc in range(ERC):
                    def bmm(e, oc=oc, erc=erc, bank=bank, half=half):
                        return e.matmul(
                            pbank(bank),
                            Bmb[:, erc * D + oc * 128:erc * D + oc * 128 + 128],
                            hb[:, erc * 1024 + half * TH:
                               erc * 1024 + half * TH + TH],
                            start=False, stop=False)
                    S.op("tensor", bmm)
                def fxadd(e, oc=oc, bank=bank, half=half):
                    return e.matmul(pbank(bank), idb[:, :],
                                    x1b_ap(half, oc),
                                    start=False, stop=True)
                S.op("tensor", fxadd, wait=[sd_idb], inc="tC")
                t_grp = S.n("tC")
                def fevac(e, oc=oc, bank=bank, half=half):
                    return e.activation(
                        res2_ap(oc)[:, half * TH:half * TH + TH],
                        pbank(bank), AF.Identity,
                        bias=ffnb[:, oc:oc + 1], scale=1.0)
                S.op("scalar", fevac, wait=[("tC", t_grp)])
                bank_free[bank] = ("cC", S.n("cC"))
                c_evac = S.n("cC")
                res2_writes[(oc, half)] = c_evac
                tile_i += 1
        stream_free[par] = ("tC", S.n("tC"))
        if pg + 2 < 16:
            emit_wt_load(pg + 2)
    t_ffn_end = ("tC", S.n("tC"))
    for oc in range(DC):
        for half in range(2):
            emit_stats2(oc, half, res2_writes[(oc, half)])

    ob_free = [None, None, None, None]
    apply_specs = []
    for half in range(2):
        hs = ln2[half]
        st = dict(b1=hs["b1"], b2=hs["b2"], t_stats=S.n("tC"))
        nm_t = negmean if half == 0 else nm2
        rs_t = rstd if half == 0 else rstd2
        gate = emit_ln_chain(st, nm_t, rs_t, make_bf=False,
                             tmps=None if half == 0 else (l2tmp1, l2tmp2))
        apply_specs.append((half, nm_t, rs_t, gate))
    for dc in range(DC):
        for half, nm_t, rs_t, gate in apply_specs:
            slot = (dc % 2) * 2 + half
            osem = f"o{slot}"
            hofs = half * TH
            eng = "vector" if half == 0 else "gpsimd"
            ww = [gate]
            if ob_free[slot] is not None:
                ww.append(ob_free[slot])
            def app1(e, dc=dc, slot=slot, hofs=hofs, nm_t=nm_t):
                return e.tensor_add(ob[slot][:, :],
                                    res2_ap(dc)[:, hofs:hofs + TH],
                                    nm_t[:, :])
            def app2(e, slot=slot, rs_t=rs_t):
                return e.tensor_mul(ob[slot][:, :], ob[slot][:, :],
                                    rs_t[:, :])
            if eng == "vector":
                S.op("vector", app1, wait=ww)
                S.op("vector", app2)
                v_ap = ("vC", S.n("vC"))
            else:
                S.op("gpsimd", app1, wait=ww, inc="gC")
                S.op("gpsimd", app2, inc="gC")
                v_ap = ("gC", S.n("gC"))
            def sdma(e, dc=dc, slot=slot, hofs=hofs):
                return e.dma_start(
                    out=outT[dc * 128:(dc + 1) * 128, hofs:hofs + TH],
                    in_=ob[slot][:, :])
            S.op("sync", sdma, wait=[v_ap], inc=osem, k=16)
            ob_free[slot] = (osem, S.n(osem))

    if tap_outs:
        wait_all = [("tC", S.n("tC")), ("vC", S.n("vC")), ("cC", S.n("cC"))]
        tap_srcs = dict(
            qkvb=qkvb, qhb=qhb, khb=khb, vtb=vtb, otb=otb, gneg=gneg,
            a1b=a1b, hbt=hb, m1all=m1all, pb0=pb0, pb1=pb1, linv0=linv0,
            rstd=rstd, x1bA=x1bA, res1=res1,
        )
        for tname, ap_d in tap_outs.items():
            src = tap_srcs[tname]
            S.op("sync", lambda e, d=ap_d, s=src: e.dma_start(
                out=d, in_=s[:, :]), wait=wait_all, inc="o0", k=16)

    fin = [(f"o{i}", S.n(f"o{i}")) for i in range(4)]
    S.op("sync", lambda e: e.nop(), wait=fin)

    with ExitStack() as stx:
        sems = {name: stx.enter_context(nc.semaphore(name)) for name in SEMS}
        blk = stx.enter_context(nc.Block())
        for eng in Sched.ENGS:
            ops = S.q[eng]
            if not ops:
                continue
            getattr(blk, eng)(lambda e, ops=ops: [f(e, sems) for f in ops])
    return nc


def stage_inputs(inputs):
    f32 = np.float32
    x = np.asarray(inputs["L0_attn_inputs"], f32)
    N, L, _ = x.shape
    per_core = []
    for c in range(8):
        n, s = divmod(c, 2)
        t0 = s * 1024
        xTh = np.zeros((D, T + 2), f32)
        lo = max(t0 - 1, 0)
        hi = min(t0 + T + 1, L)
        xTh[:, lo - (t0 - 1):hi - (t0 - 1)] = x[n, lo:hi, :].T
        per_core.append(np.ascontiguousarray(xTh))

    shared = {
        "cw": np.ascontiguousarray(np.asarray(inputs["L0_attn_conv_w"], f32)),
        "wq": np.ascontiguousarray(np.asarray(inputs["L0_attn_wq"], f32).reshape(W1, W1)),
        "wk": np.ascontiguousarray(np.asarray(inputs["L0_attn_wk"], f32).reshape(W1, W1)),
        "wv": np.ascontiguousarray(np.asarray(inputs["L0_attn_wv"], f32).reshape(W1, W1)),
        "wo": np.ascontiguousarray(np.asarray(inputs["L0_attn_wo"], f32).reshape(W1, W1)),
        "projw": np.ascontiguousarray(np.asarray(inputs["L0_attn_proj_w"], f32)),
        "Wt": np.ascontiguousarray(np.asarray(inputs["L0_ffn_W"], f32).T),
        "At": np.ascontiguousarray(np.asarray(inputs["L0_ffn_A"], f32).reshape(256, D).T),
        "Bm": np.ascontiguousarray(np.asarray(inputs["L0_ffn_B"], f32).transpose(0, 2, 1).reshape(256, D)),
        "convb": np.ascontiguousarray(np.asarray(inputs["L0_attn_conv_b"], f32).reshape(QC, 128).T),
        "bq": np.ascontiguousarray(np.asarray(inputs["L0_attn_bq"], f32).reshape(WC, 128).T),
        "bk": np.ascontiguousarray(np.asarray(inputs["L0_attn_bk"], f32).reshape(WC, 128).T),
        "bo": np.ascontiguousarray(np.asarray(inputs["L0_attn_bo"], f32).reshape(WC, 128).T),
        "projb": np.ascontiguousarray(np.asarray(inputs["L0_attn_proj_b"], f32).reshape(DC, 128).T),
        "ffnb": np.ascontiguousarray(np.asarray(inputs["L0_ffn_b"], f32).reshape(DC, 128).T),
        "probs": np.ascontiguousarray(
            np.repeat(np.asarray(inputs["L0_ffn_probs"], f32), 16).reshape(ERC, 128).T),
        "bv": np.ascontiguousarray(np.asarray(inputs["L0_attn_bv"], f32).reshape(1, W1)),
        "ones128": np.ones((128, 128), BF16N),
        "ones1": np.ones((1, 128), BF16N),
        "mask": np.ascontiguousarray(
            np.tile(np.triu(np.ones((128, 128), np.float32)), (1, 16)).astype(BF16N)),
        "idf": np.eye(128, dtype=f32),
        "idb": np.eye(128, dtype=BF16N),
    }
    in_maps = [{"xT": per_core[c], **shared} for c in range(8)]
    return in_maps


def kernel(**inputs):
    in_maps = stage_inputs(inputs)
    nc = build_nc()
    res = run_bass_kernel_spmd(nc, in_maps, core_ids=list(range(8)))
    out = np.empty((4, 2048, D), np.float32)
    for c in range(8):
        n, s = divmod(c, 2)
        out[n, s * 1024:(s + 1) * 1024, :] = res.results[c]["outT"].T
    return out
